# version 09
# baseline (speedup 1.0000x reference)
import contextlib
import os
import numpy as np
import ml_dtypes
import concourse.bass as bass
import concourse.mybir as mybir
from concourse.bass_utils import run_bass_kernel_spmd

F32 = mybir.dt.float32
BF16 = mybir.dt.bfloat16
U32 = mybir.dt.uint32
AF = mybir.ActivationFunctionType
ALU = mybir.AluOpType
AX = mybir.AxisListType

P = 128
D, E, S, FF, NEXP = 1024, 2048, 16, 4096, 8
KD, KE, KF = D // P, E // P, FF // P
TS = 512
NEG = -1e30
TPH = 2048
M1LVL = int(os.environ.get("M1LVL", "9"))
M1SKIP = set(os.environ.get("M1SKIP", "").split(","))


def set_tph(v):
    global TPH
    TPH = v


class Ctr:
    def __init__(self, h):
        self.h = h
        self.n = 0

    def inc(self, inst, k=1):
        inst.then_inc(self.h, k)
        self.n += k
        return self.n


class Eng:

    def __init__(self, e):
        self.e = e
        self.w = set()
        self.r = set()

    def _names(self, aps):
        out = set()
        for ap in aps:
            t = getattr(ap, "tensor", None)
            if t is not None:
                out.add(t.name)
        return out

    def _pre(self, outs, ins):
        on, inn = self._names(outs), self._names(ins)
        if (inn & self.w) or (on & self.w) or (on & self.r):
            self.e.drain()
            self.w.clear()
            self.r.clear()

    def _post(self, outs, ins):
        self.w |= self._names(outs)
        self.r |= self._names(ins)

    def barrier(self):
        self.e.drain()
        self.w.clear()
        self.r.clear()

    def dinc(self):
        inst = self.e.drain()
        self.w.clear()
        self.r.clear()
        return inst

    def tt(self, out, a, b, op):
        self._pre([out], [a, b])
        i = self.e.tensor_tensor(out, a, b, op)
        self._post([out], [a, b])
        return i

    def ts(self, out, in0, s1, s2, op0, op1=None):
        self._pre([out], [in0, s1, s2])
        if op1 is None:
            i = self.e.tensor_scalar(out, in0, s1, s2, op0)
        else:
            i = self.e.tensor_scalar(out, in0, s1, s2, op0, op1)
        self._post([out], [in0])
        return i

    def stt(self, out, in0, s, in1, op0, op1):
        self._pre([out], [in0, s, in1])
        i = self.e.scalar_tensor_tensor(out, in0, s, in1, op0, op1)
        self._post([out], [in0, in1])
        return i

    def cp(self, out, in_):
        self._pre([out], [in_])
        i = self.e.tensor_copy(out, in_)
        self._post([out], [in_])
        return i

    def mset(self, out, v):
        self._pre([out], [])
        i = self.e.memset(out, v)
        self._post([out], [])
        return i

    def rsum(self, out, in_):
        self._pre([out], [in_])
        i = self.e.reduce_sum(out, in_, axis=AX.X)
        self._post([out], [in_])
        return i

    def maxv(self, out, in_):
        self._pre([out], [in_])
        i = self.e.max(out=out, in_=in_)
        self._post([out], [in_])
        return i

    def maxidx(self, out, mx, vals):
        self._pre([out], [mx, vals])
        i = self.e.max_index(out, mx, vals)
        self._post([out], [mx, vals])
        return i

    def pred(self, out, mask, data):
        self._pre([out], [mask, data])
        i = self.e.copy_predicated(out, mask, data)
        self._post([out], [mask, data])
        return i

    def recip(self, out, in_):
        self._pre([out], [in_])
        i = self.e.reciprocal(out, in_)
        self._post([out], [in_])
        return i

    def act(self, out, in_, func, bias=0.0, scale=1.0):
        ins = [in_] + ([bias] if hasattr(bias, "tensor") else [])
        self._pre([out], ins)
        i = self.e.activation(out, in_, func, bias=bias, scale=scale)
        self._post([out], ins)
        return i

    def sign(self, out, in_):
        self._pre([out], [in_])
        i = self.e.sign(out, in_)
        self._post([out], [in_])
        return i


def _w1count(eorders, sh, ei):
    n, prev = 0, None
    for s2 in range(len(eorders)):
        for i, e in enumerate(eorders[s2]):
            if e != prev:
                n += 1
                prev = e
            if s2 == sh and i == ei:
                return n
    return n


def build_program(marks, upto="final", dbg=False):
    nc = bass.Bass("TRN2", target_bir_lowering=False)
    NT = TPH // TS
    NCH = TPH // P
    CPT = TS // P
    XW = TPH + 2
    W = TS + 2
    NSH = max(1, TPH // 1024)
    SHW = TPH // NSH
    QSH = SHW // TS
    W = TS + 2
    M = marks

    def mg(*key):
        return M.get(key, 0)

    dram = {}

    def din(name, shape):
        dram[name] = nc.dram_tensor(name, list(shape), F32,
                                    kind="ExternalInput").ap()
        return dram[name]

    x = din("x", (D, 2 * XW))
    inwm = din("inwm", (D, E))
    inwg = din("inwg", (D, E))
    dtw = din("dtw", (E, E))
    bcdw = din("bcdw", (E, 48))
    outw = din("outw", (E, D))
    wr = din("wr", (D, 8))
    w1 = din("w1", (NEXP, D, FF))
    w2 = din("w2", (NEXP, FF, D))
    wmod = din("wmod", (D, 6 * D))
    cvec = din("cvec", (P, KD))
    inbm = din("inbm", (P, KE))
    inbg = din("inbg", (P, KE))
    convw = din("convw", (P, 3 * KE))
    convb = din("convb", (P, KE))
    dtb = din("dtb", (P, KE))
    bcdb = din("bcdb", (S, 3))
    outb = din("outb", (P, KD))
    brv = din("brv", (8, 1))
    flagv = din("flagv", (1, 1))
    hmaskv = din("hmaskv", (P, 4))
    utc = din("utc", (P, P))
    onesc = din("onesc", (P, P))
    maskc = din("maskc", (P, P))
    identc = din("identc", (P, P))

    out_d = nc.dram_tensor("out", [D, TPH], F32, kind="ExternalOutput").ap()
    emr_d = nc.dram_tensor("emrd", [2, TPH], F32, kind="Internal").ap()
    h2_d = nc.dram_tensor("h2d", [D, TPH], BF16,
                          kind="ExternalOutput" if dbg else "Internal").ap()
    if dbg:
        dbg_rows = nc.dram_tensor("dbg_rows", [1, 8 * TPH], F32,
                                  kind="ExternalOutput").ap()
        dbg_cols = nc.dram_tensor("dbg_cols", [P, 4 * NCH], F32,
                                  kind="ExternalOutput").ap()

    def r3(ap, k=KD):
        return ap.rearrange("(k p) n -> p k n", p=P)

    with contextlib.ExitStack() as es_glob:
        G = es_glob.enter_context
        s_set = Ctr(G(nc.semaphore("s_set")))
        s_wmo = Ctr(G(nc.semaphore("s_wmo")))
        s_lxa = Ctr(G(nc.semaphore("s_lxa")))
        s_lxb = Ctr(G(nc.semaphore("s_lxb")))
        s_fx = Ctr(G(nc.semaphore("s_fx")))
        s_mm = Ctr(G(nc.semaphore("s_mm")))
        s_vv = Ctr(G(nc.semaphore("s_vv")))
        s_ss = Ctr(G(nc.semaphore("s_ss")))
        s_st = Ctr(G(nc.semaphore("s_st")))
        s_stx = Ctr(G(nc.semaphore("s_stx")))
        s_sth = Ctr(G(nc.semaphore("s_sth")))
        s_gf = Ctr(G(nc.semaphore("s_gf")))
        s_m2w = Ctr(G(nc.semaphore("s_m2w")))
        s_m1a = Ctr(G(nc.semaphore("s_m1a")))
        s_lh2 = Ctr(G(nc.semaphore("s_lh2")))
        s_fh2 = Ctr(G(nc.semaphore("s_fh2")))
        s_lw1 = Ctr(G(nc.semaphore("s_lw1")))
        s_fw1 = Ctr(G(nc.semaphore("s_fw1")))
        s_lw2 = Ctr(G(nc.semaphore("s_lw2")))
        s_fw2 = Ctr(G(nc.semaphore("s_fw2")))
        s_lw1b = Ctr(G(nc.semaphore("s_lw1b")))
        s_fw1b = Ctr(G(nc.semaphore("s_fw1b")))
        s_lw2b = Ctr(G(nc.semaphore("s_lw2b")))
        s_fw2b = Ctr(G(nc.semaphore("s_fw2b")))
        s_loa = Ctr(G(nc.semaphore("s_loa")))
        s_fo = Ctr(G(nc.semaphore("s_fo")))
        s_ler = Ctr(G(nc.semaphore("s_ler")))
        s_fer = Ctr(G(nc.semaphore("s_fer")))
        s_lmr = Ctr(G(nc.semaphore("s_lmr")))
        s_fmr = Ctr(G(nc.semaphore("s_fmr")))
        s_sta = Ctr(G(nc.semaphore("s_sta")))
        s_stb = Ctr(G(nc.semaphore("s_stb")))

        onesb_sb = G(nc.sbuf_tensor("onesb_sb", [P, P], BF16))
        bias_sb = G(nc.sbuf_tensor("bias_sb", [P, 8 * KE], F32))
        hm_sb = G(nc.sbuf_tensor("hm_sb", [P, 4], F32))
        bcdb_sb = G(nc.sbuf_tensor("bcdb_sb", [S, 3], F32))
        br_sb = G(nc.sbuf_tensor("br_sb", [8, 1], F32))
        flag_sb = G(nc.sbuf_tensor("flag_sb", [1, 1], F32))
        modv_sb = G(nc.sbuf_tensor("modv_sb", [P, 64], F32))
        scan_es = G(contextlib.ExitStack())
        S_ = scan_es.enter_context
        ut_sb = S_(nc.sbuf_tensor("ut_sb", [P, P], F32))
        ones_sb = S_(nc.sbuf_tensor("ones_sb", [P, P], F32))
        mk_sb = S_(nc.sbuf_tensor("mk_sb", [P, P], F32))
        idf_sb = S_(nc.sbuf_tensor("idf_sb", [P, P], F32))
        idb_sb = S_(nc.sbuf_tensor("idb_sb", [P, P], BF16))
        expcl_sb = S_(nc.sbuf_tensor("expcl_sb", [1, 2 * TPH], BF16))
        aall_sb = S_(nc.sbuf_tensor("aall_sb", [1, 2 * NCH], F32))
        ball_sb = S_(nc.sbuf_tensor("ball_sb", [1, 2 * NCH * S], F32))
        ysi_sb = S_(nc.sbuf_tensor("ysi_sb", [P, NCH], F32))
        call_sb = S_(nc.sbuf_tensor("call_sb", [P, NCH * S], BF16))
        ysr_sb = S_(nc.sbuf_tensor("ysr_sb", [1, TPH], BF16))
        srow_sb = S_(nc.sbuf_tensor("srow_sb", [1, 4 * S], F32))
        srowb_sb = S_(nc.sbuf_tensor("srowb_sb", [1, 2 * S], BF16))
        rows_sb = S_(nc.sbuf_tensor("rows_sb", [1, 2056], F32))

        block = G(nc.Block())
        NTt = TPH // TS
        x_es = es_glob.enter_context(contextlib.ExitStack())
        x_buf = x_es.enter_context(
            nc.sbuf_tensor("x_buf", [P, 2 * KD * (TS + 2)], BF16))

        with contextlib.ExitStack() as es:
            L = es.enter_context
            wmod_sb = L(nc.sbuf_tensor("wmod_sb", [P, KD * 3 * D], F32))
            silc_sb = L(nc.sbuf_tensor("silc_sb", [P, KD], F32))
            mod_ps = L(nc.psum_tensor("mod_ps", [P, 64], F32))

            @block.gpsimd
            def _(g):
                for src, dst in ((utc, ut_sb), (onesc, ones_sb), (maskc, mk_sb),
                                 (identc, idf_sb), (onesc, onesb_sb),
                                 (identc, idb_sb),
                                 (hmaskv, hm_sb), (bcdb, bcdb_sb),
                                 (brv, br_sb), (flagv, flag_sb)):
                    s_set.inc(g.dma_start(dst[:], src), 16)
                for off, src in ((0, inbm), (16, inbg), (80, convb), (96, dtb)):
                    s_set.inc(g.dma_start(bias_sb[:, off:off + KE], src), 16)
                s_set.inc(g.dma_start(bias_sb[:, 32:80], convw), 16)
                s_set.inc(g.dma_start(bias_sb[:, 112:120], outb), 16)
                s_set.inc(g.dma_start(bias_sb[:, 120:128], cvec), 16)
                for i0 in range(2):
                    p0, j0 = (1, i0) if i0 < NTt else (0, i0 - NTt)
                    base0 = p0 * XW + j0 * TS
                    sem0 = s_lxa if i0 % 2 == 0 else s_lxb
                    sem0.inc(g.dma_start(
                        x_buf[:, (i0 % 2) * KD * W:((i0 % 2) + 1) * KD * W]
                        .rearrange("p (k n) -> p k n", k=KD),
                        r3(x)[:, :, base0:base0 + W]), 16)
                for hf in range(2):
                    s_wmo.inc(g.dma_start(
                        wmod_sb[:].rearrange("p (k n) -> p k n", k=KD),
                        r3(wmod)[:, :, hf * 3 * D:(hf + 1) * 3 * D]), 16)
                    g.wait_ge(s_mm.h, mg("modh", hf))

            @block.scalar
            def _(s):
                sc = Eng(s)
                s.wait_ge(s_set.h, s_set.n)
                s_ss.inc(sc.act(silc_sb[:], bias_sb[:, 120:128], AF.Silu))
                M[("silc",)] = s_ss.n

            @block.tensor
            def _(t):
                t.wait_ge(s_set.h, s_set.n)
                t.wait_ge(s_ss.h, mg("silc"))
                for hf in range(2):
                    t.wait_ge(s_wmo.h, 16 * (hf + 1))
                    for j in range(24):
                        jj = hf * 24 + j
                        for k in range(KD):
                            inst = t.matmul(
                                mod_ps[:, jj:jj + 1],
                                wmod_sb[:, k * 3 * D + j * P:k * 3 * D + (j + 1) * P],
                                silc_sb[:, k:k + 1],
                                start=(hf == 0 and j == 0 and k == 0),
                                stop=(hf == 1 and j == 23 and k == KD - 1))
                    s_mm.inc(inst)
                    M[("modh", hf)] = s_mm.n

            @block.scalar
            def _(s):
                sc = Eng(s)
                s.wait_ge(s_mm.h, mg("modh", 1))
                sc.sign(modv_sb[:, 48:56], mod_ps[:, 16:24])
                s_ss.inc(sc.sign(modv_sb[:, 56:64], mod_ps[:, 40:48]))
                M[("gsign",)] = s_ss.n

            @block.vector
            def _(v):
                ve = Eng(v)
                v.wait_ge(s_mm.h, mg("modh", 1))
                ve.cp(modv_sb[:, 0:8], mod_ps[:, 0:8])
                ve.ts(modv_sb[:, 8:16], mod_ps[:, 8:16], 1.0, None, ALU.add)
                ve.cp(modv_sb[:, 24:32], mod_ps[:, 24:32])
                ve.ts(modv_sb[:, 32:40], mod_ps[:, 32:40], 1.0, None, ALU.add)
                v.wait_ge(s_ss.h, mg("gsign"))
                for src, dst, sg in ((16, 16, 48), (40, 40, 56)):
                    ve.ts(modv_sb[:, dst:dst + 8], mod_ps[:, src:src + 8],
                          -1.0, None, ALU.mult)
                    ve.tt(modv_sb[:, dst:dst + 8], modv_sb[:, dst:dst + 8],
                          mod_ps[:, src:src + 8], ALU.max)
                    ve.ts(modv_sb[:, dst:dst + 8], modv_sb[:, dst:dst + 8],
                          0.25, None, ALU.max)
                    ve.tt(modv_sb[:, dst:dst + 8], modv_sb[:, dst:dst + 8],
                          modv_sb[:, sg:sg + 8], ALU.mult)
                s_vv.inc(ve.tt(modv_sb[:, 48:56], modv_sb[:, 16:24],
                               bias_sb[:, 112:120], ALU.mult))
                M[("modv",)] = s_vv.n

        if upto == "setup":
            _finish(block, s_st)
            return nc, dram

        with contextlib.ExitStack() as es:
            L = es.enter_context
            inwm_sb = L(nc.sbuf_tensor("inwm_sb", [P, KD * E], BF16))
            dtw_sb = L(nc.sbuf_tensor("dtw_sb", [P, KE * E], BF16))
            bcdw_sb = L(nc.sbuf_tensor("bcdw_sb", [P, KE * 48], BF16))
            h_sb = L(nc.sbuf_tensor("h_sb", [P, KD * W], BF16))
            hpair_sb = L(nc.sbuf_tensor("hpair_sb", [P, KD * 2], BF16))
            xm_sb = L(nc.sbuf_tensor("xm_sb", [P, KE * W], BF16))

            xc_sb = L(nc.sbuf_tensor("xc_sb", [P, KE * TS], BF16))
            sp_sb = L(nc.sbuf_tensor("sp_sb", [P, 2 * W], BF16))
            xsq_sb = sp_sb
            expz_sb = L(nc.sbuf_tensor("expz_sb", [P, TS], BF16))
            bc_sb = L(nc.sbuf_tensor("bc_sb", [P, 2 * W], BF16))
            cs1_sb = L(nc.sbuf_tensor("cs1_sb", [P, W], F32))
            cs2_sb = L(nc.sbuf_tensor("cs2_sb", [P, W], F32))
            cs3_sb = L(nc.sbuf_tensor("cs3_sb", [P, TS], F32))
            cs4_sb = L(nc.sbuf_tensor("cs4_sb", [P, TS], F32))
            bt_sb = L(nc.sbuf_tensor("bt_sb", [S, TS], BF16))
            ct_sb = L(nc.sbuf_tensor("ct_sb", [S, TS], BF16))
            dt2_sb = L(nc.sbuf_tensor("dt2_sb", [S, TS], BF16))
            u_sb = L(nc.sbuf_tensor("u_sb", [S, TS], BF16))
            xscr_sb = L(nc.sbuf_tensor("xscr_sb", [P, P], F32))
            rm_sb = L(nc.sbuf_tensor("rm_sb", [P, P], F32))
            g_sb = L(nc.sbuf_tensor("g_sb", [P, P], BF16))
            ldc_sb = L(nc.sbuf_tensor("ldc_sb", [P, 1], F32))
            ncl_sb = L(nc.sbuf_tensor("ncl_sb", [P, 1], F32))
            utm_sb = L(nc.sbuf_tensor("utm_sb", [P, S], BF16))
            tsc_sb = L(nc.sbuf_tensor("tsc_sb", [P, S], F32))

            ps_inp = L(nc.psum_tensor("ps_inp", [P, 1024], F32))
            ps_bc = L(nc.psum_tensor("ps_bc", [P, 1024], F32))
            ps_sml = L(nc.psum_tensor("ps_sml", [P, 512], F32))
            ps_stat = L(nc.psum_tensor("ps_stat", [1, 512], F32))
            ps_halo = L(nc.psum_tensor("ps_halo", [P, 64], F32))
            ps_smlb = L(nc.psum_tensor("ps_smlb", [P, 32], BF16))

            tiles = [(1, j) for j in range(NT)] + [(0, j) for j in range(NT)]

            @block.gpsimd
            def _(g):
                g.wait_ge(s_set.h, s_set.n)
                g.wait_ge(s_mm.h, mg("modh", 1))
                s_m1a.inc(g.dma_start(
                    inwm_sb[:].rearrange("p (k n) -> p k n", k=KD), r3(inwm)), 16)
                M[("m1wa",)] = s_m1a.n
                s_set.inc(g.dma_start(
                    dtw_sb[:].rearrange("p (k n) -> p k n", k=KE), r3(dtw)), 16)
                s_set.inc(g.dma_start(
                    bcdw_sb[:].rearrange("p (k n) -> p k n", k=KE), r3(bcdw)), 16)
                M[("m1w",)] = s_set.n
                for i, (part, j) in enumerate(tiles):
                    if i < 2:
                        continue
                    base = part * XW + j * TS
                    g.wait_ge(s_fx.h, i - 1)
                    sem = s_lxa if i % 2 == 0 else s_lxb
                    sem.inc(g.dma_start(
                        x_buf[:, (i % 2) * KD * W:((i % 2) + 1) * KD * W]
                        .rearrange("p (k n) -> p k n", k=KD),
                        r3(x)[:, :, base:base + W]), 16)

            @block.tensor
            def _(t):
                for i, (part, j) in enumerate(tiles):
                    own = (part == 0)
                    xo = (i % 2) * KD * W
                    sem = s_lxa if i % 2 == 0 else s_lxb
                    t.wait_ge(sem.h, 16 * (i // 2 + 1))
                    if i >= 1:
                        t.wait_ge(s_ss.h, mg("bcde", i - 1, 0))
                        t.wait_ge(s_ss.h, mg("bcde", i - 1, 1))
                        t.wait_ge(s_ss.h, mg("bcde", i - 1, 2))
                        t.wait_ge(s_vv.h, mg("stitch", i - 1))
                    for k in range(KD):
                        t.matmul(ps_stat[0:1, 0:512], onesb_sb[:, 0:1],
                                 x_buf[:, xo + k * W:xo + k * W + 512],
                                 start=(k == 0), stop=(k == KD - 1))
                    for k in range(KD):
                        inst = t.matmul(ps_halo[0:1, 32:34], onesb_sb[:, 0:1],
                                        x_buf[:, xo + k * W + 512:xo + k * W + 514],
                                        start=(k == 0), stop=(k == KD - 1))
                    s_mm.inc(inst)
                    M[("sums", i)] = s_mm.n
                    t.wait_ge(s_vv.h, mg("meancp", i))
                    for k in range(KD):
                        t.wait_ge(s_ss.h, mg("xsq", i, k))
                        t.matmul(ps_stat[0:1, 0:512], onesb_sb[:, 0:1],
                                 xsq_sb[:, (k % 2) * W:(k % 2) * W + 512],
                                 start=(k == 0), stop=(k == KD - 1))
                        inst = t.matmul(ps_halo[0:1, 32:34], onesb_sb[:, 0:1],
                                        xsq_sb[:, (k % 2) * W + 512:(k % 2) * W + 514],
                                        start=(k == 0), stop=(k == KD - 1))
                        s_mm.inc(inst)
                        M[("sq", i, k)] = s_mm.n
                    t.wait_ge(s_vv.h, mg("arow", i))
                    t.matmul(ps_bc[:, 0:512], ones_sb[0:1, :],
                             rows_sb[0:1, 1028:1540], start=True, stop=True)
                    s_mm.inc(t.matmul(ps_bc[:, 512:514], ones_sb[0:1, :],
                                      rows_sb[0:1, 1540:1542],
                                      start=True, stop=True))
                    M[("abc", i)] = s_mm.n
                    t.wait_ge(s_vv.h, mg("acp", i))
                    t.matmul(ps_bc[:, 0:512], ones_sb[0:1, :],
                             rows_sb[0:1, 1542:2054], start=True, stop=True)
                    s_mm.inc(t.matmul(ps_bc[:, 512:514], ones_sb[0:1, :],
                                      rows_sb[0:1, 2054:2056],
                                      start=True, stop=True))
                    M[("bbc", i)] = s_mm.n
                    if M1LVL < 2:
                        continue
                    if i == 0:
                        t.wait_ge(s_m1a.h, mg("m1wa"))
                    t.wait_ge(s_vv.h, mg("happ", i))
                    hsrc = h_sb
                    ho = 0
                    for m in range(KE):
                        if m >= 2:
                            t.wait_ge(s_ss.h, mg("xme", i, m - 2))
                        for k in range(KD):
                            inst = t.matmul(
                                ps_inp[:, (m % 2) * 512:(m % 2) * 512 + 512],
                                inwm_sb[:, k * E + m * P:k * E + (m + 1) * P],
                                hsrc[:, ho + k * W + 1:ho + k * W + 513],
                                start=(k == 0), stop=(k == KD - 1))
                        s_mm.inc(inst)
                        M[("inp", i, m)] = s_mm.n
                        for k in range(KD):
                            inst = t.matmul(
                                ps_halo[:, m * 2:m * 2 + 2],
                                inwm_sb[:, k * E + m * P:k * E + (m + 1) * P],
                                hpair_sb[:, k * 2:k * 2 + 2],
                                start=(m == 0 and k == 0),
                                stop=(m == KE - 1 and k == KD - 1))
                    s_mm.inc(inst)
                    M[("halo", i)] = s_mm.n
                    if M1LVL < 3:
                        continue
                    if i == 0:
                        t.wait_ge(s_set.h, mg("m1w"))
                    for m in range(KE):
                        if m >= 2:
                            t.wait_ge(s_ss.h, mg("spe", i, m - 2))
                        for k in range(KE):
                            t.wait_ge(s_ss.h, mg("xc", i, k))
                            inst = t.matmul(
                                ps_inp[:, (m % 2) * 512:(m % 2) * 512 + 512],
                                dtw_sb[:, k * E + m * P:k * E + (m + 1) * P],
                                xc_sb[:, k * TS:(k + 1) * TS],
                                start=(k == 0), stop=(k == KE - 1))
                        s_mm.inc(inst)
                        M[("dt", i, m)] = s_mm.n
                        if m >= 1:
                            t.wait_ge(s_ss.h, mg("spe", i, m - 1))
                            inst = t.matmul(ps_stat[0:1, 0:512], onesb_sb[:, 0:1],
                                            sp_sb[:, ((m - 1) % 2) * W:
                                                  ((m - 1) % 2) * W + TS],
                                            start=(m == 1), stop=False)
                            s_mm.inc(inst)
                            M[("dtma", i, m - 1)] = s_mm.n
                    t.wait_ge(s_ss.h, mg("spe", i, KE - 1))
                    inst = t.matmul(ps_stat[0:1, 0:512], onesb_sb[:, 0:1],
                                    sp_sb[:, ((KE - 1) % 2) * W:
                                          ((KE - 1) % 2) * W + TS],
                                    start=False, stop=True)
                    s_mm.inc(inst)
                    M[("dtma", i, KE - 1)] = s_mm.n
                    M[("dtm", i)] = s_mm.n
                    if M1LVL < 4:
                        continue
                    t.wait_ge(s_ss.h, mg("spe", i, KE - 1))
                    projs = [0, 2] if not own else [0, 1, 2]
                    for pi in projs:
                        col0 = 16 * pi
                        dst = (ps_inp[0:S, 0:512], ps_inp[0:S, 512:1024],
                               ps_bc[0:S, 0:512])[pi]
                        for k in range(KE):
                            inst = t.matmul(
                                dst, bcdw_sb[:, k * 48 + col0:k * 48 + col0 + S],
                                xc_sb[:, k * TS:(k + 1) * TS],
                                start=(k == 0), stop=(k == KE - 1))
                        s_mm.inc(inst)
                        M[("bcd", i, pi)] = s_mm.n
                    if M1LVL < 5:
                        continue
                    for c in range(CPT):
                        t.wait_ge(s_vv.h, mg("ldrow", i, c))
                        s_mm.inc(t.transpose(
                            ps_sml[:, 128:129],
                            rows_sb[0:1, 514 + c * P:514 + (c + 1) * P],
                            idf_sb[0:1, 0:1]))
                        t.drain()
                        M[("ldtr", i, c)] = s_mm.n
                        if M1LVL < 6:
                            continue
                        t.wait_ge(s_vv.h, mg("ldcp", i, c))
                        s_mm.inc(t.matmul(ps_sml[:, 129:130], ut_sb[:],
                                          ldc_sb[:], start=True, stop=True))
                        t.drain()
                        M[("cl", i, c)] = s_mm.n
                        if M1LVL < 7:
                            continue
                        t.wait_ge(s_vv.h, mg("xscr", i, c))
                        t.wait_ge(s_vv.h, mg("nclcp", i, c))
                        s_mm.inc(t.matmul(ps_sml[:, 0:128], ones_sb[:],
                                          xscr_sb[:], start=True, stop=True))
                        t.drain()
                        M[("R", i, c)] = s_mm.n
                        if M1LVL < 8:
                            continue
                        t.wait_ge(s_vv.h, mg("rmcp", i, c))
                        t.wait_ge(s_ss.h, mg("expcl", i, c))
                        t.wait_ge(s_vv.h, mg("urdy", i))
                        if "utr" not in M1SKIP:
                            s_mm.inc(t.transpose(ps_smlb[:, 0:16],
                                                 u_sb[0:S, c * P:(c + 1) * P],
                                                 idb_sb[0:S, 0:S]))
                            t.drain()
                            M[("utr", i, c)] = s_mm.n
                        if M1LVL < 9:
                            continue
                        t.wait_ge(s_vv.h, mg("ucp", i, c))
                        if own and "ctr" not in M1SKIP:
                            s_mm.inc(t.transpose(ps_smlb[:, 16:32],
                                                 ct_sb[0:S, c * P:(c + 1) * P],
                                                 idb_sb[0:S, 0:S]))
                            t.drain()
                            M[("ctr", i, c)] = s_mm.n
                            t.wait_ge(s_vv.h, mg("ccp", i, c))
                            t.wait_ge(s_ss.h, mg("G", i, c))
                            if "gu" not in M1SKIP:
                                s_mm.inc(t.matmul(ps_sml[:, 176:192], g_sb[:],
                                                  utm_sb[:], start=True, stop=True))
                                t.drain()
                                M[("GU", i, c)] = s_mm.n
                            t.wait_ge(s_vv.h, mg("yscp", i, c))
                        else:
                            t.wait_ge(s_ss.h, mg("G", i, c))
                        if "bk" not in M1SKIP:
                            s_mm.inc(t.matmul(ps_sml[0:1, 192:208], g_sb[:, 127:128],
                                              utm_sb[:], start=True, stop=True))
                            t.drain()
                            M[("bk", i, c)] = s_mm.n
                        t.wait_ge(s_vv.h, mg("bcp", i, c))
                M[("m1mm",)] = s_mm.n

            @block.scalar
            def _(s):
                sc = Eng(s)
                for i, (part, j) in enumerate(tiles):
                    own = (part == 0)
                    xo = (i % 2) * KD * W
                    sem = s_lxa if i % 2 == 0 else s_lxb
                    s.wait_ge(sem.h, 16 * (i // 2 + 1))
                    if i >= 1:
                        s.wait_ge(s_mm.h, mg("dtm", i - 1))
                    for k in range(KD):
                        if k >= 2:
                            s.wait_ge(s_mm.h, mg("sq", i, k - 2))
                        s_ss.inc(sc.act(xsq_sb[:, (k % 2) * W:(k % 2 + 1) * W],
                                        x_buf[:, xo + k * W:xo + (k + 1) * W],
                                        AF.Square))
                        M[("xsq", i, k)] = s_ss.n
                    s.wait_ge(s_vv.h, mg("var", i))
                    s_ss.inc(sc.act(rows_sb[0:1, 1542:2056],
                                    rows_sb[0:1, 514:1028], AF.Sqrt))
                    M[("sqrt", i)] = s_ss.n
                    if M1LVL < 2:
                        continue
                    for m in range(KE):
                        s.wait_ge(s_mm.h, mg("inp", i, m))
                        s_ss.inc(sc.act(xm_sb[:, m * W + 1:m * W + 513],
                                        ps_inp[:, (m % 2) * 512:(m % 2) * 512 + 512],
                                        AF.Identity, bias=bias_sb[:, m:m + 1]))
                        M[("xme", i, m)] = s_ss.n
                    for k in range(KE):
                        s.wait_ge(s_vv.h, mg("conv", i, k))
                        src = (cs1_sb if k % 2 == 0 else cs3_sb)
                        s_ss.inc(sc.act(xc_sb[:, k * TS:(k + 1) * TS],
                                        src[:, 0:TS], AF.Silu,
                                        bias=bias_sb[:, 80 + k:80 + k + 1]))
                        M[("xc", i, k)] = s_ss.n
                    if M1LVL < 3:
                        continue
                    for m in range(KE):
                        s.wait_ge(s_mm.h, mg("dt", i, m))
                        if m >= 2:
                            s.wait_ge(s_mm.h, mg("dtma", i, m - 2))
                        sc.act(expz_sb[:],
                               ps_inp[:, (m % 2) * 512:(m % 2) * 512 + 512],
                               AF.Exp, bias=bias_sb[:, 96 + m:96 + m + 1])
                        s_ss.inc(sc.act(sp_sb[:, (m % 2) * W:(m % 2) * W + TS],
                                        expz_sb[:], AF.Ln, bias=1.0))
                        M[("spe", i, m)] = s_ss.n
                    if M1LVL < 4:
                        continue
                    projs = [0, 2] if not own else [0, 1, 2]
                    for pi in projs:
                        s.wait_ge(s_mm.h, mg("bcd", i, pi))
                        src = (ps_inp[0:S, 0:512], ps_inp[0:S, 512:1024],
                               ps_bc[0:S, 0:512])[pi]
                        dst = (bt_sb, ct_sb, dt2_sb)[pi]
                        s_ss.inc(sc.act(dst[:], src, AF.Identity,
                                        bias=bcdb_sb[:, pi:pi + 1]))
                        M[("bcde", i, pi)] = s_ss.n
                    if not own:
                        M[("bcde", i, 1)] = s_ss.n
                    if M1LVL < 5:
                        continue
                    for c in range(CPT):
                        if M1LVL < 8:
                            continue
                        ch = j * CPT + c
                        hoff = 0 if own else TPH
                        if "acts" not in M1SKIP:
                            s.wait_ge(s_vv.h, mg("rmcp", i, c))
                            sc.act(expcl_sb[0:1, hoff + ch * P:hoff + (ch + 1) * P],
                                   rows_sb[0:1, 1542:1670], AF.Exp)
                            hoc = 0 if own else NCH
                            s_ss.inc(sc.act(aall_sb[0:1, hoc + ch:hoc + ch + 1],
                                            rows_sb[0:1, 1669:1670], AF.Exp))
                            M[("expcl", i, c)] = s_ss.n
                        if "g" not in M1SKIP:
                            s.wait_ge(s_vv.h, mg("rmcp", i, c))
                            s_ss.inc(sc.act(g_sb[:], rm_sb[:], AF.Exp))
                            M[("G", i, c)] = s_ss.n
                M[("m1ssend",)] = s_ss.n

            @block.vector
            def _(v):
                ve = Eng(v)
                for i, (part, j) in enumerate(tiles):
                    own = (part == 0)
                    xo = (i % 2) * KD * W
                    sem = s_lxa if i % 2 == 0 else s_lxb
                    v.wait_ge(sem.h, 16 * (i // 2 + 1))
                    v.wait_ge(s_mm.h, mg("sums", i))
                    ve.ts(rows_sb[0:1, 0:512], ps_stat[0:1, 0:512],
                          1.0 / D, None, ALU.mult)
                    s_vv.inc(ve.ts(rows_sb[0:1, 512:514], ps_halo[0:1, 32:34],
                                   1.0 / D, None, ALU.mult))
                    M[("meancp", i)] = s_vv.n
                    v.wait_ge(s_mm.h, mg("sq", i, KD - 1))
                    ve.ts(rows_sb[0:1, 514:1026], ps_stat[0:1, 0:512],
                          1.0 / D, None, ALU.mult)
                    ve.ts(rows_sb[0:1, 1026:1028], ps_halo[0:1, 32:34],
                          1.0 / D, None, ALU.mult)
                    ve.tt(rows_sb[0:1, 1028:1542], rows_sb[0:1, 0:514],
                          rows_sb[0:1, 0:514], ALU.mult)
                    ve.tt(rows_sb[0:1, 514:1028], rows_sb[0:1, 514:1028],
                          rows_sb[0:1, 1028:1542], ALU.subtract)
                    s_vv.inc(ve.ts(rows_sb[0:1, 514:1028],
                                   rows_sb[0:1, 514:1028], 1e-5, None, ALU.add))
                    M[("var", i)] = s_vv.n
                    v.wait_ge(s_ss.h, mg("sqrt", i))
                    ve.recip(rows_sb[0:1, 1028:1542], rows_sb[0:1, 1542:2056])
                    ve.tt(rows_sb[0:1, 1542:2056], rows_sb[0:1, 0:514],
                          rows_sb[0:1, 1028:1542], ALU.mult)
                    s_vv.inc(ve.ts(rows_sb[0:1, 1542:2056],
                                   rows_sb[0:1, 1542:2056], -1.0, None, ALU.mult))
                    M[("arow", i)] = s_vv.n
                    v.wait_ge(s_mm.h, mg("abc", i))
                    s_vv.inc(ve.cp(bc_sb[:, 0:W], ps_bc[:, 0:514]))
                    M[("acp", i)] = s_vv.n
                    v.wait_ge(s_mm.h, mg("bbc", i))
                    if i >= 1:
                        v.wait_ge(s_mm.h, mg("halo", i - 1))
                    ve.cp(bc_sb[:, W:2 * W], ps_bc[:, 0:514])
                    hdst = h_sb
                    ho = 0
                    for k in range(KD):
                        ve.tt(cs1_sb[:], x_buf[:, xo + k * W:xo + (k + 1) * W],
                              bc_sb[:, 0:W], ALU.mult)
                        ve.tt(cs2_sb[:], cs1_sb[:], bc_sb[:, W:2 * W], ALU.add)
                        ve.ts(hdst[:, ho + k * W:ho + (k + 1) * W], cs2_sb[:],
                              modv_sb[:, 8 + k:8 + k + 1],
                              modv_sb[:, k:k + 1], ALU.mult, ALU.add)
                    ve.cp(hpair_sb[:],
                          bass.AP(h_sb, ho, [[KD * W, P], [W, KD], [513, 2]]))
                    s_vv.inc(ve.dinc())
                    M[("happ", i)] = s_vv.n
                    s_fx.inc(ve.dinc())
                    if M1LVL < 2:
                        continue
                    v.wait_ge(s_mm.h, mg("halo", i))
                    lmask = (part * 2) if j == 0 else 4
                    rmask = (part * 2 + 1) if j == NT - 1 else 4
                    for m in range(KE):
                        for hc, col, mi in ((0, 0, lmask), (1, 513, rmask)):
                            if mi == 4:
                                ve.ts(xm_sb[:, m * W + col:m * W + col + 1],
                                      ps_halo[:, m * 2 + hc:m * 2 + hc + 1],
                                      bias_sb[:, m:m + 1], None, ALU.add)
                            else:
                                ve.ts(xm_sb[:, m * W + col:m * W + col + 1],
                                      ps_halo[:, m * 2 + hc:m * 2 + hc + 1],
                                      bias_sb[:, m:m + 1],
                                      hm_sb[:, mi:mi + 1], ALU.add, ALU.mult)
                    s_vv.inc(ve.dinc())
                    M[("stitch", i)] = s_vv.n
                    for k in range(KE):
                        if k >= 2:
                            v.wait_ge(s_ss.h, mg("xc", i, k - 2))
                        v.wait_ge(s_ss.h, mg("xme", i, k))
                        if k == 0:
                            v.wait_ge(s_vv.h, mg("stitch", i))
                        acc = cs1_sb if k % 2 == 0 else cs3_sb
                        oth = cs2_sb if k % 2 == 0 else cs4_sb
                        ve.ts(oth[:, 0:TS], xm_sb[:, k * W + 2:k * W + 2 + TS],
                              bias_sb[:, 64 + k:64 + k + 1], None, ALU.mult)
                        ve.stt(acc[:, 0:TS], xm_sb[:, k * W + 1:k * W + 1 + TS],
                               bias_sb[:, 48 + k:48 + k + 1], oth[:, 0:TS],
                               ALU.mult, ALU.add)
                        s_vv.inc(ve.stt(acc[:, 0:TS], xm_sb[:, k * W:k * W + TS],
                                        bias_sb[:, 32 + k:32 + k + 1],
                                        acc[:, 0:TS], ALU.mult, ALU.add))
                        M[("conv", i, k)] = s_vv.n
                    if M1LVL < 4:
                        continue
                    v.wait_ge(s_ss.h, mg("bcde", i, 0))
                    v.wait_ge(s_ss.h, mg("bcde", i, 2))
                    s_vv.inc(ve.tt(u_sb[:], bt_sb[:], dt2_sb[:], ALU.mult))
                    M[("urdy", i)] = s_vv.n
                    if M1LVL < 5:
                        continue
                    for c in range(CPT):
                        ch = j * CPT + c
                        hoff = 0 if own else NCH * S
                        v.wait_ge(s_mm.h, mg("dtm", i))
                        s_vv.inc(ve.ts(
                            rows_sb[0:1, 514 + c * P:514 + (c + 1) * P],
                            ps_stat[0:1, c * P:(c + 1) * P],
                            -1.0 / E, -10.0, ALU.mult, ALU.max))
                        M[("ldrow", i, c)] = s_vv.n
                        if M1LVL < 6:
                            continue
                        v.wait_ge(s_mm.h, mg("ldtr", i, c))
                        ve.cp(ldc_sb[:], ps_sml[:, 128:129])
                        s_vv.inc(ve.ts(xscr_sb[:], ut_sb[:], ldc_sb[:, 0:1],
                                       None, ALU.mult))
                        M[("ldcp", i, c)] = s_vv.n
                        M[("xscr", i, c)] = s_vv.n
                        if M1LVL < 7:
                            continue
                        v.wait_ge(s_mm.h, mg("cl", i, c))
                        s_vv.inc(ve.ts(ncl_sb[:], ps_sml[:, 129:130],
                                       -1.0, None, ALU.mult))
                        M[("nclcp", i, c)] = s_vv.n
                        v.wait_ge(s_mm.h, mg("R", i, c))
                        ve.tt(rm_sb[:], ps_sml[:, 0:128], mk_sb[:], ALU.add)
                        ve.ts(rm_sb[:], rm_sb[:], ncl_sb[:, 0:1], -80.0,
                              ALU.add, ALU.max)
                        s_vv.inc(ve.ts(rows_sb[0:1, 1542:1670],
                                       ps_sml[0:1, 0:128], -80.0, None,
                                       ALU.max))
                        M[("rmcp", i, c)] = s_vv.n
                        if M1LVL < 8:
                            continue
                        if "utr" not in M1SKIP:
                            v.wait_ge(s_mm.h, mg("utr", i, c))
                            s_vv.inc(ve.cp(utm_sb[:], ps_smlb[:, 0:16]))
                            M[("ucp", i, c)] = s_vv.n
                        if M1LVL < 9:
                            continue
                        if own:
                            if "ctr" not in M1SKIP:
                                v.wait_ge(s_mm.h, mg("ctr", i, c))
                                s_vv.inc(ve.cp(call_sb[:, ch * S:(ch + 1) * S],
                                               ps_smlb[:, 16:32]))
                                M[("ccp", i, c)] = s_vv.n
                            if "gu" not in M1SKIP and "yscp" not in M1SKIP:
                                v.wait_ge(s_mm.h, mg("GU", i, c))
                                ve.tt(tsc_sb[:], call_sb[:, ch * S:(ch + 1) * S],
                                      ps_sml[:, 176:192], ALU.mult)
                                s_vv.inc(ve.rsum(ysi_sb[:, ch:ch + 1], tsc_sb[:]))
                                M[("yscp", i, c)] = s_vv.n
                        if "bk" not in M1SKIP:
                            v.wait_ge(s_mm.h, mg("bk", i, c))
                            s_vv.inc(ve.cp(
                                ball_sb[0:1, hoff + ch * S:hoff + (ch + 1) * S],
                                ps_sml[0:1, 192:208]))
                            M[("bcp", i, c)] = s_vv.n

        if upto == "m1":
            if dbg:
                _dbg_emit(block, dbg_rows, dbg_cols, expcl_sb, ball_sb, ysi_sb,
                          ysr_sb, None, s_vv, s_ss, s_gf, zero_rows=True)
            _finish(block, s_st)
            return nc, dram

        with contextlib.ExitStack() as es:
            L = es.enter_context
            inwg_sb = L(nc.sbuf_tensor("inwg_sb", [P, KD * E], BF16))
            outw_sb = L(nc.sbuf_tensor("outw_sb", [P, KE * D], BF16))
            hg_sb = L(nc.sbuf_tensor("hg_sb", [P, KD * TS], BF16))
            xsq2_sb = L(nc.sbuf_tensor("xsq2_sb", [P, 2 * TS], BF16))
            xq2_sb = L(nc.sbuf_tensor("xq2_sb", [P, 2 * TS], BF16))
            wr_sb = L(nc.sbuf_tensor("wr_sb", [P, KD * 8], BF16))
            sg_sb = L(nc.sbuf_tensor("sg_sb", [P, KE * TS], BF16))
            ap_sb = L(nc.sbuf_tensor("ap_sb", [P, KD * TS], BF16))
            x2_sb = L(nc.sbuf_tensor("x2_sb", [P, KD * TS], F32))
            x2b_sb = L(nc.sbuf_tensor("x2b_sb", [P, KD * TS], BF16))
            h2t_sb = L(nc.sbuf_tensor("h2t_sb", [P, KD * TS], BF16))
            rt_sb = L(nc.sbuf_tensor("rt_sb", [8, TS], F32))
            rtc_sb = L(nc.sbuf_tensor("rtc_sb", [P, 8], F32))
            mx_sb = L(nc.sbuf_tensor("mx_sb", [P, 8], F32))
            em2_sb = L(nc.sbuf_tensor("em2_sb", [P, 2], F32))
            mi8_sb = L(nc.sbuf_tensor("mi8_sb", [P, 8], U32))
            emr_sb = L(nc.sbuf_tensor("emr_sb", [2, TS], F32))
            mcs1_sb = L(nc.sbuf_tensor("mcs1", [P, TS], F32))
            mcs2_sb = L(nc.sbuf_tensor("mcs2", [P, TS], F32))
            bcg_sb = L(nc.sbuf_tensor("bcg_sb", [P, 2 * TS], BF16))
            bc2_sb = L(nc.sbuf_tensor("bc2_sb", [P, 2 * TS], BF16))
            p1s_sb = L(nc.sbuf_tensor("p1s_sb", [P, S], F32))
            tsc2_sb = L(nc.sbuf_tensor("tsc2_sb", [P, S], F32))

            ps_g = L(nc.psum_tensor("ps_g", [P, 1024], F32))
            ps_b2 = L(nc.psum_tensor("ps_b2", [P, 1024], F32))
            ps_s2 = L(nc.psum_tensor("ps_s2", [P, 512], F32))
            ps_st2 = L(nc.psum_tensor("ps_st2", [1, 516], F32))
            ps_tb = L(nc.psum_tensor("ps_tb", [P, 8], F32))

            @block.gpsimd
            def _(g):
                g.wait_ge(s_mm.h, mg("m1mm"))
                s_m2w.inc(g.dma_start(
                    inwg_sb[:].rearrange("p (k n) -> p k n", k=KD), r3(inwg)), 16)
                s_m2w.inc(g.dma_start(
                    outw_sb[:].rearrange("p (k n) -> p k n", k=KE), r3(outw)), 16)
                s_m2w.inc(g.dma_start(
                    wr_sb[:].rearrange("p (k n) -> p k n", k=KD), r3(wr)), 16)
                for j in range(NT):
                    i = 2 * NT + j
                    g.wait_ge(s_fx.h, i - 1)
                    sem = s_lxa if i % 2 == 0 else s_lxb
                    sem.inc(g.dma_start(
                        x_buf[:, (i % 2) * KD * W:((i % 2) + 1) * KD * W]
                        .rearrange("p (k n) -> p k n", k=KD),
                        r3(x)[:, :, j * TS:j * TS + W]), 16)
                for j in range(NT):
                    g.wait_ge(s_vv.h, mg("emrdy", j))
                    M[("m2gf_pre", j)] = s_gf.n
                    s_gf.inc(g.dma_start(emr_d[0:2, j * TS:(j + 1) * TS],
                                         emr_sb[:]), 16)
                    M[("m2gf", j)] = s_gf.n
                M[("m2gf",)] = s_gf.n

            @block.tensor
            def _(t):
                for k in range(NCH):
                    t.wait_ge(s_vv.h, mg("schain", k))
                    s_mm.inc(t.matmul(ps_s2[:, 0:S],
                                      expcl_sb[0:1, k * P:(k + 1) * P],
                                      srowb_sb[0:1, (k % 2) * S:(k % 2) * S + S],
                                      start=True, stop=True))
                    M[("P1", k)] = s_mm.n
                    t.wait_ge(s_vv.h, mg("ysc", k))
                    s_mm.inc(t.transpose(ps_s2[0:1, 256:384], p1s_sb[:, 1:2],
                                         idf_sb[:]))
                    t.drain()
                    M[("ystr", k)] = s_mm.n
                    t.wait_ge(s_vv.h, mg("ysrcp", k))

                def t_front(j):
                    i = 2 * NT + j
                    sem = s_lxa if i % 2 == 0 else s_lxb
                    t.wait_ge(sem.h, 16 * (i // 2 + 1))
                    xo = (i % 2) * KD * W
                    if j == 1:
                        t.wait_ge(s_vv.h, mg("2var", 0))
                    elif j >= 2:
                        t.wait_ge(s_vv.h, mg("var2", j - 2))
                    for k in range(KD):
                        inst = t.matmul(ps_st2[0:1, 0:512], onesb_sb[:, 0:1],
                                        x_buf[:, xo + k * W + 1:xo + k * W + 513],
                                        start=(k == 0), stop=(k == KD - 1))
                    s_mm.inc(inst)
                    M[("2sums", j)] = s_mm.n
                    t.wait_ge(s_vv.h, mg("2mean", j))
                    for k in range(KD):
                        t.wait_ge(s_ss.h, mg("2xsq", j, k))
                        inst = t.matmul(ps_st2[0:1, 0:512], onesb_sb[:, 0:1],
                                        xsq2_sb[:, (k % 2) * TS:(k % 2) * TS + TS],
                                        start=(k == 0), stop=(k == KD - 1))
                        s_mm.inc(inst)
                        M[("2sq", j, k)] = s_mm.n
                    t.wait_ge(s_vv.h, mg("2arow", j))
                    if j == 1:
                        t.wait_ge(s_vv.h, mg("2happ", 0))
                    elif j >= 2:
                        t.wait_ge(s_vv.h, mg("h2", j - 2))
                    s_mm.inc(t.matmul(ps_b2[:, 512:1024], ones_sb[0:1, :],
                                      rows_sb[0:1, 1028:1540],
                                      start=True, stop=True))
                    M[("2abc", j)] = s_mm.n
                    t.wait_ge(s_vv.h, mg("2acp", j))
                    s_mm.inc(t.matmul(ps_b2[:, 512:1024], ones_sb[0:1, :],
                                      rows_sb[0:1, 1542:2054],
                                      start=True, stop=True))
                    M[("2bbc", j)] = s_mm.n
                    t.wait_ge(s_vv.h, mg("2happ", j))
                    if j == 0:
                        t.wait_ge(s_m2w.h, s_m2w.n)
                    if j >= 1:
                        t.wait_ge(s_ss.h, mg("ape", j - 1, KD - 1))
                    for m in range(KE):
                        if m >= 2:
                            t.wait_ge(s_ss.h, mg("sge", j, m - 2))
                        for k in range(KD):
                            inst = t.matmul(
                                ps_g[:, (m % 2) * 512:(m % 2) * 512 + 512],
                                inwg_sb[:, k * E + m * P:k * E + (m + 1) * P],
                                hg_sb[:, k * TS:(k + 1) * TS],
                                start=(k == 0), stop=(k == KD - 1))
                        s_mm.inc(inst)
                        M[("gate", j, m)] = s_mm.n
                    t.wait_ge(s_ss.h, mg("sge", j, KE - 1))
                    for m in range(KD):
                        if m >= 2:
                            t.wait_ge(s_ss.h, mg("ape", j, m - 2))
                        for k in range(KE):
                            inst = t.matmul(
                                ps_g[:, (m % 2) * 512:(m % 2) * 512 + 512],
                                outw_sb[:, k * D + m * P:k * D + (m + 1) * P],
                                sg_sb[:, k * TS:(k + 1) * TS],
                                start=(k == 0), stop=(k == KE - 1))
                        s_mm.inc(inst)
                        M[("op", j, m)] = s_mm.n
                    t.wait_ge(s_vv.h, mg("ysrcp", NCH - 1))
                    if j >= 1:
                        t.wait_ge(s_vv.h, mg("x2", j - 1))
                    s_mm.inc(t.matmul(ps_b2[:, 0:512], onesb_sb[0:1, :],
                                      ysr_sb[0:1, j * TS:(j + 1) * TS],
                                      start=True, stop=True))
                    M[("ysb", j)] = s_mm.n

                def t_back(j):
                    t.wait_ge(s_vv.h, mg("x2b", j))
                    if j + 1 < NT:
                        t.wait_ge(s_vv.h, mg("2var", j + 1))
                    for k in range(KD):
                        inst = t.matmul(ps_st2[0:1, 0:512], onesb_sb[:, 0:1],
                                        x2b_sb[:, k * TS:(k + 1) * TS],
                                        start=(k == 0), stop=(k == KD - 1))
                    s_mm.inc(inst)
                    M[("sums2", j)] = s_mm.n
                    t.wait_ge(s_vv.h, mg("mean2", j))
                    for k in range(KD):
                        t.wait_ge(s_ss.h, mg("x2sqk", j, k))
                        inst = t.matmul(ps_st2[0:1, 0:512], onesb_sb[:, 0:1],
                                        xq2_sb[:, (k % 2) * TS:(k % 2) * TS + TS],
                                        start=(k == 0), stop=(k == KD - 1))
                        s_mm.inc(inst)
                        M[("sq2", j, k)] = s_mm.n
                    t.wait_ge(s_vv.h, mg("arow2", j))
                    if j + 1 < NT:
                        t.wait_ge(s_vv.h, mg("2happ", j + 1))
                    s_mm.inc(t.matmul(ps_b2[:, 512:1024], ones_sb[0:1, :],
                                      rows_sb[0:1, 1028:1540],
                                      start=True, stop=True))
                    M[("abc2", j)] = s_mm.n
                    t.wait_ge(s_vv.h, mg("acp2", j))
                    s_mm.inc(t.matmul(ps_b2[:, 512:1024], ones_sb[0:1, :],
                                      rows_sb[0:1, 1542:2054],
                                      start=True, stop=True))
                    M[("bbc2", j)] = s_mm.n
                    t.wait_ge(s_vv.h, mg("h2", j))
                    for k in range(KD):
                        inst = t.matmul(ps_s2[0:8, 0:512],
                                        wr_sb[:, k * 8:(k + 1) * 8],
                                        h2t_sb[:, k * TS:(k + 1) * TS],
                                        start=(k == 0), stop=(k == KD - 1))
                    s_mm.inc(inst)
                    M[("route", j)] = s_mm.n
                    for c in range(CPT):
                        t.wait_ge(s_ss.h, mg("rte", j))
                        if c >= 1:
                            t.wait_ge(s_vv.h, mg("em2", j, c - 1))
                        s_mm.inc(t.transpose(ps_tb[:],
                                             rt_sb[0:8, c * P:(c + 1) * P],
                                             idf_sb[0:8, 0:8]))
                        t.drain()
                        M[("rtt", j, c)] = s_mm.n
                        t.wait_ge(s_vv.h, mg("em2", j, c))
                        s_mm.inc(t.transpose(ps_s2[0:2, 256:384], em2_sb[:],
                                             idf_sb[:]))
                        t.drain()
                        M[("emt", j, c)] = s_mm.n
                        t.wait_ge(s_vv.h, mg("emcp", j, c))

                t_front(0)
                for j in range(NT):
                    if j + 1 < NT:
                        t_front(j + 1)
                    t_back(j)

            @block.scalar
            def _(s):
                sc = Eng(s)

                def s_front(j):
                    i = 2 * NT + j
                    sem = s_lxa if i % 2 == 0 else s_lxb
                    s.wait_ge(sem.h, 16 * (i // 2 + 1))
                    xo = (i % 2) * KD * W
                    if j >= 1:
                        s.wait_ge(s_mm.h, mg("2sq", j - 1, KD - 1))
                    for k in range(KD):
                        if k >= 2:
                            s.wait_ge(s_mm.h, mg("2sq", j, k - 2))
                        s_ss.inc(sc.act(xsq2_sb[:, (k % 2) * TS:(k % 2 + 1) * TS],
                                        x_buf[:, xo + k * W + 1:xo + k * W + 513],
                                        AF.Square))
                        M[("2xsq", j, k)] = s_ss.n
                    s.wait_ge(s_vv.h, mg("2var", j))
                    if j == 1:
                        s.wait_ge(s_mm.h, mg("2bbc", 0))
                    elif j >= 2:
                        s.wait_ge(s_mm.h, mg("bbc2", j - 2))
                    s_ss.inc(sc.act(rows_sb[0:1, 1542:2054],
                                    rows_sb[0:1, 514:1026], AF.Sqrt))
                    M[("2sqrt", j)] = s_ss.n
                    for m in range(KE):
                        s.wait_ge(s_mm.h, mg("gate", j, m))
                        s_ss.inc(sc.act(sg_sb[:, m * TS:(m + 1) * TS],
                                        ps_g[:, (m % 2) * 512:(m % 2) * 512 + 512],
                                        AF.Silu,
                                        bias=bias_sb[:, 16 + m:16 + m + 1]))
                        M[("sge", j, m)] = s_ss.n
                    for m in range(KD):
                        s.wait_ge(s_mm.h, mg("op", j, m))
                        if j >= 1:
                            s.wait_ge(s_vv.h, mg("x2d", j - 1, m))
                        s_ss.inc(sc.act(ap_sb[:, m * TS:(m + 1) * TS],
                                        ps_g[:, (m % 2) * 512:(m % 2) * 512 + 512],
                                        AF.Identity))
                        M[("ape", j, m)] = s_ss.n

                def s_back(j):
                    s.wait_ge(s_vv.h, mg("x2b", j))
                    if j >= 1:
                        s.wait_ge(s_mm.h, mg("sq2", j - 1, KD - 1))
                    for k in range(KD):
                        if k >= 2:
                            s.wait_ge(s_mm.h, mg("sq2", j, k - 2))
                        s_ss.inc(sc.act(xq2_sb[:, (k % 2) * TS:(k % 2 + 1) * TS],
                                        x2b_sb[:, k * TS:(k + 1) * TS],
                                        AF.Square))
                        M[("x2sqk", j, k)] = s_ss.n
                    s.wait_ge(s_vv.h, mg("var2", j))
                    if j + 1 < NT:
                        s.wait_ge(s_mm.h, mg("2bbc", j + 1))
                    s_ss.inc(sc.act(rows_sb[0:1, 1542:2054],
                                    rows_sb[0:1, 514:1026], AF.Sqrt))
                    M[("sqrt2", j)] = s_ss.n
                    s.wait_ge(s_mm.h, mg("route", j))
                    s_ss.inc(sc.act(rt_sb[:], ps_s2[0:8, 0:512], AF.Sigmoid,
                                    bias=br_sb[:, 0:1]))
                    M[("rte", j)] = s_ss.n

                s_front(0)
                for j in range(NT):
                    if j + 1 < NT:
                        s_front(j + 1)
                    s_back(j)

            @block.vector
            def _(v):
                ve = Eng(v)
                v.wait_ge(s_ss.h, mg("m1ssend"))
                ve.mset(srow_sb[0:1, 32:48], 0.0)
                cur = 32
                for k in range(NCH):
                    nxt = 48 if cur == 32 else 32
                    ve.stt(srow_sb[0:1, nxt:nxt + S], srow_sb[0:1, cur:cur + S],
                           aall_sb[0:1, NCH + k:NCH + k + 1],
                           ball_sb[0:1, NCH * S + k * S:NCH * S + (k + 1) * S],
                           ALU.mult, ALU.add)
                    cur = nxt
                ve.ts(srow_sb[0:1, 0:S], srow_sb[0:1, cur:cur + S],
                      flag_sb[0:1, 0:1], None, ALU.mult)
                s_vv.inc(ve.cp(srowb_sb[0:1, 0:S], srow_sb[0:1, 0:S]))
                M[("schain", 0)] = s_vv.n
                for k in range(NCH):
                    v.wait_ge(s_mm.h, mg("P1", k))
                    ve.tt(tsc2_sb[:], call_sb[:, k * S:(k + 1) * S],
                          ps_s2[:, 0:S], ALU.mult)
                    ve.rsum(p1s_sb[:, 0:1], tsc2_sb[:])
                    s_vv.inc(ve.tt(p1s_sb[:, 1:2], p1s_sb[:, 0:1],
                                   ysi_sb[:, k:k + 1], ALU.add))
                    M[("ysc", k)] = s_vv.n
                    nxt_k = ((k + 1) % 2) * S
                    cur_k = (k % 2) * S
                    ve.stt(srow_sb[0:1, nxt_k:nxt_k + S],
                           srow_sb[0:1, cur_k:cur_k + S],
                           aall_sb[0:1, k:k + 1],
                           ball_sb[0:1, k * S:(k + 1) * S],
                           ALU.mult, ALU.add)
                    s_vv.inc(ve.cp(srowb_sb[0:1, nxt_k:nxt_k + S],
                                   srow_sb[0:1, nxt_k:nxt_k + S]))
                    M[("schain", k + 1)] = s_vv.n
                    v.wait_ge(s_mm.h, mg("ystr", k))
                    s_vv.inc(ve.cp(ysr_sb[0:1, k * P:(k + 1) * P],
                                   ps_s2[0:1, 256:384]))
                    M[("ysrcp", k)] = s_vv.n
                def v_front(j):
                    i = 2 * NT + j
                    sem = s_lxa if i % 2 == 0 else s_lxb
                    v.wait_ge(sem.h, 16 * (i // 2 + 1))
                    xo = (i % 2) * KD * W
                    if j == 1:
                        v.wait_ge(s_mm.h, mg("2bbc", 0))
                    elif j >= 2:
                        v.wait_ge(s_mm.h, mg("bbc2", j - 2))
                    v.wait_ge(s_mm.h, mg("2sums", j))
                    s_vv.inc(ve.ts(rows_sb[0:1, 0:512], ps_st2[0:1, 0:512],
                                   1.0 / D, None, ALU.mult))
                    M[("2mean", j)] = s_vv.n
                    v.wait_ge(s_mm.h, mg("2sq", j, KD - 1))
                    ve.ts(rows_sb[0:1, 514:1026], ps_st2[0:1, 0:512],
                          1.0 / D, None, ALU.mult)
                    ve.tt(rows_sb[0:1, 1028:1540], rows_sb[0:1, 0:512],
                          rows_sb[0:1, 0:512], ALU.mult)
                    ve.tt(rows_sb[0:1, 514:1026], rows_sb[0:1, 514:1026],
                          rows_sb[0:1, 1028:1540], ALU.subtract)
                    s_vv.inc(ve.ts(rows_sb[0:1, 514:1026],
                                   rows_sb[0:1, 514:1026], 1e-5, None, ALU.add))
                    M[("2var", j)] = s_vv.n
                    v.wait_ge(s_ss.h, mg("2sqrt", j))
                    ve.recip(rows_sb[0:1, 1028:1540], rows_sb[0:1, 1542:2054])
                    ve.tt(rows_sb[0:1, 1542:2054], rows_sb[0:1, 0:512],
                          rows_sb[0:1, 1028:1540], ALU.mult)
                    s_vv.inc(ve.ts(rows_sb[0:1, 1542:2054],
                                   rows_sb[0:1, 1542:2054], -1.0, None, ALU.mult))
                    M[("2arow", j)] = s_vv.n
                    v.wait_ge(s_mm.h, mg("2abc", j))
                    s_vv.inc(ve.cp(bcg_sb[:, 0:TS], ps_b2[:, 512:1024]))
                    M[("2acp", j)] = s_vv.n
                    v.wait_ge(s_mm.h, mg("2bbc", j))
                    if j >= 1:
                        v.wait_ge(s_mm.h, mg("gate", j - 1, KE - 1))
                    ve.cp(bcg_sb[:, TS:2 * TS], ps_b2[:, 512:1024])
                    for k in range(KD):
                        ve.tt(mcs1_sb[:], x_buf[:, xo + k * W + 1:xo + k * W + 513],
                              bcg_sb[:, 0:TS], ALU.mult)
                        ve.tt(mcs2_sb[:], mcs1_sb[:], bcg_sb[:, TS:2 * TS],
                              ALU.add)
                        ve.ts(hg_sb[:, k * TS:(k + 1) * TS], mcs2_sb[:],
                              modv_sb[:, 8 + k:8 + k + 1],
                              modv_sb[:, k:k + 1], ALU.mult, ALU.add)
                    s_vv.inc(ve.dinc())
                    M[("2happ", j)] = s_vv.n

                def v_back_a(j):
                    i = 2 * NT + j
                    xo = (i % 2) * KD * W
                    v.wait_ge(s_mm.h, mg("ysb", j))
                    if j >= 1:
                        v.wait_ge(s_mm.h, mg("sums2", j - 1))
                        v.wait_ge(s_ss.h, mg("x2sqk", j - 1, KD - 1))
                        v.wait_ge(s_stx.h, mg("m2st_x", j - 1))
                    for d in range(KD):
                        v.wait_ge(s_ss.h, mg("ape", j, d))
                        ve.stt(mcs1_sb[:], ap_sb[:, d * TS:(d + 1) * TS],
                               modv_sb[:, 16 + d:16 + d + 1], ps_b2[:, 0:512],
                               ALU.mult, ALU.mult)
                        ve.ts(mcs2_sb[:], hg_sb[:, d * TS:(d + 1) * TS],
                              modv_sb[:, 16 + d:16 + d + 1],
                              modv_sb[:, 48 + d:48 + d + 1], ALU.mult, ALU.add)
                        ve.tt(mcs2_sb[:], mcs2_sb[:],
                              x_buf[:, xo + d * W + 1:xo + d * W + 513], ALU.add)
                        ve.tt(x2_sb[:, d * TS:(d + 1) * TS], mcs1_sb[:],
                              mcs2_sb[:], ALU.add)
                        s_vv.inc(ve.cp(x2b_sb[:, d * TS:(d + 1) * TS],
                                       x2_sb[:, d * TS:(d + 1) * TS]))
                        M[("x2d", j, d)] = s_vv.n
                    M[("x2", j)] = s_vv.n
                    M[("x2b", j)] = s_vv.n
                    s_fx.inc(ve.dinc())

                def v_back_b(j):
                    v.wait_ge(s_mm.h, mg("sums2", j))
                    s_vv.inc(ve.ts(rows_sb[0:1, 0:512], ps_st2[0:1, 0:512],
                                   1.0 / D, None, ALU.mult))
                    M[("mean2", j)] = s_vv.n
                    v.wait_ge(s_mm.h, mg("sq2", j, KD - 1))
                    if j + 1 < NT:
                        v.wait_ge(s_mm.h, mg("2bbc", j + 1))
                    ve.ts(rows_sb[0:1, 514:1026], ps_st2[0:1, 0:512],
                          1.0 / D, None, ALU.mult)
                    ve.tt(rows_sb[0:1, 1028:1540], rows_sb[0:1, 0:512],
                          rows_sb[0:1, 0:512], ALU.mult)
                    ve.tt(rows_sb[0:1, 514:1026], rows_sb[0:1, 514:1026],
                          rows_sb[0:1, 1028:1540], ALU.subtract)
                    s_vv.inc(ve.ts(rows_sb[0:1, 514:1026],
                                   rows_sb[0:1, 514:1026], 1e-5, None, ALU.add))
                    M[("var2", j)] = s_vv.n
                    v.wait_ge(s_ss.h, mg("sqrt2", j))
                    ve.recip(rows_sb[0:1, 1028:1540], rows_sb[0:1, 1542:2054])
                    ve.tt(rows_sb[0:1, 1542:2054], rows_sb[0:1, 0:512],
                          rows_sb[0:1, 1028:1540], ALU.mult)
                    s_vv.inc(ve.ts(rows_sb[0:1, 1542:2054],
                                   rows_sb[0:1, 1542:2054], -1.0, None, ALU.mult))
                    M[("arow2", j)] = s_vv.n
                    v.wait_ge(s_mm.h, mg("abc2", j))
                    s_vv.inc(ve.cp(bc2_sb[:, 0:TS], ps_b2[:, 512:1024]))
                    M[("acp2", j)] = s_vv.n
                    v.wait_ge(s_mm.h, mg("bbc2", j))
                    if j >= 1:
                        v.wait_ge(s_sth.h, mg("m2st_h", j - 1))
                    ve.cp(bc2_sb[:, TS:2 * TS], ps_b2[:, 512:1024])
                    for k in range(KD):
                        ve.tt(mcs1_sb[:], x2_sb[:, k * TS:(k + 1) * TS],
                              bc2_sb[:, 0:TS], ALU.mult)
                        ve.tt(mcs2_sb[:], mcs1_sb[:], bc2_sb[:, TS:2 * TS],
                              ALU.add)
                        ve.ts(h2t_sb[:, k * TS:(k + 1) * TS], mcs2_sb[:],
                              modv_sb[:, 32 + k:32 + k + 1],
                              modv_sb[:, 24 + k:24 + k + 1], ALU.mult, ALU.add)
                    s_vv.inc(ve.dinc())
                    M[("h2", j)] = s_vv.n
                    if j >= 1:
                        v.wait_ge(s_gf.h, mg("m2gf", j - 1))
                    for c in range(CPT):
                        v.wait_ge(s_mm.h, mg("rtt", j, c))
                        ve.cp(rtc_sb[:], ps_tb[:])
                        ve.maxv(mx_sb[:], rtc_sb[:])
                        ve.maxidx(mi8_sb[:], mx_sb[:], rtc_sb[:])
                        ve.cp(em2_sb[:, 0:1], mi8_sb[:, 0:1])
                        s_vv.inc(ve.cp(em2_sb[:, 1:2], mx_sb[:, 0:1]))
                        M[("em2", j, c)] = s_vv.n
                        v.wait_ge(s_mm.h, mg("emt", j, c))
                        s_vv.inc(ve.cp(emr_sb[0:2, c * P:(c + 1) * P],
                                       ps_s2[0:2, 256:384]))
                        M[("emcp", j, c)] = s_vv.n
                    s_vv.inc(ve.dinc())
                    M[("emrdy", j)] = s_vv.n

                v_front(0)
                for j in range(NT):
                    v_back_a(j)
                    if j + 1 < NT:
                        v_front(j + 1)
                    v_back_b(j)

            @block.sync
            def _(sy):
                for j in range(NT):
                    sy.wait_ge(s_vv.h, mg("x2", j))
                    s_stx.inc(sy.dma_start(
                        r3(out_d)[:, :, j * TS:(j + 1) * TS],
                        x2_sb[:].rearrange("p (k n) -> p k n", k=KD)), 16)
                    M[("m2st_x", j)] = s_stx.n
                    sy.wait_ge(s_vv.h, mg("h2", j))
                    s_sth.inc(sy.dma_start(
                        r3(h2_d)[:, :, j * TS:(j + 1) * TS],
                        h2t_sb[:].rearrange("p (k n) -> p k n", k=KD)), 16)
                    M[("m2st_h", j)] = s_sth.n
                M[("m2stx",)] = s_stx.n
                M[("m2sth",)] = s_sth.n

        x_es.close()

        if upto == "m2":
            if dbg:
                _dbg_emit(block, dbg_rows, dbg_cols, expcl_sb, ball_sb, ysi_sb,
                          ysr_sb, None, s_vv, s_ss, s_gf)
            _finish(block, s_st, s_stx, s_sth)
            return nc, dram

        scan_es.close()

        with contextlib.ExitStack() as es:
            L = es.enter_context
            acc_sb = L(nc.sbuf_tensor("acc_sb", [P, KD * SHW], BF16))
            w1_sb = L(nc.sbuf_tensor("w1_sb", [P, KD * FF], BF16))
            w2_sb = L(nc.sbuf_tensor("w2_sb", [P, KF * D], BF16))
            t1_sb = L(nc.sbuf_tensor("t1_sb", [P, KF * TS], BF16))
            h2b_sb = L(nc.sbuf_tensor("h2b_sb", [P, KD * TS], BF16))
            emap_sb = L(nc.sbuf_tensor("emap_sb", [P, SHW], BF16))
            msk_sb = L(nc.sbuf_tensor("msk_sb", [P, TS], U32))
            otl_sb = L(nc.sbuf_tensor("otl_sb", [P, 2 * TS], F32))
            emrow_sb = L(nc.sbuf_tensor("emrow_sb", [1, 2 * TS], BF16))
            mprow_sb = L(nc.sbuf_tensor("mprow_sb", [1, TS], BF16))
            ps_all = L(nc.psum_tensor("ps_all", [P, 4096], F32))
            x2r_sb = h2b_sb

            eorders = [list(range(NEXP)) if sh % 2 == 0
                       else list(range(NEXP - 1, -1, -1)) for sh in range(NSH)]

            @block.gpsimd
            def _(g):
                g.wait_ge(s_stx.h, mg("m2stx"))
                g.wait_ge(s_sth.h, mg("m2sth"))
                g.wait_ge(s_gf.h, mg("m2gf"))
                prev_e, tile_n, inst_i = None, 0, 0
                for sh in range(NSH):
                    if sh >= 1:
                        g.wait_ge(s_fer.h, sh)
                    s_ler.inc(g.dma_start(
                        emrow_sb[0:1, 0:SHW],
                        emr_d[0:1, sh * SHW:(sh + 1) * SHW]), 16)
                    for ei, e in enumerate(eorders[sh]):
                        if e != prev_e:
                            if inst_i > 0:
                                g.wait_ge(s_fw1.h, inst_i)
                            s_lw1.inc(g.dma_start(
                                w1_sb[:].rearrange("p (k n) -> p k n", k=KD)
                                [:, :, 0:FF // 2],
                                r3(w1[e])[:, :, 0:FF // 2]), 16)
                            if inst_i > 0:
                                g.wait_ge(s_fw1b.h, inst_i)
                            s_lw1b.inc(g.dma_start(
                                w1_sb[:].rearrange("p (k n) -> p k n", k=KD)
                                [:, :, FF // 2:FF],
                                r3(w1[e])[:, :, FF // 2:FF]), 16)
                            if inst_i > 0:
                                g.wait_ge(s_fw2.h, inst_i)
                            s_lw2.inc(g.dma_start(
                                w2_sb[:].rearrange("p (k n) -> p k n", k=KF)
                                [:, :, 0:D // 2],
                                r3(w2[e])[:, :, 0:D // 2]), 16)
                            if inst_i > 0:
                                g.wait_ge(s_fw2b.h, inst_i)
                            s_lw2b.inc(g.dma_start(
                                w2_sb[:].rearrange("p (k n) -> p k n", k=KF)
                                [:, :, D // 2:D],
                                r3(w2[e])[:, :, D // 2:D]), 16)
                            prev_e = e
                        for q in range(QSH):
                            tile_n += 1
                        inst_i += 1
                    for q in range(QSH):
                        tq = sh * QSH + q
                        if tq >= 1:
                            g.wait_ge(s_fo.h, tq)
                            g.wait_ge(s_fmr.h, tq)
                        g.wait_ge(s_fh2.h, tile_n)
                        s_lmr.inc(g.dma_start(
                            mprow_sb[:],
                            emr_d[1:2, sh * SHW + q * TS:sh * SHW + (q + 1) * TS]),
                            16)
                        s_loa.inc(g.dma_start(
                            x2r_sb[:].rearrange("p (k n) -> p k n", k=KD),
                            r3(out_d)[:, :, sh * SHW + q * TS:
                                      sh * SHW + (q + 1) * TS]), 16)

            @block.tensor
            def _(t):
                tile_n = 0
                t.wait_ge(s_gf.h, mg("m2gf"))
                for sh in range(NSH):
                    if tile_n >= 1:
                        t.wait_ge(s_vv.h, mg("predq", 4 * tile_n - 1))
                    t.wait_ge(s_ler.h, 16 * (sh + 1))
                    for q in range(QSH):
                        s_mm.inc(t.matmul(
                            ps_all[:, q * 512:(q + 1) * 512], onesb_sb[0:1, :],
                            emrow_sb[0:1, q * TS:(q + 1) * TS],
                            start=True, stop=True))
                        M[("embc", sh, q)] = s_mm.n
                    s_fer.inc(t.drain())
                    for ei, e in enumerate(eorders[sh]):
                        nw = _w1count(eorders, sh, ei)
                        t.wait_ge(s_lw1.h, 16 * nw)
                        for q in range(QSH):
                            a = (2 * tile_n) % 8
                            t.wait_ge(s_lh2.h, 16 * (tile_n + 1))
                            if ei == 0:
                                t.wait_ge(s_vv.h, mg("emcp3", sh))
                            if tile_n >= 1:
                                t.wait_ge(s_vv.h, mg("predq", 4 * (tile_n - 1) + 1))
                            for m in range(KF):
                                zb = a + (m % 2)
                                if m == KF // 2:
                                    t.wait_ge(s_lw1b.h, 16 * nw)
                                if m >= 2:
                                    t.wait_ge(s_ss.h, mg("gelu", tile_n, m - 2))
                                for k in range(KD):
                                    inst = t.matmul(
                                        ps_all[:, zb * 512:(zb + 1) * 512],
                                        w1_sb[:, k * FF + m * P:k * FF + (m + 1) * P],
                                        h2b_sb[:, k * TS:(k + 1) * TS],
                                        start=(k == 0), stop=(k == KD - 1))
                                s_mm.inc(inst)
                                M[("z", tile_n, m)] = s_mm.n
                                if q == QSH - 1 and m == KF // 2 - 1:
                                    s_fw1.inc(t.drain())
                            s_fh2.inc(t.drain())
                            if q == QSH - 1:
                                s_fw1b.inc(t.drain())
                            t.wait_ge(s_ss.h, mg("gelu", tile_n, KF - 1))
                            t.wait_ge(s_lw2.h, 16 * nw)
                            for q4 in range(4):
                                if q4 == 2:
                                    t.wait_ge(s_lw2b.h, 16 * nw)
                                if q == QSH - 1 and q4 == 2:
                                    s_fw2.inc(t.drain())
                                if tile_n >= 1 and q4 <= 2:
                                    t.wait_ge(s_vv.h,
                                              mg("predq", 4 * (tile_n - 1) + q4 + 1))
                                for dd in range(2):
                                    d = 2 * q4 + dd
                                    yb = (a + 2 + 2 * q4 + dd) % 8
                                    for k in range(KF):
                                        inst = t.matmul(
                                            ps_all[:, yb * 512:(yb + 1) * 512],
                                            w2_sb[:, k * D + d * P:k * D + (d + 1) * P],
                                            t1_sb[:, k * TS:(k + 1) * TS],
                                            start=(k == 0), stop=(k == KF - 1))
                                s_mm.inc(inst)
                                M[("y", tile_n, q4)] = s_mm.n
                            if q == QSH - 1:
                                s_fw2b.inc(t.drain())
                            tile_n += 1
                    t.wait_ge(s_vv.h, mg("predq", 4 * tile_n - 1))
                    for q in range(QSH):
                        tq = sh * QSH + q
                        t.wait_ge(s_lmr.h, 16 * (tq + 1))
                        s_mm.inc(t.matmul(
                            ps_all[:, 0:512], onesb_sb[0:1, :],
                            mprow_sb[0:1, 0:TS],
                            start=True, stop=True))
                        M[("mpb", sh, q)] = s_mm.n
                        s_fmr.inc(t.drain())
                        t.wait_ge(s_vv.h, mg("fin", sh, q))

            @block.scalar
            def _(s):
                sc = Eng(s)
                tile_n = 0
                for sh in range(NSH):
                    for ei in range(NEXP):
                        for q in range(QSH):
                            a = (2 * tile_n) % 8
                            for m in range(KF):
                                zb = a + (m % 2)
                                s.wait_ge(s_mm.h, mg("z", tile_n, m))
                                s_ss.inc(sc.act(
                                    t1_sb[:, m * TS:(m + 1) * TS],
                                    ps_all[:, zb * 512:(zb + 1) * 512], AF.Gelu))
                                M[("gelu", tile_n, m)] = s_ss.n
                            tile_n += 1

            @block.vector
            def _(v):
                ve = Eng(v)
                tile_n = 0
                for sh in range(NSH):
                    for q in range(QSH):
                        v.wait_ge(s_mm.h, mg("embc", sh, q))
                        ve.cp(emap_sb[:, q * TS:(q + 1) * TS],
                              ps_all[:, q * 512:(q + 1) * 512])
                    s_vv.inc(ve.dinc())
                    M[("emcp3", sh)] = s_vv.n
                    for ei, e in enumerate(eorders[sh]):
                        for q in range(QSH):
                            a = (2 * tile_n) % 8
                            ve.ts(msk_sb[:], emap_sb[:, q * TS:(q + 1) * TS],
                                  float(e), None, ALU.is_equal)
                            for q4 in range(4):
                                v.wait_ge(s_mm.h, mg("y", tile_n, q4))
                                for dd in range(2):
                                    d = 2 * q4 + dd
                                    yb = (a + 2 + 2 * q4 + dd) % 8
                                    ve.pred(acc_sb[:, d * SHW + q * TS:
                                                   d * SHW + (q + 1) * TS],
                                            msk_sb[:],
                                            ps_all[:, yb * 512:(yb + 1) * 512])
                                s_vv.inc(ve.dinc())
                                M[("predq", 4 * tile_n + q4)] = s_vv.n
                            tile_n += 1
                    for q in range(QSH):
                        tq = sh * QSH + q
                        v.wait_ge(s_loa.h, 16 * (tq + 1))
                        v.wait_ge(s_mm.h, mg("mpb", sh, q))
                        for d in range(KD):
                            sct = s_sta if d % 2 == 0 else s_stb
                            if d >= 2:
                                v.wait_ge(sct.h, mg("fst", tq, d - 2))
                            elif tq >= 1:
                                v.wait_ge(sct.h, mg("fst", tq - 1, KD - 2 + d))
                            sl = otl_sb[:, (d % 2) * TS:(d % 2) * TS + TS]
                            ve.stt(sl,
                                   acc_sb[:, d * SHW + q * TS:
                                          d * SHW + (q + 1) * TS],
                                   modv_sb[:, 40 + d:40 + d + 1],
                                   ps_all[:, 0:512], ALU.mult, ALU.mult)
                            ve.tt(sl, sl, x2r_sb[:, d * TS:(d + 1) * TS],
                                  ALU.add)
                            s_vv.inc(ve.dinc())
                            M[("fin", sh, q, d)] = s_vv.n
                        s_fo.inc(ve.dinc())
                        M[("fin", sh, q)] = s_vv.n

            @block.sync
            def _(sy):
                sy.wait_ge(s_stx.h, mg("m2stx"))
                sy.wait_ge(s_sth.h, mg("m2sth"))
                sy.wait_ge(s_gf.h, mg("m2gf"))
                tile_n = 0
                for sh in range(NSH):
                    for ei in range(NEXP):
                        for q in range(QSH):
                            if tile_n >= 1:
                                sy.wait_ge(s_fh2.h, tile_n)
                            if ei == 0 and q == 0 and sh >= 1:
                                sy.wait_ge(s_fo.h, sh * QSH)
                            s_lh2.inc(sy.dma_start(
                                h2b_sb[:].rearrange("p (k n) -> p k n", k=KD),
                                r3(h2_d)[:, :, sh * SHW + q * TS:
                                         sh * SHW + (q + 1) * TS]), 16)
                            tile_n += 1
                    for q in range(QSH):
                        tq = sh * QSH + q
                        for d in range(KD):
                            sct = s_sta if d % 2 == 0 else s_stb
                            sy.wait_ge(s_vv.h, mg("fin", sh, q, d))
                            sct.inc(sy.dma_start(
                                r3(out_d)[:, d:d + 1,
                                          sh * SHW + q * TS:sh * SHW + (q + 1) * TS],
                                otl_sb[:, (d % 2) * TS:(d % 2) * TS + TS]
                                .rearrange("p (k n) -> p k n", k=1)), 16)
                            M[("fst", tq, d)] = sct.n



        _finish(block, s_st, s_stx, s_sth, s_sta, s_stb)
    return nc, dram


def _dbg_emit(block, dbg_rows, dbg_cols, expcl_sb, ball_sb, ysi_sb,
              ysr_sb, mpr_unused, s_vv, s_ss, s_st, zero_rows=False):
    @block.gpsimd
    def _(sy):
        sy.wait_ge(s_vv.h, s_vv.n)
        sy.wait_ge(s_ss.h, s_ss.n)
        sy.wait_ge(s_st.h, s_st.n)
        if zero_rows:
            sy.memset(ysr_sb[:], 0.0)
            sy.drain()
        o = 0
        for src in (expcl_sb, ball_sb, ysr_sb, ysr_sb):
            n = src.shape[1]
            s_st.inc(sy.dma_start(dbg_rows[0:1, o:o + n], src[:]), 16)
            o += n
        s_st.inc(sy.dma_start(dbg_cols[:, 0:ysi_sb.shape[1]], ysi_sb[:]), 16)
        sy.wait_ge(s_st.h, s_st.n)


def _finish(block, s_st, *extra):
    @block.sync
    def _(sy):
        sy.wait_ge(s_st.h, s_st.n)
        for s2 in extra:
            sy.wait_ge(s2.h, s2.n)


def build_nc(upto="final", dbg=False):
    marks = {}
    build_program(marks, upto=upto, dbg=dbg)
    return build_program(marks, upto=upto, dbg=dbg)



def stage_inputs(inputs, n_cores=8):
    f = np.float32
    x = np.asarray(inputs["L0_attn_x"], f)
    B, T, Dm = x.shape
    in_w = np.asarray(inputs["L0_attn_in_w"], f)
    in_b = np.asarray(inputs["L0_attn_in_b"], f)
    conv_w = np.asarray(inputs["L0_attn_conv_w"], f)
    conv_b = np.asarray(inputs["L0_attn_conv_b"], f)
    dt_w = np.asarray(inputs["L0_attn_dt_w"], f)
    dt_b = np.asarray(inputs["L0_attn_dt_b"], f)
    B_w = np.asarray(inputs["L0_attn_B_w"], f)
    B_b = np.asarray(inputs["L0_attn_B_b"], f)
    C_w = np.asarray(inputs["L0_attn_C_w"], f)
    C_b = np.asarray(inputs["L0_attn_C_b"], f)
    D_w = np.asarray(inputs["L0_attn_D_w"], f)
    D_b = np.asarray(inputs["L0_attn_D_b"], f)
    out_w = np.asarray(inputs["L0_attn_out_w"], f)
    out_b = np.asarray(inputs["L0_attn_out_b"], f)
    Wr = np.asarray(inputs["L0_ffn_Wr"], f)
    br = np.asarray(inputs["L0_ffn_br"], f)
    W1 = np.asarray(inputs["L0_ffn_W1"], f)
    W2 = np.asarray(inputs["L0_ffn_W2"], f)
    cmat = np.asarray(inputs["L0_c"], f)
    wmod = np.asarray(inputs["L0_wmod"], f)

    def chunkmaj(vec, nch):
        return np.ascontiguousarray(vec.reshape(nch, P).T)

    shared = dict(
        inwm=np.ascontiguousarray(in_w[:E].T),
        inwg=np.ascontiguousarray(in_w[E:].T),
        dtw=np.ascontiguousarray(dt_w.T),
        bcdw=np.ascontiguousarray(np.concatenate([B_w, C_w, D_w], 0).T),
        outw=np.ascontiguousarray(out_w.T),
        wr=np.ascontiguousarray(Wr),
        w1=np.ascontiguousarray(W1),
        w2=np.ascontiguousarray(W2),
        wmod=np.ascontiguousarray(wmod),
        inbm=chunkmaj(in_b[:E], KE),
        inbg=chunkmaj(in_b[E:], KE),
        convw=np.ascontiguousarray(np.concatenate(
            [chunkmaj(conv_w[:, 0, k], KE) for k in range(3)], 1)),
        convb=chunkmaj(conv_b, KE),
        dtb=chunkmaj(dt_b, KE),
        bcdb=np.ascontiguousarray(np.stack([B_b, C_b, D_b], 1)),
        outb=chunkmaj(out_b, KD),
        brv=np.ascontiguousarray(br.reshape(8, 1)),
        utc=np.ascontiguousarray(np.triu(np.ones((P, P), f))),
        onesc=np.ones((P, P), f),
        maskc=np.ascontiguousarray(
            np.where(np.arange(P)[None, :] >= np.arange(P)[:, None],
                     0.0, NEG).astype(f)),
        identc=np.eye(P, dtype=f),
    )

    in_maps = []
    for c in range(n_cores):
        b, odd = c // 2, c % 2
        xb = x[b].T
        halves = []
        hmask = np.zeros((P, 4), f)
        for hi, h in enumerate((odd, 1 - odd)):
            t0 = h * TPH
            blk = np.zeros((Dm, TPH + 2), f)
            blk[:, 1:TPH + 1] = xb[:, t0:t0 + TPH]
            if t0 - 1 >= 0:
                blk[:, 0] = xb[:, t0 - 1]
                hmask[:, hi * 2] = 1.0
            if t0 + TPH < T:
                blk[:, TPH + 1] = xb[:, t0 + TPH]
                hmask[:, hi * 2 + 1] = 1.0
            halves.append(blk)
        m = dict(shared)
        m["x"] = np.ascontiguousarray(np.concatenate(halves, 1))
        m["cvec"] = np.ascontiguousarray(cmat[b].reshape(KD, P).T)
        m["flagv"] = np.array([[1.0 if odd else 0.0]], f)
        m["hmaskv"] = hmask
        in_maps.append(m)
    return in_maps


def kernel(**inputs):
    nc, _ = build_nc(upto="final", dbg=False)
    in_maps = stage_inputs(inputs)
    res = run_bass_kernel_spmd(nc, in_maps, core_ids=list(range(8)))
    x = np.asarray(inputs["L0_attn_x"])
    B, T, Dm = x.shape
    out = np.empty((B, T, Dm), np.float32)
    for c in range(8):
        b, odd = c // 2, c % 2
        t0 = odd * TPH
        out[b, t0:t0 + TPH, :] = np.asarray(res.results[c]["out"]).T
    return out


# version 12
# speedup vs baseline: 1.0214x; 1.0214x over previous
import contextlib
import os
import numpy as np
import ml_dtypes
import concourse.bass as bass
import concourse.mybir as mybir
from concourse.bass_utils import run_bass_kernel_spmd

F32 = mybir.dt.float32
BF16 = mybir.dt.bfloat16
U32 = mybir.dt.uint32
AF = mybir.ActivationFunctionType
ALU = mybir.AluOpType
AX = mybir.AxisListType

P = 128
D, E, S, FF, NEXP = 1024, 2048, 16, 4096, 8
KD, KE, KF = D // P, E // P, FF // P
TS = 512
NEG = -1e30
TPH = 2048
M1LVL = int(os.environ.get("M1LVL", "9"))
M1SKIP = set(os.environ.get("M1SKIP", "").split(","))


def set_tph(v):
    global TPH
    TPH = v


class Ctr:
    def __init__(self, h):
        self.h = h
        self.n = 0

    def inc(self, inst, k=1):
        inst.then_inc(self.h, k)
        self.n += k
        return self.n


class Eng:

    def __init__(self, e):
        self.e = e
        self.w = set()
        self.r = set()

    def _names(self, aps):
        out = set()
        for ap in aps:
            t = getattr(ap, "tensor", None)
            if t is not None:
                out.add(t.name)
        return out

    def _pre(self, outs, ins):
        on, inn = self._names(outs), self._names(ins)
        if (inn & self.w) or (on & self.w) or (on & self.r):
            self.e.drain()
            self.w.clear()
            self.r.clear()

    def _post(self, outs, ins):
        self.w |= self._names(outs)
        self.r |= self._names(ins)

    def barrier(self):
        self.e.drain()
        self.w.clear()
        self.r.clear()

    def dinc(self):
        inst = self.e.drain()
        self.w.clear()
        self.r.clear()
        return inst

    def tt(self, out, a, b, op):
        self._pre([out], [a, b])
        i = self.e.tensor_tensor(out, a, b, op)
        self._post([out], [a, b])
        return i

    def ts(self, out, in0, s1, s2, op0, op1=None):
        self._pre([out], [in0, s1, s2])
        if op1 is None:
            i = self.e.tensor_scalar(out, in0, s1, s2, op0)
        else:
            i = self.e.tensor_scalar(out, in0, s1, s2, op0, op1)
        self._post([out], [in0])
        return i

    def stt(self, out, in0, s, in1, op0, op1):
        self._pre([out], [in0, s, in1])
        i = self.e.scalar_tensor_tensor(out, in0, s, in1, op0, op1)
        self._post([out], [in0, in1])
        return i

    def cp(self, out, in_):
        self._pre([out], [in_])
        i = self.e.tensor_copy(out, in_)
        self._post([out], [in_])
        return i

    def mset(self, out, v):
        self._pre([out], [])
        i = self.e.memset(out, v)
        self._post([out], [])
        return i

    def rsum(self, out, in_):
        self._pre([out], [in_])
        i = self.e.reduce_sum(out, in_, axis=AX.X)
        self._post([out], [in_])
        return i

    def maxv(self, out, in_):
        self._pre([out], [in_])
        i = self.e.max(out=out, in_=in_)
        self._post([out], [in_])
        return i

    def maxidx(self, out, mx, vals):
        self._pre([out], [mx, vals])
        i = self.e.max_index(out, mx, vals)
        self._post([out], [mx, vals])
        return i

    def pred(self, out, mask, data):
        self._pre([out], [mask, data])
        i = self.e.copy_predicated(out, mask, data)
        self._post([out], [mask, data])
        return i

    def recip(self, out, in_):
        self._pre([out], [in_])
        i = self.e.reciprocal(out, in_)
        self._post([out], [in_])
        return i

    def act(self, out, in_, func, bias=0.0, scale=1.0):
        ins = [in_] + ([bias] if hasattr(bias, "tensor") else [])
        self._pre([out], ins)
        i = self.e.activation(out, in_, func, bias=bias, scale=scale)
        self._post([out], ins)
        return i

    def sign(self, out, in_):
        self._pre([out], [in_])
        i = self.e.sign(out, in_)
        self._post([out], [in_])
        return i


def _w1count(eorders, sh, ei):
    n, prev = 0, None
    for s2 in range(len(eorders)):
        for i, e in enumerate(eorders[s2]):
            if e != prev:
                n += 1
                prev = e
            if s2 == sh and i == ei:
                return n
    return n


def build_program(marks, upto="final", dbg=False):
    nc = bass.Bass("TRN2", target_bir_lowering=False)
    NT = TPH // TS
    NCH = TPH // P
    CPT = TS // P
    XW = TPH + 2
    W = TS + 2
    NSH = max(1, TPH // 1024)
    SHW = TPH // NSH
    QSH = SHW // TS
    W = TS + 2
    M = marks

    def mg(*key):
        return M.get(key, 0)

    dram = {}

    def din(name, shape):
        dram[name] = nc.dram_tensor(name, list(shape), F32,
                                    kind="ExternalInput").ap()
        return dram[name]

    x = din("x", (D, 2 * XW))
    inwm = din("inwm", (D, E))
    inwg = din("inwg", (D, E))
    dtw = din("dtw", (E, E))
    bcdw = din("bcdw", (E, 48))
    outw = din("outw", (E, D))
    wr = din("wr", (D, 8))
    w1 = din("w1", (NEXP, D, FF))
    w2 = din("w2", (NEXP, FF, D))
    wmod = din("wmod", (D, 6 * D))
    cvec = din("cvec", (P, KD))
    inbm = din("inbm", (P, KE))
    inbg = din("inbg", (P, KE))
    convw = din("convw", (P, 3 * KE))
    convb = din("convb", (P, KE))
    dtb = din("dtb", (P, KE))
    bcdb = din("bcdb", (S, 3))
    outb = din("outb", (P, KD))
    brv = din("brv", (8, 1))
    flagv = din("flagv", (1, 1))
    hmaskv = din("hmaskv", (P, 4))
    utc = din("utc", (P, P))
    onesc = din("onesc", (P, P))
    maskc = din("maskc", (P, P))
    identc = din("identc", (P, P))

    out_d = nc.dram_tensor("out", [D, TPH], F32, kind="ExternalOutput").ap()
    emr_d = nc.dram_tensor("emrd", [2, TPH], F32, kind="Internal").ap()
    h2_d = nc.dram_tensor("h2d", [D, TPH], BF16,
                          kind="ExternalOutput" if dbg else "Internal").ap()
    if dbg:
        dbg_rows = nc.dram_tensor("dbg_rows", [1, 8 * TPH], F32,
                                  kind="ExternalOutput").ap()
        dbg_cols = nc.dram_tensor("dbg_cols", [P, 4 * NCH], F32,
                                  kind="ExternalOutput").ap()

    def r3(ap, k=KD):
        return ap.rearrange("(k p) n -> p k n", p=P)

    with contextlib.ExitStack() as es_glob:
        G = es_glob.enter_context
        s_set = Ctr(G(nc.semaphore("s_set")))
        s_wmo = Ctr(G(nc.semaphore("s_wmo")))
        s_lxa = Ctr(G(nc.semaphore("s_lxa")))
        s_lxb = Ctr(G(nc.semaphore("s_lxb")))
        s_fx = Ctr(G(nc.semaphore("s_fx")))
        s_mm = Ctr(G(nc.semaphore("s_mm")))
        s_vv = Ctr(G(nc.semaphore("s_vv")))
        s_ss = Ctr(G(nc.semaphore("s_ss")))
        s_st = Ctr(G(nc.semaphore("s_st")))
        s_stx = Ctr(G(nc.semaphore("s_stx")))
        s_sth = Ctr(G(nc.semaphore("s_sth")))
        s_gf = Ctr(G(nc.semaphore("s_gf")))
        s_m2w = Ctr(G(nc.semaphore("s_m2w")))
        s_m1a = Ctr(G(nc.semaphore("s_m1a")))
        s_lh2 = Ctr(G(nc.semaphore("s_lh2")))
        s_fh2 = Ctr(G(nc.semaphore("s_fh2")))
        s_lw1 = Ctr(G(nc.semaphore("s_lw1")))
        s_fw1 = Ctr(G(nc.semaphore("s_fw1")))
        s_lw2 = Ctr(G(nc.semaphore("s_lw2")))
        s_fw2 = Ctr(G(nc.semaphore("s_fw2")))
        s_lw1b = Ctr(G(nc.semaphore("s_lw1b")))
        s_fw1b = Ctr(G(nc.semaphore("s_fw1b")))
        s_lw2b = Ctr(G(nc.semaphore("s_lw2b")))
        s_fw2b = Ctr(G(nc.semaphore("s_fw2b")))
        s_loa = Ctr(G(nc.semaphore("s_loa")))
        s_fo = Ctr(G(nc.semaphore("s_fo")))
        s_ler = Ctr(G(nc.semaphore("s_ler")))
        s_fer = Ctr(G(nc.semaphore("s_fer")))
        s_lmr = Ctr(G(nc.semaphore("s_lmr")))
        s_fmr = Ctr(G(nc.semaphore("s_fmr")))
        s_sta = Ctr(G(nc.semaphore("s_sta")))
        s_stb = Ctr(G(nc.semaphore("s_stb")))

        onesb_sb = G(nc.sbuf_tensor("onesb_sb", [P, P], BF16))
        bias_sb = G(nc.sbuf_tensor("bias_sb", [P, 8 * KE], F32))
        hm_sb = G(nc.sbuf_tensor("hm_sb", [P, 4], F32))
        bcdb_sb = G(nc.sbuf_tensor("bcdb_sb", [S, 3], F32))
        br_sb = G(nc.sbuf_tensor("br_sb", [8, 1], F32))
        flag_sb = G(nc.sbuf_tensor("flag_sb", [1, 1], F32))
        modv_sb = G(nc.sbuf_tensor("modv_sb", [P, 64], F32))
        scan_es = G(contextlib.ExitStack())
        S_ = scan_es.enter_context
        ut_sb = S_(nc.sbuf_tensor("ut_sb", [P, P], F32))
        ones_sb = S_(nc.sbuf_tensor("ones_sb", [P, P], F32))
        mk_sb = S_(nc.sbuf_tensor("mk_sb", [P, P], F32))
        idf_sb = S_(nc.sbuf_tensor("idf_sb", [P, P], F32))
        idb_sb = S_(nc.sbuf_tensor("idb_sb", [P, P], BF16))
        expcl_sb = S_(nc.sbuf_tensor("expcl_sb", [1, 2 * TPH], BF16))
        aall_sb = S_(nc.sbuf_tensor("aall_sb", [1, 2 * NCH], F32))
        ball_sb = S_(nc.sbuf_tensor("ball_sb", [1, 2 * NCH * S], F32))
        ysi_sb = S_(nc.sbuf_tensor("ysi_sb", [P, NCH], F32))
        call_sb = S_(nc.sbuf_tensor("call_sb", [P, NCH * S], BF16))
        ysr_sb = S_(nc.sbuf_tensor("ysr_sb", [1, TPH], BF16))
        srow_sb = S_(nc.sbuf_tensor("srow_sb", [1, 4 * S], F32))
        srowb_sb = S_(nc.sbuf_tensor("srowb_sb", [1, 2 * S], BF16))
        rows_sb = S_(nc.sbuf_tensor("rows_sb", [1, 2056], F32))

        block = G(nc.Block())
        NTt = TPH // TS
        x_es = es_glob.enter_context(contextlib.ExitStack())
        x_buf = x_es.enter_context(
            nc.sbuf_tensor("x_buf", [P, 2 * KD * (TS + 2)], BF16))

        with contextlib.ExitStack() as es:
            L = es.enter_context
            wmod_sb = L(nc.sbuf_tensor("wmod_sb", [P, KD * 3 * D], F32))
            silc_sb = L(nc.sbuf_tensor("silc_sb", [P, KD], F32))
            mod_ps = L(nc.psum_tensor("mod_ps", [P, 64], F32))

            @block.gpsimd
            def _(g):
                for src, dst in ((utc, ut_sb), (onesc, ones_sb), (maskc, mk_sb),
                                 (identc, idf_sb), (onesc, onesb_sb),
                                 (identc, idb_sb),
                                 (hmaskv, hm_sb), (bcdb, bcdb_sb),
                                 (brv, br_sb), (flagv, flag_sb)):
                    s_set.inc(g.dma_start(dst[:], src), 16)
                for off, src in ((0, inbm), (16, inbg), (80, convb), (96, dtb)):
                    s_set.inc(g.dma_start(bias_sb[:, off:off + KE], src), 16)
                s_set.inc(g.dma_start(bias_sb[:, 32:80], convw), 16)
                s_set.inc(g.dma_start(bias_sb[:, 112:120], outb), 16)
                s_set.inc(g.dma_start(bias_sb[:, 120:128], cvec), 16)
                for i0 in range(2):
                    p0, j0 = (1, i0) if i0 < NTt else (0, i0 - NTt)
                    base0 = p0 * XW + j0 * TS
                    sem0 = s_lxa if i0 % 2 == 0 else s_lxb
                    sem0.inc(g.dma_start(
                        x_buf[:, (i0 % 2) * KD * W:((i0 % 2) + 1) * KD * W]
                        .rearrange("p (k n) -> p k n", k=KD),
                        r3(x)[:, :, base0:base0 + W]), 16)
                for hf in range(2):
                    s_wmo.inc(g.dma_start(
                        wmod_sb[:].rearrange("p (k n) -> p k n", k=KD),
                        r3(wmod)[:, :, hf * 3 * D:(hf + 1) * 3 * D]), 16)
                    g.wait_ge(s_mm.h, mg("modh", hf))

            @block.scalar
            def _(s):
                sc = Eng(s)
                s.wait_ge(s_set.h, s_set.n)
                s_ss.inc(sc.act(silc_sb[:], bias_sb[:, 120:128], AF.Silu))
                M[("silc",)] = s_ss.n

            @block.tensor
            def _(t):
                t.wait_ge(s_set.h, s_set.n)
                t.wait_ge(s_ss.h, mg("silc"))
                for hf in range(2):
                    t.wait_ge(s_wmo.h, 16 * (hf + 1))
                    for j in range(24):
                        jj = hf * 24 + j
                        for k in range(KD):
                            inst = t.matmul(
                                mod_ps[:, jj:jj + 1],
                                wmod_sb[:, k * 3 * D + j * P:k * 3 * D + (j + 1) * P],
                                silc_sb[:, k:k + 1],
                                start=(hf == 0 and j == 0 and k == 0),
                                stop=(hf == 1 and j == 23 and k == KD - 1))
                    s_mm.inc(inst)
                    M[("modh", hf)] = s_mm.n

            @block.scalar
            def _(s):
                sc = Eng(s)
                s.wait_ge(s_mm.h, mg("modh", 1))
                sc.sign(modv_sb[:, 48:56], mod_ps[:, 16:24])
                s_ss.inc(sc.sign(modv_sb[:, 56:64], mod_ps[:, 40:48]))
                M[("gsign",)] = s_ss.n

            @block.vector
            def _(v):
                ve = Eng(v)
                v.wait_ge(s_mm.h, mg("modh", 1))
                ve.cp(modv_sb[:, 0:8], mod_ps[:, 0:8])
                ve.ts(modv_sb[:, 8:16], mod_ps[:, 8:16], 1.0, None, ALU.add)
                ve.cp(modv_sb[:, 24:32], mod_ps[:, 24:32])
                ve.ts(modv_sb[:, 32:40], mod_ps[:, 32:40], 1.0, None, ALU.add)
                v.wait_ge(s_ss.h, mg("gsign"))
                for src, dst, sg in ((16, 16, 48), (40, 40, 56)):
                    ve.ts(modv_sb[:, dst:dst + 8], mod_ps[:, src:src + 8],
                          -1.0, None, ALU.mult)
                    ve.tt(modv_sb[:, dst:dst + 8], modv_sb[:, dst:dst + 8],
                          mod_ps[:, src:src + 8], ALU.max)
                    ve.ts(modv_sb[:, dst:dst + 8], modv_sb[:, dst:dst + 8],
                          0.25, None, ALU.max)
                    ve.tt(modv_sb[:, dst:dst + 8], modv_sb[:, dst:dst + 8],
                          modv_sb[:, sg:sg + 8], ALU.mult)
                s_vv.inc(ve.tt(modv_sb[:, 48:56], modv_sb[:, 16:24],
                               bias_sb[:, 112:120], ALU.mult))
                M[("modv",)] = s_vv.n

        if upto == "setup":
            _finish(block, s_st)
            return nc, dram

        with contextlib.ExitStack() as es:
            L = es.enter_context
            inwm_sb = L(nc.sbuf_tensor("inwm_sb", [P, KD * E], BF16))
            dtw_sb = L(nc.sbuf_tensor("dtw_sb", [P, KE * E], BF16))
            bcdw_sb = L(nc.sbuf_tensor("bcdw_sb", [P, KE * 48], BF16))
            h_sb = L(nc.sbuf_tensor("h_sb", [P, KD * W], BF16))
            hpair_sb = L(nc.sbuf_tensor("hpair_sb", [P, KD * 2], BF16))
            xm_sb = L(nc.sbuf_tensor("xm_sb", [P, KE * W], BF16))

            xc_sb = L(nc.sbuf_tensor("xc_sb", [P, KE * TS], BF16))
            sp_sb = L(nc.sbuf_tensor("sp_sb", [P, 2 * W], BF16))
            xsq_sb = sp_sb
            expz_sb = L(nc.sbuf_tensor("expz_sb", [P, TS], BF16))
            bc_sb = L(nc.sbuf_tensor("bc_sb", [P, 2 * W], BF16))
            cs1_sb = L(nc.sbuf_tensor("cs1_sb", [P, W], F32))
            cs2_sb = L(nc.sbuf_tensor("cs2_sb", [P, W], F32))
            cs3_sb = L(nc.sbuf_tensor("cs3_sb", [P, TS], F32))
            cs4_sb = L(nc.sbuf_tensor("cs4_sb", [P, TS], F32))
            bt_sb = L(nc.sbuf_tensor("bt_sb", [S, TS], BF16))
            ct_sb = L(nc.sbuf_tensor("ct_sb", [S, TS], BF16))
            dt2_sb = L(nc.sbuf_tensor("dt2_sb", [S, TS], BF16))
            u_sb = L(nc.sbuf_tensor("u_sb", [S, TS], BF16))
            xscr_sb = L(nc.sbuf_tensor("xscr_sb", [P, P], F32))
            rm_sb = L(nc.sbuf_tensor("rm_sb", [P, P], F32))
            g_sb = L(nc.sbuf_tensor("g_sb", [P, P], BF16))
            ldc_sb = L(nc.sbuf_tensor("ldc_sb", [P, 1], F32))
            ncl_sb = L(nc.sbuf_tensor("ncl_sb", [P, 1], F32))
            utm_sb = L(nc.sbuf_tensor("utm_sb", [P, S], BF16))
            tsc_sb = L(nc.sbuf_tensor("tsc_sb", [P, S], F32))

            ps_inp = L(nc.psum_tensor("ps_inp", [P, 1024], F32))
            ps_bc = L(nc.psum_tensor("ps_bc", [P, 1024], F32))
            ps_sml = L(nc.psum_tensor("ps_sml", [P, 512], F32))
            ps_stat = L(nc.psum_tensor("ps_stat", [1, 512], F32))
            ps_halo = L(nc.psum_tensor("ps_halo", [P, 64], F32))
            ps_smlb = L(nc.psum_tensor("ps_smlb", [P, 32], BF16))

            tiles = [(1, j) for j in range(NT)] + [(0, j) for j in range(NT)]

            @block.gpsimd
            def _(g):
                g.wait_ge(s_set.h, s_set.n)
                g.wait_ge(s_mm.h, mg("modh", 1))
                s_m1a.inc(g.dma_start(
                    inwm_sb[:].rearrange("p (k n) -> p k n", k=KD), r3(inwm)), 16)
                M[("m1wa",)] = s_m1a.n
                s_set.inc(g.dma_start(
                    dtw_sb[:].rearrange("p (k n) -> p k n", k=KE), r3(dtw)), 16)
                s_set.inc(g.dma_start(
                    bcdw_sb[:].rearrange("p (k n) -> p k n", k=KE), r3(bcdw)), 16)
                M[("m1w",)] = s_set.n
                for i, (part, j) in enumerate(tiles):
                    if i < 2:
                        continue
                    base = part * XW + j * TS
                    g.wait_ge(s_fx.h, i - 1)
                    sem = s_lxa if i % 2 == 0 else s_lxb
                    sem.inc(g.dma_start(
                        x_buf[:, (i % 2) * KD * W:((i % 2) + 1) * KD * W]
                        .rearrange("p (k n) -> p k n", k=KD),
                        r3(x)[:, :, base:base + W]), 16)

            @block.tensor
            def _(t):
                for i, (part, j) in enumerate(tiles):
                    own = (part == 0)
                    xo = (i % 2) * KD * W
                    sem = s_lxa if i % 2 == 0 else s_lxb
                    t.wait_ge(sem.h, 16 * (i // 2 + 1))
                    if i >= 1:
                        t.wait_ge(s_ss.h, mg("bcde", i - 1, 0))
                        t.wait_ge(s_ss.h, mg("bcde", i - 1, 1))
                        t.wait_ge(s_ss.h, mg("bcde", i - 1, 2))
                        t.wait_ge(s_vv.h, mg("stitch", i - 1))
                    for k in range(KD):
                        t.matmul(ps_stat[0:1, 0:512], onesb_sb[:, 0:1],
                                 x_buf[:, xo + k * W:xo + k * W + 512],
                                 start=(k == 0), stop=(k == KD - 1))
                    for k in range(KD):
                        inst = t.matmul(ps_halo[0:1, 32:34], onesb_sb[:, 0:1],
                                        x_buf[:, xo + k * W + 512:xo + k * W + 514],
                                        start=(k == 0), stop=(k == KD - 1))
                    s_mm.inc(inst)
                    M[("sums", i)] = s_mm.n
                    t.wait_ge(s_vv.h, mg("meancp", i))
                    for k in range(KD):
                        t.wait_ge(s_ss.h, mg("xsq", i, k))
                        t.matmul(ps_stat[0:1, 0:512], onesb_sb[:, 0:1],
                                 xsq_sb[:, (k % 2) * W:(k % 2) * W + 512],
                                 start=(k == 0), stop=(k == KD - 1))
                        inst = t.matmul(ps_halo[0:1, 32:34], onesb_sb[:, 0:1],
                                        xsq_sb[:, (k % 2) * W + 512:(k % 2) * W + 514],
                                        start=(k == 0), stop=(k == KD - 1))
                        s_mm.inc(inst)
                        M[("sq", i, k)] = s_mm.n
                    t.wait_ge(s_vv.h, mg("arow", i))
                    t.matmul(ps_bc[:, 0:512], ones_sb[0:1, :],
                             rows_sb[0:1, 1028:1540], start=True, stop=True)
                    s_mm.inc(t.matmul(ps_bc[:, 512:514], ones_sb[0:1, :],
                                      rows_sb[0:1, 1540:1542],
                                      start=True, stop=True))
                    M[("abc", i)] = s_mm.n
                    t.wait_ge(s_vv.h, mg("acp", i))
                    t.matmul(ps_bc[:, 0:512], ones_sb[0:1, :],
                             rows_sb[0:1, 1542:2054], start=True, stop=True)
                    s_mm.inc(t.matmul(ps_bc[:, 512:514], ones_sb[0:1, :],
                                      rows_sb[0:1, 2054:2056],
                                      start=True, stop=True))
                    M[("bbc", i)] = s_mm.n
                    if M1LVL < 2:
                        continue
                    if i == 0:
                        t.wait_ge(s_m1a.h, mg("m1wa"))
                    t.wait_ge(s_vv.h, mg("happ", i))
                    hsrc = h_sb
                    ho = 0
                    for m in range(KE):
                        if m >= 2:
                            t.wait_ge(s_ss.h, mg("xme", i, m - 2))
                        for k in range(KD):
                            inst = t.matmul(
                                ps_inp[:, (m % 2) * 512:(m % 2) * 512 + 512],
                                inwm_sb[:, k * E + m * P:k * E + (m + 1) * P],
                                hsrc[:, ho + k * W + 1:ho + k * W + 513],
                                start=(k == 0), stop=(k == KD - 1))
                        s_mm.inc(inst)
                        M[("inp", i, m)] = s_mm.n
                        for k in range(KD):
                            inst = t.matmul(
                                ps_halo[:, m * 2:m * 2 + 2],
                                inwm_sb[:, k * E + m * P:k * E + (m + 1) * P],
                                hpair_sb[:, k * 2:k * 2 + 2],
                                start=(m == 0 and k == 0),
                                stop=(m == KE - 1 and k == KD - 1))
                    s_mm.inc(inst)
                    M[("halo", i)] = s_mm.n
                    if M1LVL < 3:
                        continue
                    if i == 0:
                        t.wait_ge(s_set.h, mg("m1w"))
                    for m in range(KE):
                        if m >= 2:
                            t.wait_ge(s_ss.h, mg("spe", i, m - 2))
                        for k in range(KE):
                            t.wait_ge(s_ss.h, mg("xc", i, k))
                            inst = t.matmul(
                                ps_inp[:, (m % 2) * 512:(m % 2) * 512 + 512],
                                dtw_sb[:, k * E + m * P:k * E + (m + 1) * P],
                                xc_sb[:, k * TS:(k + 1) * TS],
                                start=(k == 0), stop=(k == KE - 1))
                        s_mm.inc(inst)
                        M[("dt", i, m)] = s_mm.n
                        if m >= 1:
                            t.wait_ge(s_ss.h, mg("spe", i, m - 1))
                            inst = t.matmul(ps_stat[0:1, 0:512], onesb_sb[:, 0:1],
                                            sp_sb[:, ((m - 1) % 2) * W:
                                                  ((m - 1) % 2) * W + TS],
                                            start=(m == 1), stop=False)
                            s_mm.inc(inst)
                            M[("dtma", i, m - 1)] = s_mm.n
                    t.wait_ge(s_ss.h, mg("spe", i, KE - 1))
                    inst = t.matmul(ps_stat[0:1, 0:512], onesb_sb[:, 0:1],
                                    sp_sb[:, ((KE - 1) % 2) * W:
                                          ((KE - 1) % 2) * W + TS],
                                    start=False, stop=True)
                    s_mm.inc(inst)
                    M[("dtma", i, KE - 1)] = s_mm.n
                    M[("dtm", i)] = s_mm.n
                    if M1LVL < 4:
                        continue
                    t.wait_ge(s_ss.h, mg("spe", i, KE - 1))
                    projs = [0, 2] if not own else [0, 1, 2]
                    for pi in projs:
                        col0 = 16 * pi
                        dst = (ps_inp[0:S, 0:512], ps_inp[0:S, 512:1024],
                               ps_bc[0:S, 0:512])[pi]
                        for k in range(KE):
                            inst = t.matmul(
                                dst, bcdw_sb[:, k * 48 + col0:k * 48 + col0 + S],
                                xc_sb[:, k * TS:(k + 1) * TS],
                                start=(k == 0), stop=(k == KE - 1))
                        s_mm.inc(inst)
                        M[("bcd", i, pi)] = s_mm.n
                    if M1LVL < 5:
                        continue
                    for c in range(CPT):
                        t.wait_ge(s_vv.h, mg("ldrow", i, c))
                        s_mm.inc(t.transpose(
                            ps_sml[:, 128:129],
                            rows_sb[0:1, 514 + c * P:514 + (c + 1) * P],
                            idf_sb[0:1, 0:1]))
                        t.drain()
                        M[("ldtr", i, c)] = s_mm.n
                        if M1LVL < 6:
                            continue
                        t.wait_ge(s_vv.h, mg("ldcp", i, c))
                        s_mm.inc(t.matmul(ps_sml[:, 129:130], ut_sb[:],
                                          ldc_sb[:], start=True, stop=True))
                        t.drain()
                        M[("cl", i, c)] = s_mm.n
                        if M1LVL < 7:
                            continue
                        t.wait_ge(s_vv.h, mg("xscr", i, c))
                        t.wait_ge(s_vv.h, mg("nclcp", i, c))
                        s_mm.inc(t.matmul(ps_sml[:, 0:128], ones_sb[:],
                                          xscr_sb[:], start=True, stop=True))
                        t.drain()
                        M[("R", i, c)] = s_mm.n
                        if M1LVL < 8:
                            continue
                        t.wait_ge(s_vv.h, mg("rmcp", i, c))
                        t.wait_ge(s_ss.h, mg("expcl", i, c))
                        t.wait_ge(s_vv.h, mg("urdy", i))
                        if "utr" not in M1SKIP:
                            s_mm.inc(t.transpose(ps_smlb[:, 0:16],
                                                 u_sb[0:S, c * P:(c + 1) * P],
                                                 idb_sb[0:S, 0:S]))
                            t.drain()
                            M[("utr", i, c)] = s_mm.n
                        if M1LVL < 9:
                            continue
                        t.wait_ge(s_vv.h, mg("ucp", i, c))
                        if own and "ctr" not in M1SKIP:
                            s_mm.inc(t.transpose(ps_smlb[:, 16:32],
                                                 ct_sb[0:S, c * P:(c + 1) * P],
                                                 idb_sb[0:S, 0:S]))
                            t.drain()
                            M[("ctr", i, c)] = s_mm.n
                            t.wait_ge(s_vv.h, mg("ccp", i, c))
                            t.wait_ge(s_ss.h, mg("G", i, c))
                            if "gu" not in M1SKIP:
                                s_mm.inc(t.matmul(ps_sml[:, 176:192], g_sb[:],
                                                  utm_sb[:], start=True, stop=True))
                                t.drain()
                                M[("GU", i, c)] = s_mm.n
                            t.wait_ge(s_vv.h, mg("yscp", i, c))
                        else:
                            t.wait_ge(s_ss.h, mg("G", i, c))
                        if "bk" not in M1SKIP:
                            s_mm.inc(t.matmul(ps_sml[0:1, 192:208], g_sb[:, 127:128],
                                              utm_sb[:], start=True, stop=True))
                            t.drain()
                            M[("bk", i, c)] = s_mm.n
                        t.wait_ge(s_vv.h, mg("bcp", i, c))
                M[("m1mm",)] = s_mm.n

            @block.scalar
            def _(s):
                sc = Eng(s)
                for i, (part, j) in enumerate(tiles):
                    own = (part == 0)
                    xo = (i % 2) * KD * W
                    sem = s_lxa if i % 2 == 0 else s_lxb
                    s.wait_ge(sem.h, 16 * (i // 2 + 1))
                    if i >= 1:
                        s.wait_ge(s_mm.h, mg("dtm", i - 1))
                    for k in range(KD):
                        if k >= 2:
                            s.wait_ge(s_mm.h, mg("sq", i, k - 2))
                        s_ss.inc(sc.act(xsq_sb[:, (k % 2) * W:(k % 2 + 1) * W],
                                        x_buf[:, xo + k * W:xo + (k + 1) * W],
                                        AF.Square))
                        M[("xsq", i, k)] = s_ss.n
                    s.wait_ge(s_vv.h, mg("var", i))
                    s_ss.inc(sc.act(rows_sb[0:1, 1542:2056],
                                    rows_sb[0:1, 514:1028], AF.Sqrt))
                    M[("sqrt", i)] = s_ss.n
                    if M1LVL < 2:
                        continue
                    for m in range(KE):
                        s.wait_ge(s_mm.h, mg("inp", i, m))
                        s_ss.inc(sc.act(xm_sb[:, m * W + 1:m * W + 513],
                                        ps_inp[:, (m % 2) * 512:(m % 2) * 512 + 512],
                                        AF.Identity, bias=bias_sb[:, m:m + 1]))
                        M[("xme", i, m)] = s_ss.n
                    for k in range(KE):
                        s.wait_ge(s_vv.h, mg("conv", i, k))
                        src = (cs1_sb if k % 2 == 0 else cs3_sb)
                        s_ss.inc(sc.act(xc_sb[:, k * TS:(k + 1) * TS],
                                        src[:, 0:TS], AF.Silu,
                                        bias=bias_sb[:, 80 + k:80 + k + 1]))
                        M[("xc", i, k)] = s_ss.n
                    if M1LVL < 3:
                        continue
                    for m in range(KE):
                        s.wait_ge(s_mm.h, mg("dt", i, m))
                        if m >= 2:
                            s.wait_ge(s_mm.h, mg("dtma", i, m - 2))
                        sc.act(expz_sb[:],
                               ps_inp[:, (m % 2) * 512:(m % 2) * 512 + 512],
                               AF.Exp, bias=bias_sb[:, 96 + m:96 + m + 1])
                        s_ss.inc(sc.act(sp_sb[:, (m % 2) * W:(m % 2) * W + TS],
                                        expz_sb[:], AF.Ln, bias=1.0))
                        M[("spe", i, m)] = s_ss.n
                    if M1LVL < 4:
                        continue
                    projs = [0, 2] if not own else [0, 1, 2]
                    for pi in projs:
                        s.wait_ge(s_mm.h, mg("bcd", i, pi))
                        src = (ps_inp[0:S, 0:512], ps_inp[0:S, 512:1024],
                               ps_bc[0:S, 0:512])[pi]
                        dst = (bt_sb, ct_sb, dt2_sb)[pi]
                        s_ss.inc(sc.act(dst[:], src, AF.Identity,
                                        bias=bcdb_sb[:, pi:pi + 1]))
                        M[("bcde", i, pi)] = s_ss.n
                    if not own:
                        M[("bcde", i, 1)] = s_ss.n
                    if M1LVL < 5:
                        continue
                    for c in range(CPT):
                        if M1LVL < 8:
                            continue
                        ch = j * CPT + c
                        hoff = 0 if own else TPH
                        if "acts" not in M1SKIP:
                            s.wait_ge(s_vv.h, mg("rmcp", i, c))
                            sc.act(expcl_sb[0:1, hoff + ch * P:hoff + (ch + 1) * P],
                                   rows_sb[0:1, 1542:1670], AF.Exp)
                            hoc = 0 if own else NCH
                            s_ss.inc(sc.act(aall_sb[0:1, hoc + ch:hoc + ch + 1],
                                            rows_sb[0:1, 1669:1670], AF.Exp))
                            M[("expcl", i, c)] = s_ss.n
                        if "g" not in M1SKIP:
                            s.wait_ge(s_vv.h, mg("rmcp", i, c))
                            s_ss.inc(sc.act(g_sb[:], rm_sb[:], AF.Exp))
                            M[("G", i, c)] = s_ss.n
                M[("m1ssend",)] = s_ss.n

            @block.vector
            def _(v):
                ve = Eng(v)
                for i, (part, j) in enumerate(tiles):
                    own = (part == 0)
                    xo = (i % 2) * KD * W
                    sem = s_lxa if i % 2 == 0 else s_lxb
                    v.wait_ge(sem.h, 16 * (i // 2 + 1))
                    v.wait_ge(s_mm.h, mg("sums", i))
                    ve.ts(rows_sb[0:1, 0:512], ps_stat[0:1, 0:512],
                          1.0 / D, None, ALU.mult)
                    s_vv.inc(ve.ts(rows_sb[0:1, 512:514], ps_halo[0:1, 32:34],
                                   1.0 / D, None, ALU.mult))
                    M[("meancp", i)] = s_vv.n
                    v.wait_ge(s_mm.h, mg("sq", i, KD - 1))
                    ve.ts(rows_sb[0:1, 514:1026], ps_stat[0:1, 0:512],
                          1.0 / D, None, ALU.mult)
                    ve.ts(rows_sb[0:1, 1026:1028], ps_halo[0:1, 32:34],
                          1.0 / D, None, ALU.mult)
                    ve.tt(rows_sb[0:1, 1028:1542], rows_sb[0:1, 0:514],
                          rows_sb[0:1, 0:514], ALU.mult)
                    ve.tt(rows_sb[0:1, 514:1028], rows_sb[0:1, 514:1028],
                          rows_sb[0:1, 1028:1542], ALU.subtract)
                    s_vv.inc(ve.ts(rows_sb[0:1, 514:1028],
                                   rows_sb[0:1, 514:1028], 1e-5, None, ALU.add))
                    M[("var", i)] = s_vv.n
                    v.wait_ge(s_ss.h, mg("sqrt", i))
                    ve.recip(rows_sb[0:1, 1028:1542], rows_sb[0:1, 1542:2056])
                    ve.tt(rows_sb[0:1, 1542:2056], rows_sb[0:1, 0:514],
                          rows_sb[0:1, 1028:1542], ALU.mult)
                    s_vv.inc(ve.ts(rows_sb[0:1, 1542:2056],
                                   rows_sb[0:1, 1542:2056], -1.0, None, ALU.mult))
                    M[("arow", i)] = s_vv.n
                    v.wait_ge(s_mm.h, mg("abc", i))
                    s_vv.inc(ve.cp(bc_sb[:, 0:W], ps_bc[:, 0:514]))
                    M[("acp", i)] = s_vv.n
                    v.wait_ge(s_mm.h, mg("bbc", i))
                    if i >= 1:
                        v.wait_ge(s_mm.h, mg("halo", i - 1))
                    ve.cp(bc_sb[:, W:2 * W], ps_bc[:, 0:514])
                    hdst = h_sb
                    ho = 0
                    for k in range(KD):
                        ve.tt(cs1_sb[:], x_buf[:, xo + k * W:xo + (k + 1) * W],
                              bc_sb[:, 0:W], ALU.mult)
                        ve.tt(cs2_sb[:], cs1_sb[:], bc_sb[:, W:2 * W], ALU.add)
                        ve.ts(hdst[:, ho + k * W:ho + (k + 1) * W], cs2_sb[:],
                              modv_sb[:, 8 + k:8 + k + 1],
                              modv_sb[:, k:k + 1], ALU.mult, ALU.add)
                    ve.cp(hpair_sb[:],
                          bass.AP(h_sb, ho, [[KD * W, P], [W, KD], [513, 2]]))
                    s_vv.inc(ve.dinc())
                    M[("happ", i)] = s_vv.n
                    s_fx.inc(ve.dinc())
                    if M1LVL < 2:
                        continue
                    v.wait_ge(s_mm.h, mg("halo", i))
                    lmask = (part * 2) if j == 0 else 4
                    rmask = (part * 2 + 1) if j == NT - 1 else 4
                    for m in range(KE):
                        for hc, col, mi in ((0, 0, lmask), (1, 513, rmask)):
                            if mi == 4:
                                ve.ts(xm_sb[:, m * W + col:m * W + col + 1],
                                      ps_halo[:, m * 2 + hc:m * 2 + hc + 1],
                                      bias_sb[:, m:m + 1], None, ALU.add)
                            else:
                                ve.ts(xm_sb[:, m * W + col:m * W + col + 1],
                                      ps_halo[:, m * 2 + hc:m * 2 + hc + 1],
                                      bias_sb[:, m:m + 1],
                                      hm_sb[:, mi:mi + 1], ALU.add, ALU.mult)
                    s_vv.inc(ve.dinc())
                    M[("stitch", i)] = s_vv.n
                    for k in range(KE):
                        if k >= 2:
                            v.wait_ge(s_ss.h, mg("xc", i, k - 2))
                        v.wait_ge(s_ss.h, mg("xme", i, k))
                        if k == 0:
                            v.wait_ge(s_vv.h, mg("stitch", i))
                        acc = cs1_sb if k % 2 == 0 else cs3_sb
                        oth = cs2_sb if k % 2 == 0 else cs4_sb
                        ve.ts(oth[:, 0:TS], xm_sb[:, k * W + 2:k * W + 2 + TS],
                              bias_sb[:, 64 + k:64 + k + 1], None, ALU.mult)
                        ve.stt(acc[:, 0:TS], xm_sb[:, k * W + 1:k * W + 1 + TS],
                               bias_sb[:, 48 + k:48 + k + 1], oth[:, 0:TS],
                               ALU.mult, ALU.add)
                        s_vv.inc(ve.stt(acc[:, 0:TS], xm_sb[:, k * W:k * W + TS],
                                        bias_sb[:, 32 + k:32 + k + 1],
                                        acc[:, 0:TS], ALU.mult, ALU.add))
                        M[("conv", i, k)] = s_vv.n
                    if M1LVL < 4:
                        continue
                    v.wait_ge(s_ss.h, mg("bcde", i, 0))
                    v.wait_ge(s_ss.h, mg("bcde", i, 2))
                    s_vv.inc(ve.tt(u_sb[:], bt_sb[:], dt2_sb[:], ALU.mult))
                    M[("urdy", i)] = s_vv.n
                    if M1LVL < 5:
                        continue
                    for c in range(CPT):
                        ch = j * CPT + c
                        hoff = 0 if own else NCH * S
                        v.wait_ge(s_mm.h, mg("dtm", i))
                        s_vv.inc(ve.ts(
                            rows_sb[0:1, 514 + c * P:514 + (c + 1) * P],
                            ps_stat[0:1, c * P:(c + 1) * P],
                            -1.0 / E, -10.0, ALU.mult, ALU.max))
                        M[("ldrow", i, c)] = s_vv.n
                        if M1LVL < 6:
                            continue
                        v.wait_ge(s_mm.h, mg("ldtr", i, c))
                        ve.cp(ldc_sb[:], ps_sml[:, 128:129])
                        s_vv.inc(ve.ts(xscr_sb[:], ut_sb[:], ldc_sb[:, 0:1],
                                       None, ALU.mult))
                        M[("ldcp", i, c)] = s_vv.n
                        M[("xscr", i, c)] = s_vv.n
                        if M1LVL < 7:
                            continue
                        v.wait_ge(s_mm.h, mg("cl", i, c))
                        s_vv.inc(ve.ts(ncl_sb[:], ps_sml[:, 129:130],
                                       -1.0, None, ALU.mult))
                        M[("nclcp", i, c)] = s_vv.n
                        v.wait_ge(s_mm.h, mg("R", i, c))
                        ve.tt(rm_sb[:], ps_sml[:, 0:128], mk_sb[:], ALU.add)
                        ve.ts(rm_sb[:], rm_sb[:], ncl_sb[:, 0:1], -80.0,
                              ALU.add, ALU.max)
                        s_vv.inc(ve.ts(rows_sb[0:1, 1542:1670],
                                       ps_sml[0:1, 0:128], -80.0, None,
                                       ALU.max))
                        M[("rmcp", i, c)] = s_vv.n
                        if M1LVL < 8:
                            continue
                        if "utr" not in M1SKIP:
                            v.wait_ge(s_mm.h, mg("utr", i, c))
                            s_vv.inc(ve.cp(utm_sb[:], ps_smlb[:, 0:16]))
                            M[("ucp", i, c)] = s_vv.n
                        if M1LVL < 9:
                            continue
                        if own:
                            if "ctr" not in M1SKIP:
                                v.wait_ge(s_mm.h, mg("ctr", i, c))
                                s_vv.inc(ve.cp(call_sb[:, ch * S:(ch + 1) * S],
                                               ps_smlb[:, 16:32]))
                                M[("ccp", i, c)] = s_vv.n
                            if "gu" not in M1SKIP and "yscp" not in M1SKIP:
                                v.wait_ge(s_mm.h, mg("GU", i, c))
                                ve.tt(tsc_sb[:], call_sb[:, ch * S:(ch + 1) * S],
                                      ps_sml[:, 176:192], ALU.mult)
                                s_vv.inc(ve.rsum(ysi_sb[:, ch:ch + 1], tsc_sb[:]))
                                M[("yscp", i, c)] = s_vv.n
                        if "bk" not in M1SKIP:
                            v.wait_ge(s_mm.h, mg("bk", i, c))
                            s_vv.inc(ve.cp(
                                ball_sb[0:1, hoff + ch * S:hoff + (ch + 1) * S],
                                ps_sml[0:1, 192:208]))
                            M[("bcp", i, c)] = s_vv.n

        if upto == "m1":
            if dbg:
                _dbg_emit(block, dbg_rows, dbg_cols, expcl_sb, ball_sb, ysi_sb,
                          ysr_sb, None, s_vv, s_ss, s_gf, zero_rows=True)
            _finish(block, s_st)
            return nc, dram

        with contextlib.ExitStack() as es:
            L = es.enter_context
            inwg_sb = L(nc.sbuf_tensor("inwg_sb", [P, KD * E], BF16))
            outw_sb = L(nc.sbuf_tensor("outw_sb", [P, KE * D], BF16))
            hg_sb = L(nc.sbuf_tensor("hg_sb", [P, KD * TS], BF16))
            xsq2_sb = L(nc.sbuf_tensor("xsq2_sb", [P, 2 * TS], BF16))
            xq2_sb = L(nc.sbuf_tensor("xq2_sb", [P, 2 * TS], BF16))
            wr_sb = L(nc.sbuf_tensor("wr_sb", [P, KD * 8], BF16))
            sg_sb = L(nc.sbuf_tensor("sg_sb", [P, KE * TS], BF16))
            ap_sb = L(nc.sbuf_tensor("ap_sb", [P, KD * TS], BF16))
            x2_sb = L(nc.sbuf_tensor("x2_sb", [P, KD * TS], F32))
            x2b_sb = L(nc.sbuf_tensor("x2b_sb", [P, KD * TS], BF16))
            h2t_sb = L(nc.sbuf_tensor("h2t_sb", [P, KD * TS], BF16))
            rt_sb = L(nc.sbuf_tensor("rt_sb", [8, TS], F32))
            rtc_sb = L(nc.sbuf_tensor("rtc_sb", [P, 8], F32))
            mx_sb = L(nc.sbuf_tensor("mx_sb", [P, 8], F32))
            em2all_sb = L(nc.sbuf_tensor("em2all_sb", [P, 2 * CPT], F32))
            emtmp_sb = L(nc.sbuf_tensor("emtmp_sb", [8, P], F32))
            mi8_sb = L(nc.sbuf_tensor("mi8_sb", [P, 8], U32))
            mcs1_sb = L(nc.sbuf_tensor("mcs1", [P, TS], F32))
            mcs2_sb = L(nc.sbuf_tensor("mcs2", [P, TS], F32))
            bcg_sb = L(nc.sbuf_tensor("bcg_sb", [P, 2 * TS], BF16))
            bc2_sb = L(nc.sbuf_tensor("bc2_sb", [P, 2 * TS], BF16))
            p1s_sb = L(nc.sbuf_tensor("p1s_sb", [P, S], F32))
            tsc2_sb = L(nc.sbuf_tensor("tsc2_sb", [P, S], F32))

            ps_g = L(nc.psum_tensor("ps_g", [P, 1024], F32))
            ps_b2 = L(nc.psum_tensor("ps_b2", [P, 1024], F32))
            ps_s2 = L(nc.psum_tensor("ps_s2", [P, 512], F32))
            ps_st2 = L(nc.psum_tensor("ps_st2", [1, 512], F32))
            ps_tb = L(nc.psum_tensor("ps_tb", [P, 8], F32))
            ps_sq2 = L(nc.psum_tensor("ps_sq2", [1, 512], F32))

            @block.gpsimd
            def _(g):
                g.wait_ge(s_mm.h, mg("m1mm"))
                s_m2w.inc(g.dma_start(
                    inwg_sb[:].rearrange("p (k n) -> p k n", k=KD), r3(inwg)), 16)
                s_m2w.inc(g.dma_start(
                    outw_sb[:].rearrange("p (k n) -> p k n", k=KE), r3(outw)), 16)
                s_m2w.inc(g.dma_start(
                    wr_sb[:].rearrange("p (k n) -> p k n", k=KD), r3(wr)), 16)
                for j in range(NT):
                    i = 2 * NT + j
                    g.wait_ge(s_fx.h, i - 1)
                    sem = s_lxa if i % 2 == 0 else s_lxb
                    sem.inc(g.dma_start(
                        x_buf[:, (i % 2) * KD * W:((i % 2) + 1) * KD * W]
                        .rearrange("p (k n) -> p k n", k=KD),
                        r3(x)[:, :, j * TS:j * TS + W]), 16)
                for j in range(NT):
                    g.wait_ge(s_vv.h, mg("emrdy", j))
                    for c in range(CPT):
                        s_gf.inc(g.dma_start(
                            emr_d[0:2, j * TS + c * P:j * TS + (c + 1) * P],
                            emtmp_sb[2 * c:2 * c + 2, 0:P]), 16)
                    M[("m2gf", j)] = s_gf.n
                M[("m2gf",)] = s_gf.n

            @block.tensor
            def _(t):
                for k in range(NCH):
                    t.wait_ge(s_vv.h, mg("schain", k))
                    s_mm.inc(t.matmul(ps_s2[:, 0:S],
                                      expcl_sb[0:1, k * P:(k + 1) * P],
                                      srowb_sb[0:1, (k % 2) * S:(k % 2) * S + S],
                                      start=True, stop=True))
                    M[("P1", k)] = s_mm.n
                    t.wait_ge(s_vv.h, mg("ysc", k))
                    s_mm.inc(t.transpose(ps_s2[0:1, 256:384], p1s_sb[:, 1:2],
                                         idf_sb[:]))
                    t.drain()
                    M[("ystr", k)] = s_mm.n
                    t.wait_ge(s_vv.h, mg("ysrcp", k))

                def t_stats(j):
                    i = 2 * NT + j
                    sem = s_lxa if i % 2 == 0 else s_lxb
                    t.wait_ge(sem.h, 16 * (i // 2 + 1))
                    xo = (i % 2) * KD * W
                    if j == 1:
                        t.wait_ge(s_vv.h, mg("2var", 0))
                    elif j >= 2:
                        t.wait_ge(s_vv.h, mg("var2", j - 2))
                    for k in range(KD):
                        inst = t.matmul(ps_st2[0:1, 0:512], onesb_sb[:, 0:1],
                                        x_buf[:, xo + k * W + 1:xo + k * W + 513],
                                        start=(k == 0), stop=(k == KD - 1))
                    s_mm.inc(inst)
                    M[("2sums", j)] = s_mm.n
                    for k in range(KD):
                        t.wait_ge(s_ss.h, mg("2xsq", j, k))
                        inst = t.matmul(ps_sq2[0:1, 0:512], onesb_sb[:, 0:1],
                                        xsq2_sb[:, (k % 2) * TS:(k % 2) * TS + TS],
                                        start=(k == 0), stop=(k == KD - 1))
                        s_mm.inc(inst)
                        M[("2sq", j, k)] = s_mm.n

                def t_bc(j):
                    t.wait_ge(s_vv.h, mg("2arow", j))
                    if j == 1:
                        t.wait_ge(s_vv.h, mg("2happ", 0))
                    elif j >= 2:
                        t.wait_ge(s_vv.h, mg("h2", j - 2))
                    s_mm.inc(t.matmul(ps_b2[:, 512:1024], ones_sb[0:1, :],
                                      rows_sb[0:1, 1028:1540],
                                      start=True, stop=True))
                    M[("2abc", j)] = s_mm.n
                    t.wait_ge(s_vv.h, mg("2acp", j))
                    s_mm.inc(t.matmul(ps_b2[:, 512:1024], ones_sb[0:1, :],
                                      rows_sb[0:1, 1542:2054],
                                      start=True, stop=True))
                    M[("2bbc", j)] = s_mm.n

                def t_gate(j):
                    t.wait_ge(s_vv.h, mg("2happ", j))
                    if j == 0:
                        t.wait_ge(s_m2w.h, s_m2w.n)
                    if j >= 1:
                        t.wait_ge(s_ss.h, mg("ape", j - 1, KD - 1))
                    for m in range(KE):
                        if m >= 2:
                            t.wait_ge(s_ss.h, mg("sge", j, m - 2))
                        for k in range(KD):
                            inst = t.matmul(
                                ps_g[:, (m % 2) * 512:(m % 2) * 512 + 512],
                                inwg_sb[:, k * E + m * P:k * E + (m + 1) * P],
                                hg_sb[:, k * TS:(k + 1) * TS],
                                start=(k == 0), stop=(k == KD - 1))
                        s_mm.inc(inst)
                        M[("gate", j, m)] = s_mm.n

                def t_op(j):
                    t.wait_ge(s_ss.h, mg("sge", j, KE - 1))
                    for m in range(KD):
                        if m >= 2:
                            t.wait_ge(s_ss.h, mg("ape", j, m - 2))
                        for k in range(KE):
                            inst = t.matmul(
                                ps_g[:, (m % 2) * 512:(m % 2) * 512 + 512],
                                outw_sb[:, k * D + m * P:k * D + (m + 1) * P],
                                sg_sb[:, k * TS:(k + 1) * TS],
                                start=(k == 0), stop=(k == KE - 1))
                        s_mm.inc(inst)
                        M[("op", j, m)] = s_mm.n

                def t_ysb(j):
                    t.wait_ge(s_vv.h, mg("ysrcp", NCH - 1))
                    if j >= 1:
                        t.wait_ge(s_vv.h, mg("x2", j - 1))
                    s_mm.inc(t.matmul(ps_b2[:, 0:512], onesb_sb[0:1, :],
                                      ysr_sb[0:1, j * TS:(j + 1) * TS],
                                      start=True, stop=True))
                    M[("ysb", j)] = s_mm.n

                def t_bstats(j):
                    t.wait_ge(s_vv.h, mg("x2b", j))
                    if j + 1 < NT:
                        t.wait_ge(s_vv.h, mg("2mean", j + 1))
                    elif j >= 1:
                        t.wait_ge(s_vv.h, mg("mean2", j - 1))
                    for k in range(KD):
                        inst = t.matmul(ps_st2[0:1, 0:512], onesb_sb[:, 0:1],
                                        x2b_sb[:, k * TS:(k + 1) * TS],
                                        start=(k == 0), stop=(k == KD - 1))
                    s_mm.inc(inst)
                    M[("sums2", j)] = s_mm.n
                    if j + 1 < NT:
                        t.wait_ge(s_vv.h, mg("2var", j + 1))
                    elif j >= 1:
                        t.wait_ge(s_vv.h, mg("var2", j - 1))
                    for k in range(KD):
                        t.wait_ge(s_ss.h, mg("x2sqk", j, k))
                        inst = t.matmul(ps_sq2[0:1, 0:512], onesb_sb[:, 0:1],
                                        xq2_sb[:, (k % 2) * TS:(k % 2) * TS + TS],
                                        start=(k == 0), stop=(k == KD - 1))
                        s_mm.inc(inst)
                        M[("sq2", j, k)] = s_mm.n

                def t_abc2(j):
                    t.wait_ge(s_vv.h, mg("arow2", j))
                    if j + 1 < NT:
                        t.wait_ge(s_vv.h, mg("2happ", j + 1))
                    elif j >= 1:
                        t.wait_ge(s_vv.h, mg("h2", j - 1))
                    s_mm.inc(t.matmul(ps_b2[:, 512:1024], ones_sb[0:1, :],
                                      rows_sb[0:1, 1028:1540],
                                      start=True, stop=True))
                    M[("abc2", j)] = s_mm.n
                    t.wait_ge(s_vv.h, mg("acp2", j))
                    s_mm.inc(t.matmul(ps_b2[:, 512:1024], ones_sb[0:1, :],
                                      rows_sb[0:1, 1542:2054],
                                      start=True, stop=True))
                    M[("bbc2", j)] = s_mm.n

                def t_route(j):
                    t.wait_ge(s_vv.h, mg("h2", j))
                    if j >= 1:
                        t.wait_ge(s_vv.h, mg("emcp", j - 1))
                    for k in range(KD):
                        inst = t.matmul(ps_s2[0:8, 0:512],
                                        wr_sb[:, k * 8:(k + 1) * 8],
                                        h2t_sb[:, k * TS:(k + 1) * TS],
                                        start=(k == 0), stop=(k == KD - 1))
                    s_mm.inc(inst)
                    M[("route", j)] = s_mm.n

                def t_rtt(j):
                    for c in range(CPT):
                        t.wait_ge(s_ss.h, mg("rte", j))
                        if c >= 1:
                            t.wait_ge(s_vv.h, mg("em2", j, c - 1))
                        elif j >= 1:
                            t.wait_ge(s_vv.h, mg("em2", j - 1, CPT - 1))
                        s_mm.inc(t.transpose(ps_tb[:],
                                             rt_sb[0:8, c * P:(c + 1) * P],
                                             idf_sb[0:8, 0:8]))
                        t.drain()
                        M[("rtt", j, c)] = s_mm.n

                def t_emt(j):
                    t.wait_ge(s_vv.h, mg("em2", j, CPT - 1))
                    t.wait_ge(s_ss.h, mg("rte", j))
                    s_mm.inc(t.transpose(ps_s2[0:8, 256:384], em2all_sb[:],
                                         idf_sb[:]))
                    t.drain()
                    M[("emt", j)] = s_mm.n

                t_stats(0)
                t_bc(0)
                t_gate(0)
                t_op(0)
                t_ysb(0)
                for j in range(NT):
                    if j + 1 < NT:
                        t_stats(j + 1)
                    t_bstats(j)
                    if j >= 1:
                        t_emt(j - 1)
                    if j + 1 < NT:
                        t_bc(j + 1)
                        t_gate(j + 1)
                    t_abc2(j)
                    if j + 1 < NT:
                        t_op(j + 1)
                        t_ysb(j + 1)
                    t_route(j)
                    t_rtt(j)
                t_emt(NT - 1)

            @block.scalar
            def _(s):
                sc = Eng(s)

                def s_stats(j):
                    i = 2 * NT + j
                    sem = s_lxa if i % 2 == 0 else s_lxb
                    s.wait_ge(sem.h, 16 * (i // 2 + 1))
                    xo = (i % 2) * KD * W
                    if j >= 1:
                        s.wait_ge(s_mm.h, mg("2sq", j - 1, KD - 1))
                    for k in range(KD):
                        if k >= 2:
                            s.wait_ge(s_mm.h, mg("2sq", j, k - 2))
                        s_ss.inc(sc.act(xsq2_sb[:, (k % 2) * TS:(k % 2 + 1) * TS],
                                        x_buf[:, xo + k * W + 1:xo + k * W + 513],
                                        AF.Square))
                        M[("2xsq", j, k)] = s_ss.n

                def s_x2sq(j):
                    s.wait_ge(s_vv.h, mg("x2b", j))
                    if j >= 1:
                        s.wait_ge(s_mm.h, mg("sq2", j - 1, KD - 1))
                    for k in range(KD):
                        if k >= 2:
                            s.wait_ge(s_mm.h, mg("sq2", j, k - 2))
                        s_ss.inc(sc.act(xq2_sb[:, (k % 2) * TS:(k % 2 + 1) * TS],
                                        x2b_sb[:, k * TS:(k + 1) * TS],
                                        AF.Square))
                        M[("x2sqk", j, k)] = s_ss.n

                def s_2sqrt(j):
                    s.wait_ge(s_vv.h, mg("2var", j))
                    if j == 1:
                        s.wait_ge(s_mm.h, mg("2bbc", 0))
                    elif j >= 2:
                        s.wait_ge(s_mm.h, mg("bbc2", j - 2))
                    s_ss.inc(sc.act(rows_sb[0:1, 1542:2054],
                                    rows_sb[0:1, 514:1026], AF.Sqrt))
                    M[("2sqrt", j)] = s_ss.n

                def s_sqrt2(j):
                    s.wait_ge(s_vv.h, mg("var2", j))
                    if j + 1 < NT:
                        s.wait_ge(s_mm.h, mg("2bbc", j + 1))
                    s_ss.inc(sc.act(rows_sb[0:1, 1542:2054],
                                    rows_sb[0:1, 514:1026], AF.Sqrt))
                    M[("sqrt2", j)] = s_ss.n

                def s_sge(j, m):
                    s.wait_ge(s_mm.h, mg("gate", j, m))
                    s_ss.inc(sc.act(sg_sb[:, m * TS:(m + 1) * TS],
                                    ps_g[:, (m % 2) * 512:(m % 2) * 512 + 512],
                                    AF.Silu,
                                    bias=bias_sb[:, 16 + m:16 + m + 1]))
                    M[("sge", j, m)] = s_ss.n

                def s_ape(j, m):
                    s.wait_ge(s_mm.h, mg("op", j, m))
                    if j >= 1:
                        s.wait_ge(s_vv.h, mg("x2d", j - 1, m))
                    s_ss.inc(sc.act(ap_sb[:, m * TS:(m + 1) * TS],
                                    ps_g[:, (m % 2) * 512:(m % 2) * 512 + 512],
                                    AF.Identity))
                    M[("ape", j, m)] = s_ss.n

                def s_rte(j):
                    s.wait_ge(s_mm.h, mg("route", j))
                    s_ss.inc(sc.act(rt_sb[:], ps_s2[0:8, 0:512], AF.Sigmoid,
                                    bias=br_sb[:, 0:1]))
                    M[("rte", j)] = s_ss.n

                s_stats(0)
                s_2sqrt(0)
                for m in range(KE):
                    s_sge(0, m)
                for m in range(KD):
                    s_ape(0, m)
                for j in range(NT):
                    if j + 1 < NT:
                        s_stats(j + 1)
                        s_2sqrt(j + 1)
                    s_x2sq(j)
                    if j + 1 < NT:
                        s_sge(j + 1, 0)
                        s_sge(j + 1, 1)
                        s_sqrt2(j)
                        for m in range(2, KE):
                            s_sge(j + 1, m)
                        for m in range(KD):
                            s_ape(j + 1, m)
                    else:
                        s_sqrt2(j)
                    s_rte(j)

            @block.vector
            def _(v):
                ve = Eng(v)
                v.wait_ge(s_ss.h, mg("m1ssend"))
                ve.mset(srow_sb[0:1, 32:48], 0.0)
                cur = 32
                for k in range(NCH):
                    nxt = 48 if cur == 32 else 32
                    ve.stt(srow_sb[0:1, nxt:nxt + S], srow_sb[0:1, cur:cur + S],
                           aall_sb[0:1, NCH + k:NCH + k + 1],
                           ball_sb[0:1, NCH * S + k * S:NCH * S + (k + 1) * S],
                           ALU.mult, ALU.add)
                    cur = nxt
                ve.ts(srow_sb[0:1, 0:S], srow_sb[0:1, cur:cur + S],
                      flag_sb[0:1, 0:1], None, ALU.mult)
                s_vv.inc(ve.cp(srowb_sb[0:1, 0:S], srow_sb[0:1, 0:S]))
                M[("schain", 0)] = s_vv.n
                for k in range(NCH):
                    v.wait_ge(s_mm.h, mg("P1", k))
                    ve.tt(tsc2_sb[:], call_sb[:, k * S:(k + 1) * S],
                          ps_s2[:, 0:S], ALU.mult)
                    ve.rsum(p1s_sb[:, 0:1], tsc2_sb[:])
                    s_vv.inc(ve.tt(p1s_sb[:, 1:2], p1s_sb[:, 0:1],
                                   ysi_sb[:, k:k + 1], ALU.add))
                    M[("ysc", k)] = s_vv.n
                    nxt_k = ((k + 1) % 2) * S
                    cur_k = (k % 2) * S
                    ve.stt(srow_sb[0:1, nxt_k:nxt_k + S],
                           srow_sb[0:1, cur_k:cur_k + S],
                           aall_sb[0:1, k:k + 1],
                           ball_sb[0:1, k * S:(k + 1) * S],
                           ALU.mult, ALU.add)
                    s_vv.inc(ve.cp(srowb_sb[0:1, nxt_k:nxt_k + S],
                                   srow_sb[0:1, nxt_k:nxt_k + S]))
                    M[("schain", k + 1)] = s_vv.n
                    v.wait_ge(s_mm.h, mg("ystr", k))
                    s_vv.inc(ve.cp(ysr_sb[0:1, k * P:(k + 1) * P],
                                   ps_s2[0:1, 256:384]))
                    M[("ysrcp", k)] = s_vv.n
                def fv_rows(j):
                    if j == 1:
                        v.wait_ge(s_mm.h, mg("2abc", 0))
                    elif j >= 2:
                        v.wait_ge(s_mm.h, mg("abc2", j - 2))
                    v.wait_ge(s_mm.h, mg("2sums", j))
                    s_vv.inc(ve.ts(rows_sb[0:1, 0:512], ps_st2[0:1, 0:512],
                                   1.0 / D, None, ALU.mult))
                    M[("2mean", j)] = s_vv.n
                    v.wait_ge(s_mm.h, mg("2sq", j, KD - 1))
                    ve.ts(rows_sb[0:1, 514:1026], ps_sq2[0:1, 0:512],
                          1.0 / D, None, ALU.mult)
                    ve.tt(rows_sb[0:1, 1028:1540], rows_sb[0:1, 0:512],
                          rows_sb[0:1, 0:512], ALU.mult)
                    ve.tt(rows_sb[0:1, 514:1026], rows_sb[0:1, 514:1026],
                          rows_sb[0:1, 1028:1540], ALU.subtract)
                    s_vv.inc(ve.ts(rows_sb[0:1, 514:1026],
                                   rows_sb[0:1, 514:1026], 1e-5, None, ALU.add))
                    M[("2var", j)] = s_vv.n
                    v.wait_ge(s_ss.h, mg("2sqrt", j))
                    ve.recip(rows_sb[0:1, 1028:1540], rows_sb[0:1, 1542:2054])
                    ve.tt(rows_sb[0:1, 1542:2054], rows_sb[0:1, 0:512],
                          rows_sb[0:1, 1028:1540], ALU.mult)
                    s_vv.inc(ve.ts(rows_sb[0:1, 1542:2054],
                                   rows_sb[0:1, 1542:2054], -1.0, None, ALU.mult))
                    M[("2arow", j)] = s_vv.n

                def fv_happ(j):
                    i = 2 * NT + j
                    sem = s_lxa if i % 2 == 0 else s_lxb
                    v.wait_ge(sem.h, 16 * (i // 2 + 1))
                    xo = (i % 2) * KD * W
                    v.wait_ge(s_mm.h, mg("2abc", j))
                    s_vv.inc(ve.cp(bcg_sb[:, 0:TS], ps_b2[:, 512:1024]))
                    M[("2acp", j)] = s_vv.n
                    v.wait_ge(s_mm.h, mg("2bbc", j))
                    if j >= 1:
                        v.wait_ge(s_mm.h, mg("gate", j - 1, KE - 1))
                    ve.cp(bcg_sb[:, TS:2 * TS], ps_b2[:, 512:1024])
                    for k in range(KD):
                        ve.tt(mcs1_sb[:], x_buf[:, xo + k * W + 1:xo + k * W + 513],
                              bcg_sb[:, 0:TS], ALU.mult)
                        ve.tt(mcs2_sb[:], mcs1_sb[:], bcg_sb[:, TS:2 * TS],
                              ALU.add)
                        ve.ts(hg_sb[:, k * TS:(k + 1) * TS], mcs2_sb[:],
                              modv_sb[:, 8 + k:8 + k + 1],
                              modv_sb[:, k:k + 1], ALU.mult, ALU.add)
                    s_vv.inc(ve.dinc())
                    M[("2happ", j)] = s_vv.n

                def bv_a(j):
                    i = 2 * NT + j
                    xo = (i % 2) * KD * W
                    v.wait_ge(s_mm.h, mg("ysb", j))
                    if j >= 1:
                        v.wait_ge(s_mm.h, mg("sums2", j - 1))
                        v.wait_ge(s_ss.h, mg("x2sqk", j - 1, KD - 1))
                        v.wait_ge(s_stx.h, mg("m2st_x", j - 1))
                    for d in range(KD):
                        v.wait_ge(s_ss.h, mg("ape", j, d))
                        ve.stt(mcs1_sb[:], ap_sb[:, d * TS:(d + 1) * TS],
                               modv_sb[:, 16 + d:16 + d + 1], ps_b2[:, 0:512],
                               ALU.mult, ALU.mult)
                        ve.ts(mcs2_sb[:], hg_sb[:, d * TS:(d + 1) * TS],
                              modv_sb[:, 16 + d:16 + d + 1],
                              modv_sb[:, 48 + d:48 + d + 1], ALU.mult, ALU.add)
                        ve.tt(mcs2_sb[:], mcs2_sb[:],
                              x_buf[:, xo + d * W + 1:xo + d * W + 513], ALU.add)
                        ve.tt(x2_sb[:, d * TS:(d + 1) * TS], mcs1_sb[:],
                              mcs2_sb[:], ALU.add)
                        s_vv.inc(ve.cp(x2b_sb[:, d * TS:(d + 1) * TS],
                                       x2_sb[:, d * TS:(d + 1) * TS]))
                        M[("x2d", j, d)] = s_vv.n
                    M[("x2", j)] = s_vv.n
                    M[("x2b", j)] = s_vv.n
                    s_fx.inc(ve.dinc())

                def bv_b1(j):
                    v.wait_ge(s_mm.h, mg("sums2", j))
                    s_vv.inc(ve.ts(rows_sb[0:1, 0:512], ps_st2[0:1, 0:512],
                                   1.0 / D, None, ALU.mult))
                    M[("mean2", j)] = s_vv.n
                    v.wait_ge(s_mm.h, mg("sq2", j, KD - 1))
                    if j + 1 < NT:
                        v.wait_ge(s_ss.h, mg("2sqrt", j + 1))
                        v.wait_ge(s_mm.h, mg("2abc", j + 1))
                    ve.ts(rows_sb[0:1, 514:1026], ps_sq2[0:1, 0:512],
                          1.0 / D, None, ALU.mult)
                    ve.tt(rows_sb[0:1, 1028:1540], rows_sb[0:1, 0:512],
                          rows_sb[0:1, 0:512], ALU.mult)
                    ve.tt(rows_sb[0:1, 514:1026], rows_sb[0:1, 514:1026],
                          rows_sb[0:1, 1028:1540], ALU.subtract)
                    s_vv.inc(ve.ts(rows_sb[0:1, 514:1026],
                                   rows_sb[0:1, 514:1026], 1e-5, None, ALU.add))
                    M[("var2", j)] = s_vv.n
                    v.wait_ge(s_ss.h, mg("sqrt2", j))
                    ve.recip(rows_sb[0:1, 1028:1540], rows_sb[0:1, 1542:2054])
                    ve.tt(rows_sb[0:1, 1542:2054], rows_sb[0:1, 0:512],
                          rows_sb[0:1, 1028:1540], ALU.mult)
                    s_vv.inc(ve.ts(rows_sb[0:1, 1542:2054],
                                   rows_sb[0:1, 1542:2054], -1.0, None, ALU.mult))
                    M[("arow2", j)] = s_vv.n

                def bv_emcp(j):
                    v.wait_ge(s_mm.h, mg("emt", j))
                    if j >= 1:
                        v.wait_ge(s_gf.h, mg("m2gf", j - 1))
                    ve.cp(emtmp_sb[0:8, 0:P], ps_s2[0:8, 256:384])
                    s_vv.inc(ve.dinc())
                    M[("emcp", j)] = s_vv.n
                    M[("emrdy", j)] = s_vv.n

                def bv_b2(j):
                    v.wait_ge(s_mm.h, mg("abc2", j))
                    s_vv.inc(ve.cp(bc2_sb[:, 0:TS], ps_b2[:, 512:1024]))
                    M[("acp2", j)] = s_vv.n
                    v.wait_ge(s_mm.h, mg("bbc2", j))
                    if j >= 1:
                        v.wait_ge(s_sth.h, mg("m2st_h", j - 1))
                    ve.cp(bc2_sb[:, TS:2 * TS], ps_b2[:, 512:1024])
                    for k in range(KD):
                        ve.tt(mcs1_sb[:], x2_sb[:, k * TS:(k + 1) * TS],
                              bc2_sb[:, 0:TS], ALU.mult)
                        ve.tt(mcs2_sb[:], mcs1_sb[:], bc2_sb[:, TS:2 * TS],
                              ALU.add)
                        ve.ts(h2t_sb[:, k * TS:(k + 1) * TS], mcs2_sb[:],
                              modv_sb[:, 32 + k:32 + k + 1],
                              modv_sb[:, 24 + k:24 + k + 1], ALU.mult, ALU.add)
                    s_vv.inc(ve.dinc())
                    M[("h2", j)] = s_vv.n

                def bv_argmax(j):
                    for c in range(CPT):
                        v.wait_ge(s_mm.h, mg("rtt", j, c))
                        ve.cp(rtc_sb[:], ps_tb[:])
                        ve.maxv(mx_sb[:], rtc_sb[:])
                        ve.maxidx(mi8_sb[:], mx_sb[:], rtc_sb[:])
                        ve.cp(em2all_sb[:, 2 * c:2 * c + 1], mi8_sb[:, 0:1])
                        s_vv.inc(ve.cp(em2all_sb[:, 2 * c + 1:2 * c + 2],
                                       mx_sb[:, 0:1]))
                        M[("em2", j, c)] = s_vv.n

                fv_rows(0)
                fv_happ(0)
                for j in range(NT):
                    bv_a(j)
                    if j + 1 < NT:
                        fv_rows(j + 1)
                        fv_happ(j + 1)
                    bv_b1(j)
                    if j >= 1:
                        bv_emcp(j - 1)
                    bv_b2(j)
                    bv_argmax(j)
                bv_emcp(NT - 1)

            @block.sync
            def _(sy):
                for j in range(NT):
                    sy.wait_ge(s_vv.h, mg("x2", j))
                    s_stx.inc(sy.dma_start(
                        r3(out_d)[:, :, j * TS:(j + 1) * TS],
                        x2_sb[:].rearrange("p (k n) -> p k n", k=KD)), 16)
                    M[("m2st_x", j)] = s_stx.n
                    sy.wait_ge(s_vv.h, mg("h2", j))
                    s_sth.inc(sy.dma_start(
                        r3(h2_d)[:, :, j * TS:(j + 1) * TS],
                        h2t_sb[:].rearrange("p (k n) -> p k n", k=KD)), 16)
                    M[("m2st_h", j)] = s_sth.n
                M[("m2stx",)] = s_stx.n
                M[("m2sth",)] = s_sth.n

        x_es.close()

        if upto == "m2":
            if dbg:
                _dbg_emit(block, dbg_rows, dbg_cols, expcl_sb, ball_sb, ysi_sb,
                          ysr_sb, None, s_vv, s_ss, s_gf)
            _finish(block, s_st, s_stx, s_sth)
            return nc, dram

        scan_es.close()

        with contextlib.ExitStack() as es:
            L = es.enter_context
            acc_sb = L(nc.sbuf_tensor("acc_sb", [P, KD * SHW], BF16))
            w1_sb = L(nc.sbuf_tensor("w1_sb", [P, KD * FF], BF16))
            w2_sb = L(nc.sbuf_tensor("w2_sb", [P, KF * D], BF16))
            t1_sb = L(nc.sbuf_tensor("t1_sb", [P, KF * TS], BF16))
            h2b_sb = L(nc.sbuf_tensor("h2b_sb", [P, KD * TS], BF16))
            emap_sb = L(nc.sbuf_tensor("emap_sb", [P, SHW], BF16))
            msk_sb = L(nc.sbuf_tensor("msk_sb", [P, TS], U32))
            otl_sb = L(nc.sbuf_tensor("otl_sb", [P, 2 * TS], F32))
            emrow_sb = L(nc.sbuf_tensor("emrow_sb", [1, 2 * TS], BF16))
            mprow_sb = L(nc.sbuf_tensor("mprow_sb", [1, TS], BF16))
            ps_all = L(nc.psum_tensor("ps_all", [P, 4096], F32))
            x2r_sb = h2b_sb

            eorders = [list(range(NEXP)) if sh % 2 == 0
                       else list(range(NEXP - 1, -1, -1)) for sh in range(NSH)]

            @block.gpsimd
            def _(g):
                g.wait_ge(s_stx.h, mg("m2stx"))
                g.wait_ge(s_sth.h, mg("m2sth"))
                g.wait_ge(s_gf.h, mg("m2gf"))
                prev_e, tile_n, inst_i = None, 0, 0
                for sh in range(NSH):
                    if sh >= 1:
                        g.wait_ge(s_fer.h, sh)
                    s_ler.inc(g.dma_start(
                        emrow_sb[0:1, 0:SHW],
                        emr_d[0:1, sh * SHW:(sh + 1) * SHW]), 16)
                    for ei, e in enumerate(eorders[sh]):
                        if e != prev_e:
                            if inst_i > 0:
                                g.wait_ge(s_fw1.h, inst_i)
                            s_lw1.inc(g.dma_start(
                                w1_sb[:].rearrange("p (k n) -> p k n", k=KD)
                                [:, :, 0:FF // 2],
                                r3(w1[e])[:, :, 0:FF // 2]), 16)
                            if inst_i > 0:
                                g.wait_ge(s_fw1b.h, inst_i)
                            s_lw1b.inc(g.dma_start(
                                w1_sb[:].rearrange("p (k n) -> p k n", k=KD)
                                [:, :, FF // 2:FF],
                                r3(w1[e])[:, :, FF // 2:FF]), 16)
                            if inst_i > 0:
                                g.wait_ge(s_fw2.h, inst_i)
                            s_lw2.inc(g.dma_start(
                                w2_sb[:].rearrange("p (k n) -> p k n", k=KF)
                                [:, :, 0:D // 2],
                                r3(w2[e])[:, :, 0:D // 2]), 16)
                            if inst_i > 0:
                                g.wait_ge(s_fw2b.h, inst_i)
                            s_lw2b.inc(g.dma_start(
                                w2_sb[:].rearrange("p (k n) -> p k n", k=KF)
                                [:, :, D // 2:D],
                                r3(w2[e])[:, :, D // 2:D]), 16)
                            prev_e = e
                        for q in range(QSH):
                            tile_n += 1
                        inst_i += 1
                    for q in range(QSH):
                        tq = sh * QSH + q
                        if tq >= 1:
                            g.wait_ge(s_fo.h, tq)
                            g.wait_ge(s_fmr.h, tq)
                        g.wait_ge(s_fh2.h, tile_n)
                        s_lmr.inc(g.dma_start(
                            mprow_sb[:],
                            emr_d[1:2, sh * SHW + q * TS:sh * SHW + (q + 1) * TS]),
                            16)
                        s_loa.inc(g.dma_start(
                            x2r_sb[:].rearrange("p (k n) -> p k n", k=KD),
                            r3(out_d)[:, :, sh * SHW + q * TS:
                                      sh * SHW + (q + 1) * TS]), 16)

            @block.tensor
            def _(t):
                tile_n = 0
                t.wait_ge(s_gf.h, mg("m2gf"))
                for sh in range(NSH):
                    if tile_n >= 1:
                        t.wait_ge(s_vv.h, mg("predq", 4 * tile_n - 1))
                    t.wait_ge(s_ler.h, 16 * (sh + 1))
                    for q in range(QSH):
                        s_mm.inc(t.matmul(
                            ps_all[:, q * 512:(q + 1) * 512], onesb_sb[0:1, :],
                            emrow_sb[0:1, q * TS:(q + 1) * TS],
                            start=True, stop=True))
                        M[("embc", sh, q)] = s_mm.n
                    s_fer.inc(t.drain())
                    for ei, e in enumerate(eorders[sh]):
                        nw = _w1count(eorders, sh, ei)
                        t.wait_ge(s_lw1.h, 16 * nw)
                        for q in range(QSH):
                            a = (2 * tile_n) % 8
                            t.wait_ge(s_lh2.h, 16 * (tile_n + 1))
                            if ei == 0:
                                t.wait_ge(s_vv.h, mg("emcp3", sh))
                            if tile_n >= 1:
                                t.wait_ge(s_vv.h, mg("predq", 4 * (tile_n - 1) + 1))
                            for m in range(KF):
                                zb = a + (m % 2)
                                if m == KF // 2:
                                    t.wait_ge(s_lw1b.h, 16 * nw)
                                if m >= 2:
                                    t.wait_ge(s_ss.h, mg("gelu", tile_n, m - 2))
                                for k in range(KD):
                                    inst = t.matmul(
                                        ps_all[:, zb * 512:(zb + 1) * 512],
                                        w1_sb[:, k * FF + m * P:k * FF + (m + 1) * P],
                                        h2b_sb[:, k * TS:(k + 1) * TS],
                                        start=(k == 0), stop=(k == KD - 1))
                                s_mm.inc(inst)
                                M[("z", tile_n, m)] = s_mm.n
                                if q == QSH - 1 and m == KF // 2 - 1:
                                    s_fw1.inc(t.drain())
                            s_fh2.inc(t.drain())
                            if q == QSH - 1:
                                s_fw1b.inc(t.drain())
                            t.wait_ge(s_ss.h, mg("gelu", tile_n, KF - 1))
                            t.wait_ge(s_lw2.h, 16 * nw)
                            for q4 in range(4):
                                if q4 == 2:
                                    t.wait_ge(s_lw2b.h, 16 * nw)
                                if q == QSH - 1 and q4 == 2:
                                    s_fw2.inc(t.drain())
                                if tile_n >= 1 and q4 <= 2:
                                    t.wait_ge(s_vv.h,
                                              mg("predq", 4 * (tile_n - 1) + q4 + 1))
                                for dd in range(2):
                                    d = 2 * q4 + dd
                                    yb = (a + 2 + 2 * q4 + dd) % 8
                                    for k in range(KF):
                                        inst = t.matmul(
                                            ps_all[:, yb * 512:(yb + 1) * 512],
                                            w2_sb[:, k * D + d * P:k * D + (d + 1) * P],
                                            t1_sb[:, k * TS:(k + 1) * TS],
                                            start=(k == 0), stop=(k == KF - 1))
                                s_mm.inc(inst)
                                M[("y", tile_n, q4)] = s_mm.n
                            if q == QSH - 1:
                                s_fw2b.inc(t.drain())
                            tile_n += 1
                    t.wait_ge(s_vv.h, mg("predq", 4 * tile_n - 1))
                    for q in range(QSH):
                        tq = sh * QSH + q
                        t.wait_ge(s_lmr.h, 16 * (tq + 1))
                        s_mm.inc(t.matmul(
                            ps_all[:, 0:512], onesb_sb[0:1, :],
                            mprow_sb[0:1, 0:TS],
                            start=True, stop=True))
                        M[("mpb", sh, q)] = s_mm.n
                        s_fmr.inc(t.drain())
                        t.wait_ge(s_vv.h, mg("fin", sh, q))

            @block.scalar
            def _(s):
                sc = Eng(s)
                tile_n = 0
                for sh in range(NSH):
                    for ei in range(NEXP):
                        for q in range(QSH):
                            a = (2 * tile_n) % 8
                            for m in range(KF):
                                zb = a + (m % 2)
                                s.wait_ge(s_mm.h, mg("z", tile_n, m))
                                s_ss.inc(sc.act(
                                    t1_sb[:, m * TS:(m + 1) * TS],
                                    ps_all[:, zb * 512:(zb + 1) * 512], AF.Gelu))
                                M[("gelu", tile_n, m)] = s_ss.n
                            tile_n += 1

            @block.vector
            def _(v):
                ve = Eng(v)
                tile_n = 0
                for sh in range(NSH):
                    for q in range(QSH):
                        v.wait_ge(s_mm.h, mg("embc", sh, q))
                        ve.cp(emap_sb[:, q * TS:(q + 1) * TS],
                              ps_all[:, q * 512:(q + 1) * 512])
                    s_vv.inc(ve.dinc())
                    M[("emcp3", sh)] = s_vv.n
                    for ei, e in enumerate(eorders[sh]):
                        for q in range(QSH):
                            a = (2 * tile_n) % 8
                            ve.ts(msk_sb[:], emap_sb[:, q * TS:(q + 1) * TS],
                                  float(e), None, ALU.is_equal)
                            for q4 in range(4):
                                v.wait_ge(s_mm.h, mg("y", tile_n, q4))
                                for dd in range(2):
                                    d = 2 * q4 + dd
                                    yb = (a + 2 + 2 * q4 + dd) % 8
                                    ve.pred(acc_sb[:, d * SHW + q * TS:
                                                   d * SHW + (q + 1) * TS],
                                            msk_sb[:],
                                            ps_all[:, yb * 512:(yb + 1) * 512])
                                s_vv.inc(ve.dinc())
                                M[("predq", 4 * tile_n + q4)] = s_vv.n
                            tile_n += 1
                    for q in range(QSH):
                        tq = sh * QSH + q
                        v.wait_ge(s_loa.h, 16 * (tq + 1))
                        v.wait_ge(s_mm.h, mg("mpb", sh, q))
                        for d in range(KD):
                            sct = s_sta if d % 2 == 0 else s_stb
                            if d >= 2:
                                v.wait_ge(sct.h, mg("fst", tq, d - 2))
                            elif tq >= 1:
                                v.wait_ge(sct.h, mg("fst", tq - 1, KD - 2 + d))
                            sl = otl_sb[:, (d % 2) * TS:(d % 2) * TS + TS]
                            ve.stt(sl,
                                   acc_sb[:, d * SHW + q * TS:
                                          d * SHW + (q + 1) * TS],
                                   modv_sb[:, 40 + d:40 + d + 1],
                                   ps_all[:, 0:512], ALU.mult, ALU.mult)
                            ve.tt(sl, sl, x2r_sb[:, d * TS:(d + 1) * TS],
                                  ALU.add)
                            s_vv.inc(ve.dinc())
                            M[("fin", sh, q, d)] = s_vv.n
                        s_fo.inc(ve.dinc())
                        M[("fin", sh, q)] = s_vv.n

            @block.sync
            def _(sy):
                sy.wait_ge(s_stx.h, mg("m2stx"))
                sy.wait_ge(s_sth.h, mg("m2sth"))
                sy.wait_ge(s_gf.h, mg("m2gf"))
                tile_n = 0
                for sh in range(NSH):
                    for ei in range(NEXP):
                        for q in range(QSH):
                            if tile_n >= 1:
                                sy.wait_ge(s_fh2.h, tile_n)
                            if ei == 0 and q == 0 and sh >= 1:
                                sy.wait_ge(s_fo.h, sh * QSH)
                            s_lh2.inc(sy.dma_start(
                                h2b_sb[:].rearrange("p (k n) -> p k n", k=KD),
                                r3(h2_d)[:, :, sh * SHW + q * TS:
                                         sh * SHW + (q + 1) * TS]), 16)
                            tile_n += 1
                    for q in range(QSH):
                        tq = sh * QSH + q
                        for d in range(KD):
                            sct = s_sta if d % 2 == 0 else s_stb
                            sy.wait_ge(s_vv.h, mg("fin", sh, q, d))
                            sct.inc(sy.dma_start(
                                r3(out_d)[:, d:d + 1,
                                          sh * SHW + q * TS:sh * SHW + (q + 1) * TS],
                                otl_sb[:, (d % 2) * TS:(d % 2) * TS + TS]
                                .rearrange("p (k n) -> p k n", k=1)), 16)
                            M[("fst", tq, d)] = sct.n



        _finish(block, s_st, s_stx, s_sth, s_sta, s_stb)
    return nc, dram


def _dbg_emit(block, dbg_rows, dbg_cols, expcl_sb, ball_sb, ysi_sb,
              ysr_sb, mpr_unused, s_vv, s_ss, s_st, zero_rows=False):
    @block.gpsimd
    def _(sy):
        sy.wait_ge(s_vv.h, s_vv.n)
        sy.wait_ge(s_ss.h, s_ss.n)
        sy.wait_ge(s_st.h, s_st.n)
        if zero_rows:
            sy.memset(ysr_sb[:], 0.0)
            sy.drain()
        o = 0
        for src in (expcl_sb, ball_sb, ysr_sb, ysr_sb):
            n = src.shape[1]
            s_st.inc(sy.dma_start(dbg_rows[0:1, o:o + n], src[:]), 16)
            o += n
        s_st.inc(sy.dma_start(dbg_cols[:, 0:ysi_sb.shape[1]], ysi_sb[:]), 16)
        sy.wait_ge(s_st.h, s_st.n)


def _finish(block, s_st, *extra):
    @block.sync
    def _(sy):
        sy.wait_ge(s_st.h, s_st.n)
        for s2 in extra:
            sy.wait_ge(s2.h, s2.n)


def build_nc(upto="final", dbg=False):
    marks = {}
    build_program(marks, upto=upto, dbg=dbg)
    return build_program(marks, upto=upto, dbg=dbg)



def stage_inputs(inputs, n_cores=8):
    f = np.float32
    x = np.asarray(inputs["L0_attn_x"], f)
    B, T, Dm = x.shape
    in_w = np.asarray(inputs["L0_attn_in_w"], f)
    in_b = np.asarray(inputs["L0_attn_in_b"], f)
    conv_w = np.asarray(inputs["L0_attn_conv_w"], f)
    conv_b = np.asarray(inputs["L0_attn_conv_b"], f)
    dt_w = np.asarray(inputs["L0_attn_dt_w"], f)
    dt_b = np.asarray(inputs["L0_attn_dt_b"], f)
    B_w = np.asarray(inputs["L0_attn_B_w"], f)
    B_b = np.asarray(inputs["L0_attn_B_b"], f)
    C_w = np.asarray(inputs["L0_attn_C_w"], f)
    C_b = np.asarray(inputs["L0_attn_C_b"], f)
    D_w = np.asarray(inputs["L0_attn_D_w"], f)
    D_b = np.asarray(inputs["L0_attn_D_b"], f)
    out_w = np.asarray(inputs["L0_attn_out_w"], f)
    out_b = np.asarray(inputs["L0_attn_out_b"], f)
    Wr = np.asarray(inputs["L0_ffn_Wr"], f)
    br = np.asarray(inputs["L0_ffn_br"], f)
    W1 = np.asarray(inputs["L0_ffn_W1"], f)
    W2 = np.asarray(inputs["L0_ffn_W2"], f)
    cmat = np.asarray(inputs["L0_c"], f)
    wmod = np.asarray(inputs["L0_wmod"], f)

    def chunkmaj(vec, nch):
        return np.ascontiguousarray(vec.reshape(nch, P).T)

    shared = dict(
        inwm=np.ascontiguousarray(in_w[:E].T),
        inwg=np.ascontiguousarray(in_w[E:].T),
        dtw=np.ascontiguousarray(dt_w.T),
        bcdw=np.ascontiguousarray(np.concatenate([B_w, C_w, D_w], 0).T),
        outw=np.ascontiguousarray(out_w.T),
        wr=np.ascontiguousarray(Wr),
        w1=np.ascontiguousarray(W1),
        w2=np.ascontiguousarray(W2),
        wmod=np.ascontiguousarray(wmod),
        inbm=chunkmaj(in_b[:E], KE),
        inbg=chunkmaj(in_b[E:], KE),
        convw=np.ascontiguousarray(np.concatenate(
            [chunkmaj(conv_w[:, 0, k], KE) for k in range(3)], 1)),
        convb=chunkmaj(conv_b, KE),
        dtb=chunkmaj(dt_b, KE),
        bcdb=np.ascontiguousarray(np.stack([B_b, C_b, D_b], 1)),
        outb=chunkmaj(out_b, KD),
        brv=np.ascontiguousarray(br.reshape(8, 1)),
        utc=np.ascontiguousarray(np.triu(np.ones((P, P), f))),
        onesc=np.ones((P, P), f),
        maskc=np.ascontiguousarray(
            np.where(np.arange(P)[None, :] >= np.arange(P)[:, None],
                     0.0, NEG).astype(f)),
        identc=np.eye(P, dtype=f),
    )

    in_maps = []
    for c in range(n_cores):
        b, odd = c // 2, c % 2
        xb = x[b].T
        halves = []
        hmask = np.zeros((P, 4), f)
        for hi, h in enumerate((odd, 1 - odd)):
            t0 = h * TPH
            blk = np.zeros((Dm, TPH + 2), f)
            blk[:, 1:TPH + 1] = xb[:, t0:t0 + TPH]
            if t0 - 1 >= 0:
                blk[:, 0] = xb[:, t0 - 1]
                hmask[:, hi * 2] = 1.0
            if t0 + TPH < T:
                blk[:, TPH + 1] = xb[:, t0 + TPH]
                hmask[:, hi * 2 + 1] = 1.0
            halves.append(blk)
        m = dict(shared)
        m["x"] = np.ascontiguousarray(np.concatenate(halves, 1))
        m["cvec"] = np.ascontiguousarray(cmat[b].reshape(KD, P).T)
        m["flagv"] = np.array([[1.0 if odd else 0.0]], f)
        m["hmaskv"] = hmask
        in_maps.append(m)
    return in_maps


def kernel(**inputs):
    nc, _ = build_nc(upto="final", dbg=False)
    in_maps = stage_inputs(inputs)
    res = run_bass_kernel_spmd(nc, in_maps, core_ids=list(range(8)))
    x = np.asarray(inputs["L0_attn_x"])
    B, T, Dm = x.shape
    out = np.empty((B, T, Dm), np.float32)
    for c in range(8):
        b, odd = c // 2, c % 2
        t0 = odd * TPH
        out[b, t0:t0 + TPH, :] = np.asarray(res.results[c]["out"]).T
    return out


# version 13
# speedup vs baseline: 1.0241x; 1.0027x over previous
import contextlib
import os
import numpy as np
import ml_dtypes
import concourse.bass as bass
import concourse.mybir as mybir
from concourse.bass_utils import run_bass_kernel_spmd

F32 = mybir.dt.float32
BF16 = mybir.dt.bfloat16
U32 = mybir.dt.uint32
AF = mybir.ActivationFunctionType
ALU = mybir.AluOpType
AX = mybir.AxisListType

P = 128
D, E, S, FF, NEXP = 1024, 2048, 16, 4096, 8
KD, KE, KF = D // P, E // P, FF // P
TS = 512
NEG = -1e30
TPH = 2048
M1LVL = int(os.environ.get("M1LVL", "9"))
M1SKIP = set(os.environ.get("M1SKIP", "").split(","))


def set_tph(v):
    global TPH
    TPH = v


class Ctr:
    def __init__(self, h):
        self.h = h
        self.n = 0

    def inc(self, inst, k=1):
        inst.then_inc(self.h, k)
        self.n += k
        return self.n


class Eng:

    def __init__(self, e):
        self.e = e
        self.w = set()
        self.r = set()

    def _names(self, aps):
        out = set()
        for ap in aps:
            t = getattr(ap, "tensor", None)
            if t is not None:
                out.add(t.name)
        return out

    def _pre(self, outs, ins):
        on, inn = self._names(outs), self._names(ins)
        if (inn & self.w) or (on & self.w) or (on & self.r):
            self.e.drain()
            self.w.clear()
            self.r.clear()

    def _post(self, outs, ins):
        self.w |= self._names(outs)
        self.r |= self._names(ins)

    def barrier(self):
        self.e.drain()
        self.w.clear()
        self.r.clear()

    def dinc(self):
        inst = self.e.drain()
        self.w.clear()
        self.r.clear()
        return inst

    def tt(self, out, a, b, op):
        self._pre([out], [a, b])
        i = self.e.tensor_tensor(out, a, b, op)
        self._post([out], [a, b])
        return i

    def ts(self, out, in0, s1, s2, op0, op1=None):
        self._pre([out], [in0, s1, s2])
        if op1 is None:
            i = self.e.tensor_scalar(out, in0, s1, s2, op0)
        else:
            i = self.e.tensor_scalar(out, in0, s1, s2, op0, op1)
        self._post([out], [in0])
        return i

    def stt(self, out, in0, s, in1, op0, op1):
        self._pre([out], [in0, s, in1])
        i = self.e.scalar_tensor_tensor(out, in0, s, in1, op0, op1)
        self._post([out], [in0, in1])
        return i

    def cp(self, out, in_):
        self._pre([out], [in_])
        i = self.e.tensor_copy(out, in_)
        self._post([out], [in_])
        return i

    def mset(self, out, v):
        self._pre([out], [])
        i = self.e.memset(out, v)
        self._post([out], [])
        return i

    def rsum(self, out, in_):
        self._pre([out], [in_])
        i = self.e.reduce_sum(out, in_, axis=AX.X)
        self._post([out], [in_])
        return i

    def maxv(self, out, in_):
        self._pre([out], [in_])
        i = self.e.max(out=out, in_=in_)
        self._post([out], [in_])
        return i

    def maxidx(self, out, mx, vals):
        self._pre([out], [mx, vals])
        i = self.e.max_index(out, mx, vals)
        self._post([out], [mx, vals])
        return i

    def pred(self, out, mask, data):
        self._pre([out], [mask, data])
        i = self.e.copy_predicated(out, mask, data)
        self._post([out], [mask, data])
        return i

    def recip(self, out, in_):
        self._pre([out], [in_])
        i = self.e.reciprocal(out, in_)
        self._post([out], [in_])
        return i

    def act(self, out, in_, func, bias=0.0, scale=1.0):
        ins = [in_] + ([bias] if hasattr(bias, "tensor") else [])
        self._pre([out], ins)
        i = self.e.activation(out, in_, func, bias=bias, scale=scale)
        self._post([out], ins)
        return i

    def sign(self, out, in_):
        self._pre([out], [in_])
        i = self.e.sign(out, in_)
        self._post([out], [in_])
        return i


def _w1count(eorders, sh, ei):
    n, prev = 0, None
    for s2 in range(len(eorders)):
        for i, e in enumerate(eorders[s2]):
            if e != prev:
                n += 1
                prev = e
            if s2 == sh and i == ei:
                return n
    return n


def build_program(marks, upto="final", dbg=False):
    nc = bass.Bass("TRN2", target_bir_lowering=False)
    NT = TPH // TS
    NCH = TPH // P
    CPT = TS // P
    XW = TPH + 2
    W = TS + 2
    NSH = max(1, TPH // 1024)
    SHW = TPH // NSH
    QSH = SHW // TS
    W = TS + 2
    M = marks

    def mg(*key):
        return M.get(key, 0)

    dram = {}

    def din(name, shape):
        dram[name] = nc.dram_tensor(name, list(shape), F32,
                                    kind="ExternalInput").ap()
        return dram[name]

    x = din("x", (D, 2 * XW))
    inwm = din("inwm", (D, E))
    inwg = din("inwg", (D, E))
    dtw = din("dtw", (E, E))
    bcdw = din("bcdw", (E, 48))
    outw = din("outw", (E, D))
    wr = din("wr", (D, 8))
    w1 = din("w1", (NEXP, D, FF))
    w2 = din("w2", (NEXP, FF, D))
    wmod = din("wmod", (D, 6 * D))
    cvec = din("cvec", (P, KD))
    inbm = din("inbm", (P, KE))
    inbg = din("inbg", (P, KE))
    convw = din("convw", (P, 3 * KE))
    convb = din("convb", (P, KE))
    dtb = din("dtb", (P, KE))
    bcdb = din("bcdb", (S, 3))
    outb = din("outb", (P, KD))
    brv = din("brv", (8, 1))
    flagv = din("flagv", (1, 1))
    hmaskv = din("hmaskv", (P, 4))
    utc = din("utc", (P, P))
    onesc = din("onesc", (P, P))
    maskc = din("maskc", (P, P))
    identc = din("identc", (P, P))

    out_d = nc.dram_tensor("out", [D, TPH], F32, kind="ExternalOutput").ap()
    emr_d = nc.dram_tensor("emrd", [2, TPH], F32, kind="Internal").ap()
    h2_d = nc.dram_tensor("h2d", [D, TPH], BF16,
                          kind="ExternalOutput" if dbg else "Internal").ap()
    if dbg:
        dbg_rows = nc.dram_tensor("dbg_rows", [1, 8 * TPH], F32,
                                  kind="ExternalOutput").ap()
        dbg_cols = nc.dram_tensor("dbg_cols", [P, 4 * NCH], F32,
                                  kind="ExternalOutput").ap()

    def r3(ap, k=KD):
        return ap.rearrange("(k p) n -> p k n", p=P)

    with contextlib.ExitStack() as es_glob:
        G = es_glob.enter_context
        s_set = Ctr(G(nc.semaphore("s_set")))
        s_wmo = Ctr(G(nc.semaphore("s_wmo")))
        s_lxa = Ctr(G(nc.semaphore("s_lxa")))
        s_lxb = Ctr(G(nc.semaphore("s_lxb")))
        s_fx = Ctr(G(nc.semaphore("s_fx")))
        s_mm = Ctr(G(nc.semaphore("s_mm")))
        s_vv = Ctr(G(nc.semaphore("s_vv")))
        s_ss = Ctr(G(nc.semaphore("s_ss")))
        s_st = Ctr(G(nc.semaphore("s_st")))
        s_stx = Ctr(G(nc.semaphore("s_stx")))
        s_sth = Ctr(G(nc.semaphore("s_sth")))
        s_gf = Ctr(G(nc.semaphore("s_gf")))
        s_m2w = Ctr(G(nc.semaphore("s_m2w")))
        s_m1a = Ctr(G(nc.semaphore("s_m1a")))
        s_lh2 = Ctr(G(nc.semaphore("s_lh2")))
        s_fh2 = Ctr(G(nc.semaphore("s_fh2")))
        s_lw1 = Ctr(G(nc.semaphore("s_lw1")))
        s_fw1 = Ctr(G(nc.semaphore("s_fw1")))
        s_lw2 = Ctr(G(nc.semaphore("s_lw2")))
        s_fw2 = Ctr(G(nc.semaphore("s_fw2")))
        s_lw1b = Ctr(G(nc.semaphore("s_lw1b")))
        s_fw1b = Ctr(G(nc.semaphore("s_fw1b")))
        s_lw2b = Ctr(G(nc.semaphore("s_lw2b")))
        s_fw2b = Ctr(G(nc.semaphore("s_fw2b")))
        s_loa = Ctr(G(nc.semaphore("s_loa")))
        s_fo = Ctr(G(nc.semaphore("s_fo")))
        s_ler = Ctr(G(nc.semaphore("s_ler")))
        s_fer = Ctr(G(nc.semaphore("s_fer")))
        s_lmr = Ctr(G(nc.semaphore("s_lmr")))
        s_fmr = Ctr(G(nc.semaphore("s_fmr")))
        s_sta = Ctr(G(nc.semaphore("s_sta")))
        s_stb = Ctr(G(nc.semaphore("s_stb")))

        onesb_sb = G(nc.sbuf_tensor("onesb_sb", [P, P], BF16))
        bias_sb = G(nc.sbuf_tensor("bias_sb", [P, 8 * KE], F32))
        hm_sb = G(nc.sbuf_tensor("hm_sb", [P, 4], F32))
        bcdb_sb = G(nc.sbuf_tensor("bcdb_sb", [S, 3], F32))
        br_sb = G(nc.sbuf_tensor("br_sb", [8, 1], F32))
        flag_sb = G(nc.sbuf_tensor("flag_sb", [1, 1], F32))
        modv_sb = G(nc.sbuf_tensor("modv_sb", [P, 64], F32))
        scan_es = G(contextlib.ExitStack())
        S_ = scan_es.enter_context
        ut_sb = S_(nc.sbuf_tensor("ut_sb", [P, P], F32))
        ones_sb = S_(nc.sbuf_tensor("ones_sb", [P, P], F32))
        mk_sb = S_(nc.sbuf_tensor("mk_sb", [P, P], F32))
        idf_sb = S_(nc.sbuf_tensor("idf_sb", [P, P], F32))
        idb_sb = S_(nc.sbuf_tensor("idb_sb", [P, P], BF16))
        expcl_sb = S_(nc.sbuf_tensor("expcl_sb", [1, 2 * TPH], BF16))
        aall_sb = S_(nc.sbuf_tensor("aall_sb", [1, 2 * NCH], F32))
        ball_sb = S_(nc.sbuf_tensor("ball_sb", [1, 2 * NCH * S], F32))
        ysi_sb = S_(nc.sbuf_tensor("ysi_sb", [P, NCH], F32))
        call_sb = S_(nc.sbuf_tensor("call_sb", [P, NCH * S], BF16))
        ysr_sb = S_(nc.sbuf_tensor("ysr_sb", [1, TPH], BF16))
        srow_sb = S_(nc.sbuf_tensor("srow_sb", [1, 4 * S], F32))
        srowb_sb = S_(nc.sbuf_tensor("srowb_sb", [1, 2 * S], BF16))
        rows_sb = S_(nc.sbuf_tensor("rows_sb", [1, 2056], F32))

        block = G(nc.Block())
        NTt = TPH // TS
        x_es = es_glob.enter_context(contextlib.ExitStack())
        x_buf = x_es.enter_context(
            nc.sbuf_tensor("x_buf", [P, 2 * KD * (TS + 2)], BF16))

        with contextlib.ExitStack() as es:
            L = es.enter_context
            wmod_sb = L(nc.sbuf_tensor("wmod_sb", [P, KD * 3 * D], F32))
            silc_sb = L(nc.sbuf_tensor("silc_sb", [P, KD], F32))
            mod_ps = L(nc.psum_tensor("mod_ps", [P, 64], F32))

            @block.gpsimd
            def _(g):
                for src, dst in ((utc, ut_sb), (onesc, ones_sb), (maskc, mk_sb),
                                 (identc, idf_sb), (onesc, onesb_sb),
                                 (identc, idb_sb),
                                 (hmaskv, hm_sb), (bcdb, bcdb_sb),
                                 (brv, br_sb), (flagv, flag_sb)):
                    s_set.inc(g.dma_start(dst[:], src), 16)
                for off, src in ((0, inbm), (16, inbg), (80, convb), (96, dtb)):
                    s_set.inc(g.dma_start(bias_sb[:, off:off + KE], src), 16)
                s_set.inc(g.dma_start(bias_sb[:, 32:80], convw), 16)
                s_set.inc(g.dma_start(bias_sb[:, 112:120], outb), 16)
                s_set.inc(g.dma_start(bias_sb[:, 120:128], cvec), 16)
                for i0 in range(2):
                    p0, j0 = (1, i0) if i0 < NTt else (0, i0 - NTt)
                    base0 = p0 * XW + j0 * TS
                    sem0 = s_lxa if i0 % 2 == 0 else s_lxb
                    sem0.inc(g.dma_start(
                        x_buf[:, (i0 % 2) * KD * W:((i0 % 2) + 1) * KD * W]
                        .rearrange("p (k n) -> p k n", k=KD),
                        r3(x)[:, :, base0:base0 + W]), 16)
                for hf in range(2):
                    s_wmo.inc(g.dma_start(
                        wmod_sb[:].rearrange("p (k n) -> p k n", k=KD),
                        r3(wmod)[:, :, hf * 3 * D:(hf + 1) * 3 * D]), 16)
                    g.wait_ge(s_mm.h, mg("modh", hf))

            @block.scalar
            def _(s):
                sc = Eng(s)
                s.wait_ge(s_set.h, s_set.n)
                s_ss.inc(sc.act(silc_sb[:], bias_sb[:, 120:128], AF.Silu))
                M[("silc",)] = s_ss.n

            @block.tensor
            def _(t):
                t.wait_ge(s_set.h, s_set.n)
                t.wait_ge(s_ss.h, mg("silc"))
                for hf in range(2):
                    t.wait_ge(s_wmo.h, 16 * (hf + 1))
                    for j in range(24):
                        jj = hf * 24 + j
                        for k in range(KD):
                            inst = t.matmul(
                                mod_ps[:, jj:jj + 1],
                                wmod_sb[:, k * 3 * D + j * P:k * 3 * D + (j + 1) * P],
                                silc_sb[:, k:k + 1],
                                start=(hf == 0 and j == 0 and k == 0),
                                stop=(hf == 1 and j == 23 and k == KD - 1))
                    s_mm.inc(inst)
                    M[("modh", hf)] = s_mm.n

            @block.scalar
            def _(s):
                sc = Eng(s)
                s.wait_ge(s_mm.h, mg("modh", 1))
                sc.sign(modv_sb[:, 48:56], mod_ps[:, 16:24])
                s_ss.inc(sc.sign(modv_sb[:, 56:64], mod_ps[:, 40:48]))
                M[("gsign",)] = s_ss.n

            @block.vector
            def _(v):
                ve = Eng(v)
                v.wait_ge(s_mm.h, mg("modh", 1))
                ve.cp(modv_sb[:, 0:8], mod_ps[:, 0:8])
                ve.ts(modv_sb[:, 8:16], mod_ps[:, 8:16], 1.0, None, ALU.add)
                ve.cp(modv_sb[:, 24:32], mod_ps[:, 24:32])
                ve.ts(modv_sb[:, 32:40], mod_ps[:, 32:40], 1.0, None, ALU.add)
                v.wait_ge(s_ss.h, mg("gsign"))
                for src, dst, sg in ((16, 16, 48), (40, 40, 56)):
                    ve.ts(modv_sb[:, dst:dst + 8], mod_ps[:, src:src + 8],
                          -1.0, None, ALU.mult)
                    ve.tt(modv_sb[:, dst:dst + 8], modv_sb[:, dst:dst + 8],
                          mod_ps[:, src:src + 8], ALU.max)
                    ve.ts(modv_sb[:, dst:dst + 8], modv_sb[:, dst:dst + 8],
                          0.25, None, ALU.max)
                    ve.tt(modv_sb[:, dst:dst + 8], modv_sb[:, dst:dst + 8],
                          modv_sb[:, sg:sg + 8], ALU.mult)
                s_vv.inc(ve.tt(modv_sb[:, 48:56], modv_sb[:, 16:24],
                               bias_sb[:, 112:120], ALU.mult))
                M[("modv",)] = s_vv.n

        if upto == "setup":
            _finish(block, s_st)
            return nc, dram

        with contextlib.ExitStack() as es:
            L = es.enter_context
            inwm_sb = L(nc.sbuf_tensor("inwm_sb", [P, KD * E], BF16))
            dtw_sb = L(nc.sbuf_tensor("dtw_sb", [P, KE * E], BF16))
            bcdw_sb = L(nc.sbuf_tensor("bcdw_sb", [P, KE * 48], BF16))
            h_sb = L(nc.sbuf_tensor("h_sb", [P, KD * W], BF16))
            hpair_sb = L(nc.sbuf_tensor("hpair_sb", [P, KD * 2], BF16))
            xm_sb = L(nc.sbuf_tensor("xm_sb", [P, KE * W], BF16))

            xc_sb = L(nc.sbuf_tensor("xc_sb", [P, KE * TS], BF16))
            sp_sb = L(nc.sbuf_tensor("sp_sb", [P, 2 * W], BF16))
            xsq_sb = sp_sb
            expz_sb = L(nc.sbuf_tensor("expz_sb", [P, TS], BF16))
            bc_sb = L(nc.sbuf_tensor("bc_sb", [P, 2 * W], BF16))
            cs1_sb = L(nc.sbuf_tensor("cs1_sb", [P, W], F32))
            cs2_sb = L(nc.sbuf_tensor("cs2_sb", [P, W], F32))
            cs3_sb = L(nc.sbuf_tensor("cs3_sb", [P, TS], F32))
            cs4_sb = L(nc.sbuf_tensor("cs4_sb", [P, TS], F32))
            bt_sb = L(nc.sbuf_tensor("bt_sb", [S, TS], BF16))
            ct_sb = L(nc.sbuf_tensor("ct_sb", [S, TS], BF16))
            dt2_sb = L(nc.sbuf_tensor("dt2_sb", [S, TS], BF16))
            u_sb = L(nc.sbuf_tensor("u_sb", [S, TS], BF16))
            xscr_sb = L(nc.sbuf_tensor("xscr_sb", [P, P], F32))
            rm_sb = L(nc.sbuf_tensor("rm_sb", [P, P], F32))
            g_sb = L(nc.sbuf_tensor("g_sb", [P, P], BF16))
            ldc_sb = L(nc.sbuf_tensor("ldc_sb", [P, 1], F32))
            ncl_sb = L(nc.sbuf_tensor("ncl_sb", [P, 1], F32))
            utm_sb = L(nc.sbuf_tensor("utm_sb", [P, S], BF16))
            tsc_sb = L(nc.sbuf_tensor("tsc_sb", [P, S], F32))

            ps_inp = L(nc.psum_tensor("ps_inp", [P, 1024], F32))
            ps_bc = L(nc.psum_tensor("ps_bc", [P, 1024], F32))
            ps_sml = L(nc.psum_tensor("ps_sml", [P, 512], F32))
            ps_stat = L(nc.psum_tensor("ps_stat", [1, 512], F32))
            ps_halo = L(nc.psum_tensor("ps_halo", [P, 64], F32))
            ps_smlb = L(nc.psum_tensor("ps_smlb", [P, 32], BF16))

            tiles = [(1, j) for j in range(NT)] + [(0, j) for j in range(NT)]

            @block.gpsimd
            def _(g):
                g.wait_ge(s_set.h, s_set.n)
                g.wait_ge(s_mm.h, mg("modh", 1))
                s_m1a.inc(g.dma_start(
                    inwm_sb[:].rearrange("p (k n) -> p k n", k=KD), r3(inwm)), 16)
                M[("m1wa",)] = s_m1a.n
                s_set.inc(g.dma_start(
                    dtw_sb[:].rearrange("p (k n) -> p k n", k=KE), r3(dtw)), 16)
                s_set.inc(g.dma_start(
                    bcdw_sb[:].rearrange("p (k n) -> p k n", k=KE), r3(bcdw)), 16)
                M[("m1w",)] = s_set.n
                for i, (part, j) in enumerate(tiles):
                    if i < 2:
                        continue
                    base = part * XW + j * TS
                    g.wait_ge(s_fx.h, i - 1)
                    sem = s_lxa if i % 2 == 0 else s_lxb
                    sem.inc(g.dma_start(
                        x_buf[:, (i % 2) * KD * W:((i % 2) + 1) * KD * W]
                        .rearrange("p (k n) -> p k n", k=KD),
                        r3(x)[:, :, base:base + W]), 16)

            @block.tensor
            def _(t):
                for i, (part, j) in enumerate(tiles):
                    own = (part == 0)
                    xo = (i % 2) * KD * W
                    sem = s_lxa if i % 2 == 0 else s_lxb
                    t.wait_ge(sem.h, 16 * (i // 2 + 1))
                    if i >= 1:
                        t.wait_ge(s_ss.h, mg("bcde", i - 1, 0))
                        t.wait_ge(s_ss.h, mg("bcde", i - 1, 1))
                        t.wait_ge(s_ss.h, mg("bcde", i - 1, 2))
                        t.wait_ge(s_vv.h, mg("stitch", i - 1))
                    for k in range(KD):
                        t.matmul(ps_stat[0:1, 0:512], onesb_sb[:, 0:1],
                                 x_buf[:, xo + k * W:xo + k * W + 512],
                                 start=(k == 0), stop=(k == KD - 1))
                    for k in range(KD):
                        inst = t.matmul(ps_halo[0:1, 32:34], onesb_sb[:, 0:1],
                                        x_buf[:, xo + k * W + 512:xo + k * W + 514],
                                        start=(k == 0), stop=(k == KD - 1))
                    s_mm.inc(inst)
                    M[("sums", i)] = s_mm.n
                    t.wait_ge(s_vv.h, mg("meancp", i))
                    for k in range(KD):
                        t.wait_ge(s_ss.h, mg("xsq", i, k))
                        t.matmul(ps_stat[0:1, 0:512], onesb_sb[:, 0:1],
                                 xsq_sb[:, (k % 2) * W:(k % 2) * W + 512],
                                 start=(k == 0), stop=(k == KD - 1))
                        inst = t.matmul(ps_halo[0:1, 32:34], onesb_sb[:, 0:1],
                                        xsq_sb[:, (k % 2) * W + 512:(k % 2) * W + 514],
                                        start=(k == 0), stop=(k == KD - 1))
                        s_mm.inc(inst)
                        M[("sq", i, k)] = s_mm.n
                    t.wait_ge(s_vv.h, mg("arow", i))
                    t.matmul(ps_bc[:, 0:512], ones_sb[0:1, :],
                             rows_sb[0:1, 1028:1540], start=True, stop=True)
                    s_mm.inc(t.matmul(ps_bc[:, 512:514], ones_sb[0:1, :],
                                      rows_sb[0:1, 1540:1542],
                                      start=True, stop=True))
                    M[("abc", i)] = s_mm.n
                    t.wait_ge(s_vv.h, mg("acp", i))
                    t.matmul(ps_bc[:, 0:512], ones_sb[0:1, :],
                             rows_sb[0:1, 1542:2054], start=True, stop=True)
                    s_mm.inc(t.matmul(ps_bc[:, 512:514], ones_sb[0:1, :],
                                      rows_sb[0:1, 2054:2056],
                                      start=True, stop=True))
                    M[("bbc", i)] = s_mm.n
                    if M1LVL < 2:
                        continue
                    if i == 0:
                        t.wait_ge(s_m1a.h, mg("m1wa"))
                    t.wait_ge(s_vv.h, mg("happ", i))
                    hsrc = h_sb
                    ho = 0
                    for m in range(KE):
                        if m >= 2:
                            t.wait_ge(s_ss.h, mg("xme", i, m - 2))
                        for k in range(KD):
                            inst = t.matmul(
                                ps_inp[:, (m % 2) * 512:(m % 2) * 512 + 512],
                                inwm_sb[:, k * E + m * P:k * E + (m + 1) * P],
                                hsrc[:, ho + k * W + 1:ho + k * W + 513],
                                start=(k == 0), stop=(k == KD - 1))
                        s_mm.inc(inst)
                        M[("inp", i, m)] = s_mm.n
                        for k in range(KD):
                            inst = t.matmul(
                                ps_halo[:, m * 2:m * 2 + 2],
                                inwm_sb[:, k * E + m * P:k * E + (m + 1) * P],
                                hpair_sb[:, k * 2:k * 2 + 2],
                                start=(m == 0 and k == 0),
                                stop=(m == KE - 1 and k == KD - 1))
                    s_mm.inc(inst)
                    M[("halo", i)] = s_mm.n
                    if M1LVL < 3:
                        continue
                    if i == 0:
                        t.wait_ge(s_set.h, mg("m1w"))
                    for m in range(KE):
                        if m >= 2:
                            t.wait_ge(s_ss.h, mg("spe", i, m - 2))
                        for k in range(KE):
                            t.wait_ge(s_ss.h, mg("xc", i, k))
                            inst = t.matmul(
                                ps_inp[:, (m % 2) * 512:(m % 2) * 512 + 512],
                                dtw_sb[:, k * E + m * P:k * E + (m + 1) * P],
                                xc_sb[:, k * TS:(k + 1) * TS],
                                start=(k == 0), stop=(k == KE - 1))
                        s_mm.inc(inst)
                        M[("dt", i, m)] = s_mm.n
                        if m >= 1:
                            t.wait_ge(s_ss.h, mg("spe", i, m - 1))
                            inst = t.matmul(ps_stat[0:1, 0:512], onesb_sb[:, 0:1],
                                            sp_sb[:, ((m - 1) % 2) * W:
                                                  ((m - 1) % 2) * W + TS],
                                            start=(m == 1), stop=False)
                            s_mm.inc(inst)
                            M[("dtma", i, m - 1)] = s_mm.n
                    t.wait_ge(s_ss.h, mg("spe", i, KE - 1))
                    inst = t.matmul(ps_stat[0:1, 0:512], onesb_sb[:, 0:1],
                                    sp_sb[:, ((KE - 1) % 2) * W:
                                          ((KE - 1) % 2) * W + TS],
                                    start=False, stop=True)
                    s_mm.inc(inst)
                    M[("dtma", i, KE - 1)] = s_mm.n
                    M[("dtm", i)] = s_mm.n
                    if M1LVL < 4:
                        continue
                    t.wait_ge(s_ss.h, mg("spe", i, KE - 1))
                    projs = [0, 2] if not own else [0, 1, 2]
                    for pi in projs:
                        col0 = 16 * pi
                        dst = (ps_inp[0:S, 0:512], ps_inp[0:S, 512:1024],
                               ps_bc[0:S, 0:512])[pi]
                        for k in range(KE):
                            inst = t.matmul(
                                dst, bcdw_sb[:, k * 48 + col0:k * 48 + col0 + S],
                                xc_sb[:, k * TS:(k + 1) * TS],
                                start=(k == 0), stop=(k == KE - 1))
                        s_mm.inc(inst)
                        M[("bcd", i, pi)] = s_mm.n
                    if M1LVL < 5:
                        continue
                    for c in range(CPT):
                        t.wait_ge(s_vv.h, mg("ldrow", i, c))
                        s_mm.inc(t.transpose(
                            ps_sml[:, 128:129],
                            rows_sb[0:1, 514 + c * P:514 + (c + 1) * P],
                            idf_sb[0:1, 0:1]))
                        t.drain()
                        M[("ldtr", i, c)] = s_mm.n
                        if M1LVL < 6:
                            continue
                        t.wait_ge(s_vv.h, mg("ldcp", i, c))
                        s_mm.inc(t.matmul(ps_sml[:, 129:130], ut_sb[:],
                                          ldc_sb[:], start=True, stop=True))
                        t.drain()
                        M[("cl", i, c)] = s_mm.n
                        if M1LVL < 7:
                            continue
                        t.wait_ge(s_vv.h, mg("xscr", i, c))
                        t.wait_ge(s_vv.h, mg("nclcp", i, c))
                        s_mm.inc(t.matmul(ps_sml[:, 0:128], ones_sb[:],
                                          xscr_sb[:], start=True, stop=True))
                        t.drain()
                        M[("R", i, c)] = s_mm.n
                        if M1LVL < 8:
                            continue
                        t.wait_ge(s_vv.h, mg("rmcp", i, c))
                        t.wait_ge(s_ss.h, mg("expcl", i, c))
                        t.wait_ge(s_vv.h, mg("urdy", i))
                        if "utr" not in M1SKIP:
                            s_mm.inc(t.transpose(ps_smlb[:, 0:16],
                                                 u_sb[0:S, c * P:(c + 1) * P],
                                                 idb_sb[0:S, 0:S]))
                            t.drain()
                            M[("utr", i, c)] = s_mm.n
                        if M1LVL < 9:
                            continue
                        t.wait_ge(s_vv.h, mg("ucp", i, c))
                        if own and "ctr" not in M1SKIP:
                            s_mm.inc(t.transpose(ps_smlb[:, 16:32],
                                                 ct_sb[0:S, c * P:(c + 1) * P],
                                                 idb_sb[0:S, 0:S]))
                            t.drain()
                            M[("ctr", i, c)] = s_mm.n
                            t.wait_ge(s_vv.h, mg("ccp", i, c))
                            t.wait_ge(s_ss.h, mg("G", i, c))
                            if "gu" not in M1SKIP:
                                s_mm.inc(t.matmul(ps_sml[:, 176:192], g_sb[:],
                                                  utm_sb[:], start=True, stop=True))
                                t.drain()
                                M[("GU", i, c)] = s_mm.n
                            t.wait_ge(s_vv.h, mg("yscp", i, c))
                        else:
                            t.wait_ge(s_ss.h, mg("G", i, c))
                        if "bk" not in M1SKIP:
                            s_mm.inc(t.matmul(ps_sml[0:1, 192:208], g_sb[:, 127:128],
                                              utm_sb[:], start=True, stop=True))
                            t.drain()
                            M[("bk", i, c)] = s_mm.n
                        t.wait_ge(s_vv.h, mg("bcp", i, c))
                M[("m1mm",)] = s_mm.n

            @block.scalar
            def _(s):
                sc = Eng(s)
                for i, (part, j) in enumerate(tiles):
                    own = (part == 0)
                    xo = (i % 2) * KD * W
                    sem = s_lxa if i % 2 == 0 else s_lxb
                    s.wait_ge(sem.h, 16 * (i // 2 + 1))
                    if i >= 1:
                        s.wait_ge(s_mm.h, mg("dtm", i - 1))
                    for k in range(KD):
                        if k >= 2:
                            s.wait_ge(s_mm.h, mg("sq", i, k - 2))
                        s_ss.inc(sc.act(xsq_sb[:, (k % 2) * W:(k % 2 + 1) * W],
                                        x_buf[:, xo + k * W:xo + (k + 1) * W],
                                        AF.Square))
                        M[("xsq", i, k)] = s_ss.n
                    s.wait_ge(s_vv.h, mg("var", i))
                    s_ss.inc(sc.act(rows_sb[0:1, 1542:2056],
                                    rows_sb[0:1, 514:1028], AF.Sqrt))
                    M[("sqrt", i)] = s_ss.n
                    if M1LVL < 2:
                        continue
                    for m in range(KE):
                        s.wait_ge(s_mm.h, mg("inp", i, m))
                        s_ss.inc(sc.act(xm_sb[:, m * W + 1:m * W + 513],
                                        ps_inp[:, (m % 2) * 512:(m % 2) * 512 + 512],
                                        AF.Identity, bias=bias_sb[:, m:m + 1]))
                        M[("xme", i, m)] = s_ss.n
                    for k in range(KE):
                        s.wait_ge(s_vv.h, mg("conv", i, k))
                        src = (cs1_sb if k % 2 == 0 else cs3_sb)
                        s_ss.inc(sc.act(xc_sb[:, k * TS:(k + 1) * TS],
                                        src[:, 0:TS], AF.Silu,
                                        bias=bias_sb[:, 80 + k:80 + k + 1]))
                        M[("xc", i, k)] = s_ss.n
                    if M1LVL < 3:
                        continue
                    for m in range(KE):
                        s.wait_ge(s_mm.h, mg("dt", i, m))
                        if m >= 2:
                            s.wait_ge(s_mm.h, mg("dtma", i, m - 2))
                        sc.act(expz_sb[:],
                               ps_inp[:, (m % 2) * 512:(m % 2) * 512 + 512],
                               AF.Exp, bias=bias_sb[:, 96 + m:96 + m + 1])
                        s_ss.inc(sc.act(sp_sb[:, (m % 2) * W:(m % 2) * W + TS],
                                        expz_sb[:], AF.Ln, bias=1.0))
                        M[("spe", i, m)] = s_ss.n
                    if M1LVL < 4:
                        continue
                    projs = [0, 2] if not own else [0, 1, 2]
                    for pi in projs:
                        s.wait_ge(s_mm.h, mg("bcd", i, pi))
                        src = (ps_inp[0:S, 0:512], ps_inp[0:S, 512:1024],
                               ps_bc[0:S, 0:512])[pi]
                        dst = (bt_sb, ct_sb, dt2_sb)[pi]
                        s_ss.inc(sc.act(dst[:], src, AF.Identity,
                                        bias=bcdb_sb[:, pi:pi + 1]))
                        M[("bcde", i, pi)] = s_ss.n
                    if not own:
                        M[("bcde", i, 1)] = s_ss.n
                    if M1LVL < 5:
                        continue
                    for c in range(CPT):
                        if M1LVL < 8:
                            continue
                        ch = j * CPT + c
                        hoff = 0 if own else TPH
                        if "acts" not in M1SKIP:
                            s.wait_ge(s_vv.h, mg("rmcp", i, c))
                            sc.act(expcl_sb[0:1, hoff + ch * P:hoff + (ch + 1) * P],
                                   rows_sb[0:1, 1542:1670], AF.Exp)
                            hoc = 0 if own else NCH
                            s_ss.inc(sc.act(aall_sb[0:1, hoc + ch:hoc + ch + 1],
                                            rows_sb[0:1, 1669:1670], AF.Exp))
                            M[("expcl", i, c)] = s_ss.n
                        if "g" not in M1SKIP:
                            s.wait_ge(s_vv.h, mg("rmcp", i, c))
                            s_ss.inc(sc.act(g_sb[:], rm_sb[:], AF.Exp))
                            M[("G", i, c)] = s_ss.n
                M[("m1ssend",)] = s_ss.n

            @block.vector
            def _(v):
                ve = Eng(v)
                for i, (part, j) in enumerate(tiles):
                    own = (part == 0)
                    xo = (i % 2) * KD * W
                    sem = s_lxa if i % 2 == 0 else s_lxb
                    v.wait_ge(sem.h, 16 * (i // 2 + 1))
                    v.wait_ge(s_mm.h, mg("sums", i))
                    ve.ts(rows_sb[0:1, 0:512], ps_stat[0:1, 0:512],
                          1.0 / D, None, ALU.mult)
                    s_vv.inc(ve.ts(rows_sb[0:1, 512:514], ps_halo[0:1, 32:34],
                                   1.0 / D, None, ALU.mult))
                    M[("meancp", i)] = s_vv.n
                    v.wait_ge(s_mm.h, mg("sq", i, KD - 1))
                    ve.ts(rows_sb[0:1, 514:1026], ps_stat[0:1, 0:512],
                          1.0 / D, None, ALU.mult)
                    ve.ts(rows_sb[0:1, 1026:1028], ps_halo[0:1, 32:34],
                          1.0 / D, None, ALU.mult)
                    ve.tt(rows_sb[0:1, 1028:1542], rows_sb[0:1, 0:514],
                          rows_sb[0:1, 0:514], ALU.mult)
                    ve.tt(rows_sb[0:1, 514:1028], rows_sb[0:1, 514:1028],
                          rows_sb[0:1, 1028:1542], ALU.subtract)
                    s_vv.inc(ve.ts(rows_sb[0:1, 514:1028],
                                   rows_sb[0:1, 514:1028], 1e-5, None, ALU.add))
                    M[("var", i)] = s_vv.n
                    v.wait_ge(s_ss.h, mg("sqrt", i))
                    ve.recip(rows_sb[0:1, 1028:1542], rows_sb[0:1, 1542:2056])
                    ve.tt(rows_sb[0:1, 1542:2056], rows_sb[0:1, 0:514],
                          rows_sb[0:1, 1028:1542], ALU.mult)
                    s_vv.inc(ve.ts(rows_sb[0:1, 1542:2056],
                                   rows_sb[0:1, 1542:2056], -1.0, None, ALU.mult))
                    M[("arow", i)] = s_vv.n
                    v.wait_ge(s_mm.h, mg("abc", i))
                    s_vv.inc(ve.cp(bc_sb[:, 0:W], ps_bc[:, 0:514]))
                    M[("acp", i)] = s_vv.n
                    v.wait_ge(s_mm.h, mg("bbc", i))
                    if i >= 1:
                        v.wait_ge(s_mm.h, mg("halo", i - 1))
                    ve.cp(bc_sb[:, W:2 * W], ps_bc[:, 0:514])
                    hdst = h_sb
                    ho = 0
                    for k in range(KD):
                        ve.tt(cs1_sb[:], x_buf[:, xo + k * W:xo + (k + 1) * W],
                              bc_sb[:, 0:W], ALU.mult)
                        ve.tt(cs2_sb[:], cs1_sb[:], bc_sb[:, W:2 * W], ALU.add)
                        ve.ts(hdst[:, ho + k * W:ho + (k + 1) * W], cs2_sb[:],
                              modv_sb[:, 8 + k:8 + k + 1],
                              modv_sb[:, k:k + 1], ALU.mult, ALU.add)
                    ve.cp(hpair_sb[:],
                          bass.AP(h_sb, ho, [[KD * W, P], [W, KD], [513, 2]]))
                    s_vv.inc(ve.dinc())
                    M[("happ", i)] = s_vv.n
                    s_fx.inc(ve.dinc())
                    if M1LVL < 2:
                        continue
                    v.wait_ge(s_mm.h, mg("halo", i))
                    lmask = (part * 2) if j == 0 else 4
                    rmask = (part * 2 + 1) if j == NT - 1 else 4
                    for m in range(KE):
                        for hc, col, mi in ((0, 0, lmask), (1, 513, rmask)):
                            if mi == 4:
                                ve.ts(xm_sb[:, m * W + col:m * W + col + 1],
                                      ps_halo[:, m * 2 + hc:m * 2 + hc + 1],
                                      bias_sb[:, m:m + 1], None, ALU.add)
                            else:
                                ve.ts(xm_sb[:, m * W + col:m * W + col + 1],
                                      ps_halo[:, m * 2 + hc:m * 2 + hc + 1],
                                      bias_sb[:, m:m + 1],
                                      hm_sb[:, mi:mi + 1], ALU.add, ALU.mult)
                    s_vv.inc(ve.dinc())
                    M[("stitch", i)] = s_vv.n
                    for k in range(KE):
                        if k >= 2:
                            v.wait_ge(s_ss.h, mg("xc", i, k - 2))
                        v.wait_ge(s_ss.h, mg("xme", i, k))
                        if k == 0:
                            v.wait_ge(s_vv.h, mg("stitch", i))
                        acc = cs1_sb if k % 2 == 0 else cs3_sb
                        oth = cs2_sb if k % 2 == 0 else cs4_sb
                        ve.ts(oth[:, 0:TS], xm_sb[:, k * W + 2:k * W + 2 + TS],
                              bias_sb[:, 64 + k:64 + k + 1], None, ALU.mult)
                        ve.stt(acc[:, 0:TS], xm_sb[:, k * W + 1:k * W + 1 + TS],
                               bias_sb[:, 48 + k:48 + k + 1], oth[:, 0:TS],
                               ALU.mult, ALU.add)
                        s_vv.inc(ve.stt(acc[:, 0:TS], xm_sb[:, k * W:k * W + TS],
                                        bias_sb[:, 32 + k:32 + k + 1],
                                        acc[:, 0:TS], ALU.mult, ALU.add))
                        M[("conv", i, k)] = s_vv.n
                    if M1LVL < 4:
                        continue
                    v.wait_ge(s_ss.h, mg("bcde", i, 0))
                    v.wait_ge(s_ss.h, mg("bcde", i, 2))
                    s_vv.inc(ve.tt(u_sb[:], bt_sb[:], dt2_sb[:], ALU.mult))
                    M[("urdy", i)] = s_vv.n
                    if M1LVL < 5:
                        continue
                    for c in range(CPT):
                        ch = j * CPT + c
                        hoff = 0 if own else NCH * S
                        v.wait_ge(s_mm.h, mg("dtm", i))
                        s_vv.inc(ve.ts(
                            rows_sb[0:1, 514 + c * P:514 + (c + 1) * P],
                            ps_stat[0:1, c * P:(c + 1) * P],
                            -1.0 / E, -10.0, ALU.mult, ALU.max))
                        M[("ldrow", i, c)] = s_vv.n
                        if M1LVL < 6:
                            continue
                        v.wait_ge(s_mm.h, mg("ldtr", i, c))
                        ve.cp(ldc_sb[:], ps_sml[:, 128:129])
                        s_vv.inc(ve.ts(xscr_sb[:], ut_sb[:], ldc_sb[:, 0:1],
                                       None, ALU.mult))
                        M[("ldcp", i, c)] = s_vv.n
                        M[("xscr", i, c)] = s_vv.n
                        if M1LVL < 7:
                            continue
                        v.wait_ge(s_mm.h, mg("cl", i, c))
                        s_vv.inc(ve.ts(ncl_sb[:], ps_sml[:, 129:130],
                                       -1.0, None, ALU.mult))
                        M[("nclcp", i, c)] = s_vv.n
                        v.wait_ge(s_mm.h, mg("R", i, c))
                        ve.tt(rm_sb[:], ps_sml[:, 0:128], mk_sb[:], ALU.add)
                        ve.ts(rm_sb[:], rm_sb[:], ncl_sb[:, 0:1], -80.0,
                              ALU.add, ALU.max)
                        s_vv.inc(ve.ts(rows_sb[0:1, 1542:1670],
                                       ps_sml[0:1, 0:128], -80.0, None,
                                       ALU.max))
                        M[("rmcp", i, c)] = s_vv.n
                        if M1LVL < 8:
                            continue
                        if "utr" not in M1SKIP:
                            v.wait_ge(s_mm.h, mg("utr", i, c))
                            s_vv.inc(ve.cp(utm_sb[:], ps_smlb[:, 0:16]))
                            M[("ucp", i, c)] = s_vv.n
                        if M1LVL < 9:
                            continue
                        if own:
                            if "ctr" not in M1SKIP:
                                v.wait_ge(s_mm.h, mg("ctr", i, c))
                                s_vv.inc(ve.cp(call_sb[:, ch * S:(ch + 1) * S],
                                               ps_smlb[:, 16:32]))
                                M[("ccp", i, c)] = s_vv.n
                            if "gu" not in M1SKIP and "yscp" not in M1SKIP:
                                v.wait_ge(s_mm.h, mg("GU", i, c))
                                ve.tt(tsc_sb[:], call_sb[:, ch * S:(ch + 1) * S],
                                      ps_sml[:, 176:192], ALU.mult)
                                s_vv.inc(ve.rsum(ysi_sb[:, ch:ch + 1], tsc_sb[:]))
                                M[("yscp", i, c)] = s_vv.n
                        if "bk" not in M1SKIP:
                            v.wait_ge(s_mm.h, mg("bk", i, c))
                            s_vv.inc(ve.cp(
                                ball_sb[0:1, hoff + ch * S:hoff + (ch + 1) * S],
                                ps_sml[0:1, 192:208]))
                            M[("bcp", i, c)] = s_vv.n

        if upto == "m1":
            if dbg:
                _dbg_emit(block, dbg_rows, dbg_cols, expcl_sb, ball_sb, ysi_sb,
                          ysr_sb, None, s_vv, s_ss, s_gf, zero_rows=True)
            _finish(block, s_st)
            return nc, dram

        with contextlib.ExitStack() as es:
            L = es.enter_context
            inwg_sb = L(nc.sbuf_tensor("inwg_sb", [P, KD * E], BF16))
            outw_sb = L(nc.sbuf_tensor("outw_sb", [P, KE * D], BF16))
            hg_sb = L(nc.sbuf_tensor("hg_sb", [P, KD * TS], BF16))
            xsq2_sb = L(nc.sbuf_tensor("xsq2_sb", [P, 2 * TS], BF16))
            xq2_sb = L(nc.sbuf_tensor("xq2_sb", [P, 2 * TS], BF16))
            wr_sb = L(nc.sbuf_tensor("wr_sb", [P, KD * 8], BF16))
            sg_sb = L(nc.sbuf_tensor("sg_sb", [P, KE * TS], BF16))
            ap_sb = L(nc.sbuf_tensor("ap_sb", [P, KD * TS], BF16))
            x2_sb = L(nc.sbuf_tensor("x2_sb", [P, KD * TS], F32))
            x2b_sb = L(nc.sbuf_tensor("x2b_sb", [P, KD * TS], BF16))
            h2t_sb = L(nc.sbuf_tensor("h2t_sb", [P, KD * TS], BF16))
            rt_sb = L(nc.sbuf_tensor("rt_sb", [8, TS], F32))
            rtc_sb = L(nc.sbuf_tensor("rtc_sb", [P, 8], F32))
            mx_sb = L(nc.sbuf_tensor("mx_sb", [P, 8], F32))
            em2all_sb = L(nc.sbuf_tensor("em2all_sb", [P, 2 * CPT], F32))
            emtmp_sb = L(nc.sbuf_tensor("emtmp_sb", [8, P], F32))
            mi8_sb = L(nc.sbuf_tensor("mi8_sb", [P, 8], U32))
            mcs1_sb = L(nc.sbuf_tensor("mcs1", [P, TS], F32))
            mcs2_sb = L(nc.sbuf_tensor("mcs2", [P, TS], F32))
            bcg_sb = L(nc.sbuf_tensor("bcg_sb", [P, 2 * TS], BF16))
            bc2_sb = L(nc.sbuf_tensor("bc2_sb", [P, 2 * TS], BF16))
            p1s_sb = L(nc.sbuf_tensor("p1s_sb", [P, S], F32))
            tsc2_sb = L(nc.sbuf_tensor("tsc2_sb", [P, S], F32))

            ps_g = L(nc.psum_tensor("ps_g", [P, 1024], F32))
            ps_b2 = L(nc.psum_tensor("ps_b2", [P, 1024], F32))
            ps_s2 = L(nc.psum_tensor("ps_s2", [P, 512], F32))
            ps_st2 = L(nc.psum_tensor("ps_st2", [1, 512], F32))
            ps_tb = L(nc.psum_tensor("ps_tb", [P, 8], F32))
            ps_sq2 = L(nc.psum_tensor("ps_sq2", [1, 512], F32))

            @block.gpsimd
            def _(g):
                g.wait_ge(s_mm.h, mg("m1mm"))
                s_m2w.inc(g.dma_start(
                    inwg_sb[:].rearrange("p (k n) -> p k n", k=KD), r3(inwg)), 16)
                s_m2w.inc(g.dma_start(
                    outw_sb[:].rearrange("p (k n) -> p k n", k=KE), r3(outw)), 16)
                s_m2w.inc(g.dma_start(
                    wr_sb[:].rearrange("p (k n) -> p k n", k=KD), r3(wr)), 16)
                for j in range(NT):
                    i = 2 * NT + j
                    g.wait_ge(s_fx.h, i - 1)
                    sem = s_lxa if i % 2 == 0 else s_lxb
                    sem.inc(g.dma_start(
                        x_buf[:, (i % 2) * KD * W:((i % 2) + 1) * KD * W]
                        .rearrange("p (k n) -> p k n", k=KD),
                        r3(x)[:, :, j * TS:j * TS + W]), 16)
                for j in range(NT):
                    g.wait_ge(s_vv.h, mg("emrdy", j))
                    for c in range(CPT):
                        s_gf.inc(g.dma_start(
                            emr_d[0:2, j * TS + c * P:j * TS + (c + 1) * P],
                            emtmp_sb[2 * c:2 * c + 2, 0:P]), 16)
                    M[("m2gf", j)] = s_gf.n
                M[("m2gf",)] = s_gf.n

            @block.tensor
            def _(t):
                for k in range(NCH):
                    t.wait_ge(s_vv.h, mg("schain", k))
                    s_mm.inc(t.matmul(ps_s2[:, 0:S],
                                      expcl_sb[0:1, k * P:(k + 1) * P],
                                      srowb_sb[0:1, (k % 2) * S:(k % 2) * S + S],
                                      start=True, stop=True))
                    M[("P1", k)] = s_mm.n
                    t.wait_ge(s_vv.h, mg("ysc", k))
                    s_mm.inc(t.transpose(ps_s2[0:1, 256:384], p1s_sb[:, 1:2],
                                         idf_sb[:]))
                    t.drain()
                    M[("ystr", k)] = s_mm.n
                    t.wait_ge(s_vv.h, mg("ysrcp", k))

                def t_stats(j):
                    i = 2 * NT + j
                    sem = s_lxa if i % 2 == 0 else s_lxb
                    t.wait_ge(sem.h, 16 * (i // 2 + 1))
                    xo = (i % 2) * KD * W
                    if j == 1:
                        t.wait_ge(s_vv.h, mg("2var", 0))
                    elif j >= 2:
                        t.wait_ge(s_vv.h, mg("var2", j - 2))
                    for k in range(KD):
                        inst = t.matmul(ps_st2[0:1, 0:512], onesb_sb[:, 0:1],
                                        x_buf[:, xo + k * W + 1:xo + k * W + 513],
                                        start=(k == 0), stop=(k == KD - 1))
                    s_mm.inc(inst)
                    M[("2sums", j)] = s_mm.n
                    for k in range(KD):
                        t.wait_ge(s_ss.h, mg("2xsq", j, k))
                        inst = t.matmul(ps_sq2[0:1, 0:512], onesb_sb[:, 0:1],
                                        xsq2_sb[:, (k % 2) * TS:(k % 2) * TS + TS],
                                        start=(k == 0), stop=(k == KD - 1))
                        s_mm.inc(inst)
                        M[("2sq", j, k)] = s_mm.n

                def t_bc(j):
                    t.wait_ge(s_vv.h, mg("2arow", j))
                    if j == 1:
                        t.wait_ge(s_vv.h, mg("2happ", 0))
                    elif j >= 2:
                        t.wait_ge(s_vv.h, mg("h2", j - 2))
                    s_mm.inc(t.matmul(ps_b2[:, 512:1024], ones_sb[0:1, :],
                                      rows_sb[0:1, 1028:1540],
                                      start=True, stop=True))
                    M[("2abc", j)] = s_mm.n
                    t.wait_ge(s_vv.h, mg("2acp", j))
                    s_mm.inc(t.matmul(ps_b2[:, 512:1024], ones_sb[0:1, :],
                                      rows_sb[0:1, 1542:2054],
                                      start=True, stop=True))
                    M[("2bbc", j)] = s_mm.n

                def t_gate(j):
                    t.wait_ge(s_vv.h, mg("2happ", j))
                    if j == 0:
                        t.wait_ge(s_m2w.h, s_m2w.n)
                    if j >= 1:
                        t.wait_ge(s_ss.h, mg("ape", j - 1, KD - 1))
                    for m in range(KE):
                        if m >= 2:
                            t.wait_ge(s_ss.h, mg("sge", j, m - 2))
                        for k in range(KD):
                            inst = t.matmul(
                                ps_g[:, (m % 2) * 512:(m % 2) * 512 + 512],
                                inwg_sb[:, k * E + m * P:k * E + (m + 1) * P],
                                hg_sb[:, k * TS:(k + 1) * TS],
                                start=(k == 0), stop=(k == KD - 1))
                        s_mm.inc(inst)
                        M[("gate", j, m)] = s_mm.n

                def t_op(j):
                    t.wait_ge(s_ss.h, mg("sge", j, KE - 1))
                    for m in range(KD):
                        if m >= 2:
                            t.wait_ge(s_ss.h, mg("ape", j, m - 2))
                        for k in range(KE):
                            inst = t.matmul(
                                ps_g[:, (m % 2) * 512:(m % 2) * 512 + 512],
                                outw_sb[:, k * D + m * P:k * D + (m + 1) * P],
                                sg_sb[:, k * TS:(k + 1) * TS],
                                start=(k == 0), stop=(k == KE - 1))
                        s_mm.inc(inst)
                        M[("op", j, m)] = s_mm.n

                def t_ysb(j):
                    t.wait_ge(s_vv.h, mg("ysrcp", NCH - 1))
                    if j >= 1:
                        t.wait_ge(s_vv.h, mg("x2", j - 1))
                    s_mm.inc(t.matmul(ps_b2[:, 0:512], onesb_sb[0:1, :],
                                      ysr_sb[0:1, j * TS:(j + 1) * TS],
                                      start=True, stop=True))
                    M[("ysb", j)] = s_mm.n

                def t_bstats(j):
                    t.wait_ge(s_vv.h, mg("x2b", j))
                    if j + 1 < NT:
                        t.wait_ge(s_vv.h, mg("2mean", j + 1))
                    elif j >= 1:
                        t.wait_ge(s_vv.h, mg("mean2", j - 1))
                    for k in range(KD):
                        inst = t.matmul(ps_st2[0:1, 0:512], onesb_sb[:, 0:1],
                                        x2b_sb[:, k * TS:(k + 1) * TS],
                                        start=(k == 0), stop=(k == KD - 1))
                    s_mm.inc(inst)
                    M[("sums2", j)] = s_mm.n
                    if j + 1 < NT:
                        t.wait_ge(s_vv.h, mg("2var", j + 1))
                    elif j >= 1:
                        t.wait_ge(s_vv.h, mg("var2", j - 1))
                    for k in range(KD):
                        t.wait_ge(s_ss.h, mg("x2sqk", j, k))
                        inst = t.matmul(ps_sq2[0:1, 0:512], onesb_sb[:, 0:1],
                                        xq2_sb[:, (k % 2) * TS:(k % 2) * TS + TS],
                                        start=(k == 0), stop=(k == KD - 1))
                        s_mm.inc(inst)
                        M[("sq2", j, k)] = s_mm.n

                def t_abc2(j):
                    t.wait_ge(s_vv.h, mg("arow2", j))
                    if j + 1 < NT:
                        t.wait_ge(s_vv.h, mg("2happ", j + 1))
                    elif j >= 1:
                        t.wait_ge(s_vv.h, mg("h2", j - 1))
                    s_mm.inc(t.matmul(ps_b2[:, 512:1024], ones_sb[0:1, :],
                                      rows_sb[0:1, 1028:1540],
                                      start=True, stop=True))
                    M[("abc2", j)] = s_mm.n
                    t.wait_ge(s_vv.h, mg("acp2", j))
                    s_mm.inc(t.matmul(ps_b2[:, 512:1024], ones_sb[0:1, :],
                                      rows_sb[0:1, 1542:2054],
                                      start=True, stop=True))
                    M[("bbc2", j)] = s_mm.n

                def t_route(j):
                    t.wait_ge(s_vv.h, mg("h2", j))
                    if j >= 1:
                        t.wait_ge(s_vv.h, mg("emcp", j - 1))
                    for k in range(KD):
                        inst = t.matmul(ps_s2[0:8, 0:512],
                                        wr_sb[:, k * 8:(k + 1) * 8],
                                        h2t_sb[:, k * TS:(k + 1) * TS],
                                        start=(k == 0), stop=(k == KD - 1))
                    s_mm.inc(inst)
                    M[("route", j)] = s_mm.n

                def t_rtt(j):
                    for c in range(CPT):
                        t.wait_ge(s_ss.h, mg("rte", j))
                        if c >= 1:
                            t.wait_ge(s_vv.h, mg("em2", j, c - 1))
                        elif j >= 1:
                            t.wait_ge(s_vv.h, mg("em2", j - 1, CPT - 1))
                        s_mm.inc(t.transpose(ps_tb[:],
                                             rt_sb[0:8, c * P:(c + 1) * P],
                                             idf_sb[0:8, 0:8]))
                        t.drain()
                        M[("rtt", j, c)] = s_mm.n

                def t_emt(j):
                    t.wait_ge(s_vv.h, mg("em2", j, CPT - 1))
                    t.wait_ge(s_ss.h, mg("rte", j))
                    s_mm.inc(t.transpose(ps_s2[0:8, 256:384], em2all_sb[:],
                                         idf_sb[:]))
                    t.drain()
                    M[("emt", j)] = s_mm.n

                t_stats(0)
                t_bc(0)
                t_gate(0)
                t_op(0)
                t_ysb(0)
                for j in range(NT):
                    if j + 1 < NT:
                        t_stats(j + 1)
                        t_bc(j + 1)
                    if j >= 1:
                        t_emt(j - 1)
                    if j + 1 < NT:
                        t_gate(j + 1)
                    t_bstats(j)
                    t_abc2(j)
                    if j + 1 < NT:
                        t_op(j + 1)
                        t_ysb(j + 1)
                    t_route(j)
                    t_rtt(j)
                t_emt(NT - 1)

            @block.scalar
            def _(s):
                sc = Eng(s)

                def s_stats(j):
                    i = 2 * NT + j
                    sem = s_lxa if i % 2 == 0 else s_lxb
                    s.wait_ge(sem.h, 16 * (i // 2 + 1))
                    xo = (i % 2) * KD * W
                    if j >= 1:
                        s.wait_ge(s_mm.h, mg("2sq", j - 1, KD - 1))
                    for k in range(KD):
                        if k >= 2:
                            s.wait_ge(s_mm.h, mg("2sq", j, k - 2))
                        s_ss.inc(sc.act(xsq2_sb[:, (k % 2) * TS:(k % 2 + 1) * TS],
                                        x_buf[:, xo + k * W + 1:xo + k * W + 513],
                                        AF.Square))
                        M[("2xsq", j, k)] = s_ss.n

                def s_x2sq(j):
                    s.wait_ge(s_vv.h, mg("x2b", j))
                    if j >= 1:
                        s.wait_ge(s_mm.h, mg("sq2", j - 1, KD - 1))
                    for k in range(KD):
                        if k >= 2:
                            s.wait_ge(s_mm.h, mg("sq2", j, k - 2))
                        s_ss.inc(sc.act(xq2_sb[:, (k % 2) * TS:(k % 2 + 1) * TS],
                                        x2b_sb[:, k * TS:(k + 1) * TS],
                                        AF.Square))
                        M[("x2sqk", j, k)] = s_ss.n

                def s_2sqrt(j):
                    s.wait_ge(s_vv.h, mg("2var", j))
                    if j == 1:
                        s.wait_ge(s_mm.h, mg("2bbc", 0))
                    elif j >= 2:
                        s.wait_ge(s_mm.h, mg("bbc2", j - 2))
                    s_ss.inc(sc.act(rows_sb[0:1, 1542:2054],
                                    rows_sb[0:1, 514:1026], AF.Sqrt))
                    M[("2sqrt", j)] = s_ss.n

                def s_sqrt2(j):
                    s.wait_ge(s_vv.h, mg("var2", j))
                    if j + 1 < NT:
                        s.wait_ge(s_mm.h, mg("2bbc", j + 1))
                    s_ss.inc(sc.act(rows_sb[0:1, 1542:2054],
                                    rows_sb[0:1, 514:1026], AF.Sqrt))
                    M[("sqrt2", j)] = s_ss.n

                def s_sge(j, m):
                    s.wait_ge(s_mm.h, mg("gate", j, m))
                    s_ss.inc(sc.act(sg_sb[:, m * TS:(m + 1) * TS],
                                    ps_g[:, (m % 2) * 512:(m % 2) * 512 + 512],
                                    AF.Silu,
                                    bias=bias_sb[:, 16 + m:16 + m + 1]))
                    M[("sge", j, m)] = s_ss.n

                def s_ape(j, m):
                    s.wait_ge(s_mm.h, mg("op", j, m))
                    if j >= 1:
                        s.wait_ge(s_vv.h, mg("x2d", j - 1, m))
                    s_ss.inc(sc.act(ap_sb[:, m * TS:(m + 1) * TS],
                                    ps_g[:, (m % 2) * 512:(m % 2) * 512 + 512],
                                    AF.Identity))
                    M[("ape", j, m)] = s_ss.n

                def s_rte(j):
                    s.wait_ge(s_mm.h, mg("route", j))
                    s_ss.inc(sc.act(rt_sb[:], ps_s2[0:8, 0:512], AF.Sigmoid,
                                    bias=br_sb[:, 0:1]))
                    M[("rte", j)] = s_ss.n

                s_stats(0)
                s_2sqrt(0)
                for m in range(KE):
                    s_sge(0, m)
                for m in range(KD):
                    s_ape(0, m)
                for j in range(NT):
                    if j + 1 < NT:
                        s_stats(j + 1)
                        s_2sqrt(j + 1)
                        for m in range(KE):
                            s_sge(j + 1, m)
                    s_x2sq(j)
                    s_sqrt2(j)
                    if j + 1 < NT:
                        for m in range(KD):
                            s_ape(j + 1, m)
                    s_rte(j)

            @block.vector
            def _(v):
                ve = Eng(v)
                v.wait_ge(s_ss.h, mg("m1ssend"))
                ve.mset(srow_sb[0:1, 32:48], 0.0)
                cur = 32
                for k in range(NCH):
                    nxt = 48 if cur == 32 else 32
                    ve.stt(srow_sb[0:1, nxt:nxt + S], srow_sb[0:1, cur:cur + S],
                           aall_sb[0:1, NCH + k:NCH + k + 1],
                           ball_sb[0:1, NCH * S + k * S:NCH * S + (k + 1) * S],
                           ALU.mult, ALU.add)
                    cur = nxt
                ve.ts(srow_sb[0:1, 0:S], srow_sb[0:1, cur:cur + S],
                      flag_sb[0:1, 0:1], None, ALU.mult)
                s_vv.inc(ve.cp(srowb_sb[0:1, 0:S], srow_sb[0:1, 0:S]))
                M[("schain", 0)] = s_vv.n
                for k in range(NCH):
                    v.wait_ge(s_mm.h, mg("P1", k))
                    ve.tt(tsc2_sb[:], call_sb[:, k * S:(k + 1) * S],
                          ps_s2[:, 0:S], ALU.mult)
                    ve.rsum(p1s_sb[:, 0:1], tsc2_sb[:])
                    s_vv.inc(ve.tt(p1s_sb[:, 1:2], p1s_sb[:, 0:1],
                                   ysi_sb[:, k:k + 1], ALU.add))
                    M[("ysc", k)] = s_vv.n
                    nxt_k = ((k + 1) % 2) * S
                    cur_k = (k % 2) * S
                    ve.stt(srow_sb[0:1, nxt_k:nxt_k + S],
                           srow_sb[0:1, cur_k:cur_k + S],
                           aall_sb[0:1, k:k + 1],
                           ball_sb[0:1, k * S:(k + 1) * S],
                           ALU.mult, ALU.add)
                    s_vv.inc(ve.cp(srowb_sb[0:1, nxt_k:nxt_k + S],
                                   srow_sb[0:1, nxt_k:nxt_k + S]))
                    M[("schain", k + 1)] = s_vv.n
                    v.wait_ge(s_mm.h, mg("ystr", k))
                    s_vv.inc(ve.cp(ysr_sb[0:1, k * P:(k + 1) * P],
                                   ps_s2[0:1, 256:384]))
                    M[("ysrcp", k)] = s_vv.n
                def fv_rows(j):
                    if j == 1:
                        v.wait_ge(s_mm.h, mg("2abc", 0))
                    elif j >= 2:
                        v.wait_ge(s_mm.h, mg("abc2", j - 2))
                    v.wait_ge(s_mm.h, mg("2sums", j))
                    s_vv.inc(ve.ts(rows_sb[0:1, 0:512], ps_st2[0:1, 0:512],
                                   1.0 / D, None, ALU.mult))
                    M[("2mean", j)] = s_vv.n
                    v.wait_ge(s_mm.h, mg("2sq", j, KD - 1))
                    ve.ts(rows_sb[0:1, 514:1026], ps_sq2[0:1, 0:512],
                          1.0 / D, None, ALU.mult)
                    ve.tt(rows_sb[0:1, 1028:1540], rows_sb[0:1, 0:512],
                          rows_sb[0:1, 0:512], ALU.mult)
                    ve.tt(rows_sb[0:1, 514:1026], rows_sb[0:1, 514:1026],
                          rows_sb[0:1, 1028:1540], ALU.subtract)
                    s_vv.inc(ve.ts(rows_sb[0:1, 514:1026],
                                   rows_sb[0:1, 514:1026], 1e-5, None, ALU.add))
                    M[("2var", j)] = s_vv.n
                    v.wait_ge(s_ss.h, mg("2sqrt", j))
                    ve.recip(rows_sb[0:1, 1028:1540], rows_sb[0:1, 1542:2054])
                    ve.tt(rows_sb[0:1, 1542:2054], rows_sb[0:1, 0:512],
                          rows_sb[0:1, 1028:1540], ALU.mult)
                    s_vv.inc(ve.ts(rows_sb[0:1, 1542:2054],
                                   rows_sb[0:1, 1542:2054], -1.0, None, ALU.mult))
                    M[("2arow", j)] = s_vv.n

                def fv_happ(j):
                    i = 2 * NT + j
                    sem = s_lxa if i % 2 == 0 else s_lxb
                    v.wait_ge(sem.h, 16 * (i // 2 + 1))
                    xo = (i % 2) * KD * W
                    v.wait_ge(s_mm.h, mg("2abc", j))
                    s_vv.inc(ve.cp(bcg_sb[:, 0:TS], ps_b2[:, 512:1024]))
                    M[("2acp", j)] = s_vv.n
                    v.wait_ge(s_mm.h, mg("2bbc", j))
                    if j >= 1:
                        v.wait_ge(s_mm.h, mg("gate", j - 1, KE - 1))
                    ve.cp(bcg_sb[:, TS:2 * TS], ps_b2[:, 512:1024])
                    for k in range(KD):
                        ve.tt(mcs1_sb[:], x_buf[:, xo + k * W + 1:xo + k * W + 513],
                              bcg_sb[:, 0:TS], ALU.mult)
                        ve.tt(mcs2_sb[:], mcs1_sb[:], bcg_sb[:, TS:2 * TS],
                              ALU.add)
                        ve.ts(hg_sb[:, k * TS:(k + 1) * TS], mcs2_sb[:],
                              modv_sb[:, 8 + k:8 + k + 1],
                              modv_sb[:, k:k + 1], ALU.mult, ALU.add)
                    s_vv.inc(ve.dinc())
                    M[("2happ", j)] = s_vv.n

                def bv_a_hg(j):
                    i = 2 * NT + j
                    xo = (i % 2) * KD * W
                    if j >= 1:
                        v.wait_ge(s_stx.h, mg("m2st_x", j - 1))
                    for d in range(KD):
                        ve.ts(x2_sb[:, d * TS:(d + 1) * TS],
                              hg_sb[:, d * TS:(d + 1) * TS],
                              modv_sb[:, 16 + d:16 + d + 1],
                              modv_sb[:, 48 + d:48 + d + 1], ALU.mult, ALU.add)
                        ve.tt(x2_sb[:, d * TS:(d + 1) * TS],
                              x2_sb[:, d * TS:(d + 1) * TS],
                              x_buf[:, xo + d * W + 1:xo + d * W + 513], ALU.add)
                    s_vv.inc(ve.dinc())
                    M[("x2hg", j)] = s_vv.n

                def bv_a_ap(j):
                    v.wait_ge(s_mm.h, mg("ysb", j))
                    if j >= 1:
                        v.wait_ge(s_mm.h, mg("sums2", j - 1))
                        v.wait_ge(s_ss.h, mg("x2sqk", j - 1, KD - 1))
                    for d in range(KD):
                        v.wait_ge(s_ss.h, mg("ape", j, d))
                        ve.stt(mcs1_sb[:], ap_sb[:, d * TS:(d + 1) * TS],
                               modv_sb[:, 16 + d:16 + d + 1], ps_b2[:, 0:512],
                               ALU.mult, ALU.mult)
                        ve.tt(x2_sb[:, d * TS:(d + 1) * TS],
                              x2_sb[:, d * TS:(d + 1) * TS],
                              mcs1_sb[:], ALU.add)
                        s_vv.inc(ve.cp(x2b_sb[:, d * TS:(d + 1) * TS],
                                       x2_sb[:, d * TS:(d + 1) * TS]))
                        M[("x2d", j, d)] = s_vv.n
                    M[("x2", j)] = s_vv.n
                    M[("x2b", j)] = s_vv.n
                    s_fx.inc(ve.dinc())

                def bv_b1(j):
                    v.wait_ge(s_mm.h, mg("sums2", j))
                    s_vv.inc(ve.ts(rows_sb[0:1, 0:512], ps_st2[0:1, 0:512],
                                   1.0 / D, None, ALU.mult))
                    M[("mean2", j)] = s_vv.n
                    v.wait_ge(s_mm.h, mg("sq2", j, KD - 1))
                    if j + 1 < NT:
                        v.wait_ge(s_ss.h, mg("2sqrt", j + 1))
                        v.wait_ge(s_mm.h, mg("2abc", j + 1))
                    ve.ts(rows_sb[0:1, 514:1026], ps_sq2[0:1, 0:512],
                          1.0 / D, None, ALU.mult)
                    ve.tt(rows_sb[0:1, 1028:1540], rows_sb[0:1, 0:512],
                          rows_sb[0:1, 0:512], ALU.mult)
                    ve.tt(rows_sb[0:1, 514:1026], rows_sb[0:1, 514:1026],
                          rows_sb[0:1, 1028:1540], ALU.subtract)
                    s_vv.inc(ve.ts(rows_sb[0:1, 514:1026],
                                   rows_sb[0:1, 514:1026], 1e-5, None, ALU.add))
                    M[("var2", j)] = s_vv.n
                    v.wait_ge(s_ss.h, mg("sqrt2", j))
                    ve.recip(rows_sb[0:1, 1028:1540], rows_sb[0:1, 1542:2054])
                    ve.tt(rows_sb[0:1, 1542:2054], rows_sb[0:1, 0:512],
                          rows_sb[0:1, 1028:1540], ALU.mult)
                    s_vv.inc(ve.ts(rows_sb[0:1, 1542:2054],
                                   rows_sb[0:1, 1542:2054], -1.0, None, ALU.mult))
                    M[("arow2", j)] = s_vv.n

                def bv_emcp(j):
                    v.wait_ge(s_mm.h, mg("emt", j))
                    if j >= 1:
                        v.wait_ge(s_gf.h, mg("m2gf", j - 1))
                    ve.cp(emtmp_sb[0:8, 0:P], ps_s2[0:8, 256:384])
                    s_vv.inc(ve.dinc())
                    M[("emcp", j)] = s_vv.n
                    M[("emrdy", j)] = s_vv.n

                def bv_b2(j):
                    v.wait_ge(s_mm.h, mg("abc2", j))
                    s_vv.inc(ve.cp(bc2_sb[:, 0:TS], ps_b2[:, 512:1024]))
                    M[("acp2", j)] = s_vv.n
                    v.wait_ge(s_mm.h, mg("bbc2", j))
                    if j >= 1:
                        v.wait_ge(s_sth.h, mg("m2st_h", j - 1))
                    ve.cp(bc2_sb[:, TS:2 * TS], ps_b2[:, 512:1024])
                    for k in range(KD):
                        ve.tt(mcs1_sb[:], x2_sb[:, k * TS:(k + 1) * TS],
                              bc2_sb[:, 0:TS], ALU.mult)
                        ve.tt(mcs2_sb[:], mcs1_sb[:], bc2_sb[:, TS:2 * TS],
                              ALU.add)
                        ve.ts(h2t_sb[:, k * TS:(k + 1) * TS], mcs2_sb[:],
                              modv_sb[:, 32 + k:32 + k + 1],
                              modv_sb[:, 24 + k:24 + k + 1], ALU.mult, ALU.add)
                    s_vv.inc(ve.dinc())
                    M[("h2", j)] = s_vv.n

                def bv_argmax(j):
                    for c in range(CPT):
                        v.wait_ge(s_mm.h, mg("rtt", j, c))
                        ve.cp(rtc_sb[:], ps_tb[:])
                        ve.maxv(mx_sb[:], rtc_sb[:])
                        ve.maxidx(mi8_sb[:], mx_sb[:], rtc_sb[:])
                        ve.cp(em2all_sb[:, 2 * c:2 * c + 1], mi8_sb[:, 0:1])
                        s_vv.inc(ve.cp(em2all_sb[:, 2 * c + 1:2 * c + 2],
                                       mx_sb[:, 0:1]))
                        M[("em2", j, c)] = s_vv.n

                fv_rows(0)
                fv_happ(0)
                for j in range(NT):
                    bv_a_hg(j)
                    if j + 1 < NT:
                        fv_rows(j + 1)
                        fv_happ(j + 1)
                    bv_a_ap(j)
                    bv_b1(j)
                    if j >= 1:
                        bv_emcp(j - 1)
                    bv_b2(j)
                    bv_argmax(j)
                bv_emcp(NT - 1)

            @block.sync
            def _(sy):
                for j in range(NT):
                    sy.wait_ge(s_vv.h, mg("x2", j))
                    s_stx.inc(sy.dma_start(
                        r3(out_d)[:, :, j * TS:(j + 1) * TS],
                        x2_sb[:].rearrange("p (k n) -> p k n", k=KD)), 16)
                    M[("m2st_x", j)] = s_stx.n
                    sy.wait_ge(s_vv.h, mg("h2", j))
                    s_sth.inc(sy.dma_start(
                        r3(h2_d)[:, :, j * TS:(j + 1) * TS],
                        h2t_sb[:].rearrange("p (k n) -> p k n", k=KD)), 16)
                    M[("m2st_h", j)] = s_sth.n
                M[("m2stx",)] = s_stx.n
                M[("m2sth",)] = s_sth.n

        x_es.close()

        if upto == "m2":
            if dbg:
                _dbg_emit(block, dbg_rows, dbg_cols, expcl_sb, ball_sb, ysi_sb,
                          ysr_sb, None, s_vv, s_ss, s_gf)
            _finish(block, s_st, s_stx, s_sth)
            return nc, dram

        scan_es.close()

        with contextlib.ExitStack() as es:
            L = es.enter_context
            acc_sb = L(nc.sbuf_tensor("acc_sb", [P, KD * SHW], BF16))
            w1_sb = L(nc.sbuf_tensor("w1_sb", [P, KD * FF], BF16))
            w2_sb = L(nc.sbuf_tensor("w2_sb", [P, KF * D], BF16))
            t1_sb = L(nc.sbuf_tensor("t1_sb", [P, KF * TS], BF16))
            h2b_sb = L(nc.sbuf_tensor("h2b_sb", [P, KD * TS], BF16))
            emap_sb = L(nc.sbuf_tensor("emap_sb", [P, SHW], BF16))
            msk_sb = L(nc.sbuf_tensor("msk_sb", [P, TS], U32))
            otl_sb = L(nc.sbuf_tensor("otl_sb", [P, 2 * TS], F32))
            emrow_sb = L(nc.sbuf_tensor("emrow_sb", [1, 2 * TS], BF16))
            mprow_sb = L(nc.sbuf_tensor("mprow_sb", [1, TS], BF16))
            ps_all = L(nc.psum_tensor("ps_all", [P, 4096], F32))
            x2r_sb = h2b_sb

            eorders = [list(range(NEXP)) if sh % 2 == 0
                       else list(range(NEXP - 1, -1, -1)) for sh in range(NSH)]

            @block.gpsimd
            def _(g):
                g.wait_ge(s_stx.h, mg("m2stx"))
                g.wait_ge(s_sth.h, mg("m2sth"))
                g.wait_ge(s_gf.h, mg("m2gf"))
                prev_e, tile_n, inst_i = None, 0, 0
                for sh in range(NSH):
                    if sh >= 1:
                        g.wait_ge(s_fer.h, sh)
                    s_ler.inc(g.dma_start(
                        emrow_sb[0:1, 0:SHW],
                        emr_d[0:1, sh * SHW:(sh + 1) * SHW]), 16)
                    for ei, e in enumerate(eorders[sh]):
                        if e != prev_e:
                            if inst_i > 0:
                                g.wait_ge(s_fw1.h, inst_i)
                            s_lw1.inc(g.dma_start(
                                w1_sb[:].rearrange("p (k n) -> p k n", k=KD)
                                [:, :, 0:FF // 2],
                                r3(w1[e])[:, :, 0:FF // 2]), 16)
                            if inst_i > 0:
                                g.wait_ge(s_fw1b.h, inst_i)
                            s_lw1b.inc(g.dma_start(
                                w1_sb[:].rearrange("p (k n) -> p k n", k=KD)
                                [:, :, FF // 2:FF],
                                r3(w1[e])[:, :, FF // 2:FF]), 16)
                            if inst_i > 0:
                                g.wait_ge(s_fw2.h, inst_i)
                            s_lw2.inc(g.dma_start(
                                w2_sb[:].rearrange("p (k n) -> p k n", k=KF)
                                [:, :, 0:D // 2],
                                r3(w2[e])[:, :, 0:D // 2]), 16)
                            if inst_i > 0:
                                g.wait_ge(s_fw2b.h, inst_i)
                            s_lw2b.inc(g.dma_start(
                                w2_sb[:].rearrange("p (k n) -> p k n", k=KF)
                                [:, :, D // 2:D],
                                r3(w2[e])[:, :, D // 2:D]), 16)
                            prev_e = e
                        for q in range(QSH):
                            tile_n += 1
                        inst_i += 1
                    for q in range(QSH):
                        tq = sh * QSH + q
                        if tq >= 1:
                            g.wait_ge(s_fo.h, tq)
                            g.wait_ge(s_fmr.h, tq)
                        g.wait_ge(s_fh2.h, tile_n)
                        s_lmr.inc(g.dma_start(
                            mprow_sb[:],
                            emr_d[1:2, sh * SHW + q * TS:sh * SHW + (q + 1) * TS]),
                            16)
                        s_loa.inc(g.dma_start(
                            x2r_sb[:].rearrange("p (k n) -> p k n", k=KD),
                            r3(out_d)[:, :, sh * SHW + q * TS:
                                      sh * SHW + (q + 1) * TS]), 16)

            @block.tensor
            def _(t):
                tile_n = 0
                t.wait_ge(s_gf.h, mg("m2gf"))
                for sh in range(NSH):
                    if tile_n >= 1:
                        t.wait_ge(s_vv.h, mg("predq", 4 * tile_n - 1))
                    t.wait_ge(s_ler.h, 16 * (sh + 1))
                    for q in range(QSH):
                        s_mm.inc(t.matmul(
                            ps_all[:, q * 512:(q + 1) * 512], onesb_sb[0:1, :],
                            emrow_sb[0:1, q * TS:(q + 1) * TS],
                            start=True, stop=True))
                        M[("embc", sh, q)] = s_mm.n
                    s_fer.inc(t.drain())
                    for ei, e in enumerate(eorders[sh]):
                        nw = _w1count(eorders, sh, ei)
                        t.wait_ge(s_lw1.h, 16 * nw)
                        for q in range(QSH):
                            a = (2 * tile_n) % 8
                            t.wait_ge(s_lh2.h, 16 * (tile_n + 1))
                            if ei == 0:
                                t.wait_ge(s_vv.h, mg("emcp3", sh))
                            if tile_n >= 1:
                                t.wait_ge(s_vv.h, mg("predq", 4 * (tile_n - 1) + 1))
                            for m in range(KF):
                                zb = a + (m % 2)
                                if m == KF // 2:
                                    t.wait_ge(s_lw1b.h, 16 * nw)
                                if m >= 2:
                                    t.wait_ge(s_ss.h, mg("gelu", tile_n, m - 2))
                                for k in range(KD):
                                    inst = t.matmul(
                                        ps_all[:, zb * 512:(zb + 1) * 512],
                                        w1_sb[:, k * FF + m * P:k * FF + (m + 1) * P],
                                        h2b_sb[:, k * TS:(k + 1) * TS],
                                        start=(k == 0), stop=(k == KD - 1))
                                s_mm.inc(inst)
                                M[("z", tile_n, m)] = s_mm.n
                                if q == QSH - 1 and m == KF // 2 - 1:
                                    s_fw1.inc(t.drain())
                            s_fh2.inc(t.drain())
                            if q == QSH - 1:
                                s_fw1b.inc(t.drain())
                            t.wait_ge(s_ss.h, mg("gelu", tile_n, KF - 1))
                            t.wait_ge(s_lw2.h, 16 * nw)
                            for q4 in range(4):
                                if q4 == 2:
                                    t.wait_ge(s_lw2b.h, 16 * nw)
                                if q == QSH - 1 and q4 == 2:
                                    s_fw2.inc(t.drain())
                                if tile_n >= 1 and q4 <= 2:
                                    t.wait_ge(s_vv.h,
                                              mg("predq", 4 * (tile_n - 1) + q4 + 1))
                                for dd in range(2):
                                    d = 2 * q4 + dd
                                    yb = (a + 2 + 2 * q4 + dd) % 8
                                    for k in range(KF):
                                        inst = t.matmul(
                                            ps_all[:, yb * 512:(yb + 1) * 512],
                                            w2_sb[:, k * D + d * P:k * D + (d + 1) * P],
                                            t1_sb[:, k * TS:(k + 1) * TS],
                                            start=(k == 0), stop=(k == KF - 1))
                                s_mm.inc(inst)
                                M[("y", tile_n, q4)] = s_mm.n
                            if q == QSH - 1:
                                s_fw2b.inc(t.drain())
                            tile_n += 1
                    t.wait_ge(s_vv.h, mg("predq", 4 * tile_n - 1))
                    for q in range(QSH):
                        tq = sh * QSH + q
                        t.wait_ge(s_lmr.h, 16 * (tq + 1))
                        s_mm.inc(t.matmul(
                            ps_all[:, 0:512], onesb_sb[0:1, :],
                            mprow_sb[0:1, 0:TS],
                            start=True, stop=True))
                        M[("mpb", sh, q)] = s_mm.n
                        s_fmr.inc(t.drain())
                        t.wait_ge(s_vv.h, mg("fin", sh, q))

            @block.scalar
            def _(s):
                sc = Eng(s)
                tile_n = 0
                for sh in range(NSH):
                    for ei in range(NEXP):
                        for q in range(QSH):
                            a = (2 * tile_n) % 8
                            for m in range(KF):
                                zb = a + (m % 2)
                                s.wait_ge(s_mm.h, mg("z", tile_n, m))
                                s_ss.inc(sc.act(
                                    t1_sb[:, m * TS:(m + 1) * TS],
                                    ps_all[:, zb * 512:(zb + 1) * 512], AF.Gelu))
                                M[("gelu", tile_n, m)] = s_ss.n
                            tile_n += 1

            @block.vector
            def _(v):
                ve = Eng(v)
                tile_n = 0
                for sh in range(NSH):
                    for q in range(QSH):
                        v.wait_ge(s_mm.h, mg("embc", sh, q))
                        ve.cp(emap_sb[:, q * TS:(q + 1) * TS],
                              ps_all[:, q * 512:(q + 1) * 512])
                    s_vv.inc(ve.dinc())
                    M[("emcp3", sh)] = s_vv.n
                    for ei, e in enumerate(eorders[sh]):
                        for q in range(QSH):
                            a = (2 * tile_n) % 8
                            ve.ts(msk_sb[:], emap_sb[:, q * TS:(q + 1) * TS],
                                  float(e), None, ALU.is_equal)
                            for q4 in range(4):
                                v.wait_ge(s_mm.h, mg("y", tile_n, q4))
                                for dd in range(2):
                                    d = 2 * q4 + dd
                                    yb = (a + 2 + 2 * q4 + dd) % 8
                                    ve.pred(acc_sb[:, d * SHW + q * TS:
                                                   d * SHW + (q + 1) * TS],
                                            msk_sb[:],
                                            ps_all[:, yb * 512:(yb + 1) * 512])
                                s_vv.inc(ve.dinc())
                                M[("predq", 4 * tile_n + q4)] = s_vv.n
                            tile_n += 1
                    for q in range(QSH):
                        tq = sh * QSH + q
                        v.wait_ge(s_loa.h, 16 * (tq + 1))
                        v.wait_ge(s_mm.h, mg("mpb", sh, q))
                        for d in range(KD):
                            sct = s_sta if d % 2 == 0 else s_stb
                            if d >= 2:
                                v.wait_ge(sct.h, mg("fst", tq, d - 2))
                            elif tq >= 1:
                                v.wait_ge(sct.h, mg("fst", tq - 1, KD - 2 + d))
                            sl = otl_sb[:, (d % 2) * TS:(d % 2) * TS + TS]
                            ve.stt(sl,
                                   acc_sb[:, d * SHW + q * TS:
                                          d * SHW + (q + 1) * TS],
                                   modv_sb[:, 40 + d:40 + d + 1],
                                   ps_all[:, 0:512], ALU.mult, ALU.mult)
                            ve.tt(sl, sl, x2r_sb[:, d * TS:(d + 1) * TS],
                                  ALU.add)
                            s_vv.inc(ve.dinc())
                            M[("fin", sh, q, d)] = s_vv.n
                        s_fo.inc(ve.dinc())
                        M[("fin", sh, q)] = s_vv.n

            @block.sync
            def _(sy):
                sy.wait_ge(s_stx.h, mg("m2stx"))
                sy.wait_ge(s_sth.h, mg("m2sth"))
                sy.wait_ge(s_gf.h, mg("m2gf"))
                tile_n = 0
                for sh in range(NSH):
                    for ei in range(NEXP):
                        for q in range(QSH):
                            if tile_n >= 1:
                                sy.wait_ge(s_fh2.h, tile_n)
                            if ei == 0 and q == 0 and sh >= 1:
                                sy.wait_ge(s_fo.h, sh * QSH)
                            s_lh2.inc(sy.dma_start(
                                h2b_sb[:].rearrange("p (k n) -> p k n", k=KD),
                                r3(h2_d)[:, :, sh * SHW + q * TS:
                                         sh * SHW + (q + 1) * TS]), 16)
                            tile_n += 1
                    for q in range(QSH):
                        tq = sh * QSH + q
                        for d in range(KD):
                            sct = s_sta if d % 2 == 0 else s_stb
                            sy.wait_ge(s_vv.h, mg("fin", sh, q, d))
                            sct.inc(sy.dma_start(
                                r3(out_d)[:, d:d + 1,
                                          sh * SHW + q * TS:sh * SHW + (q + 1) * TS],
                                otl_sb[:, (d % 2) * TS:(d % 2) * TS + TS]
                                .rearrange("p (k n) -> p k n", k=1)), 16)
                            M[("fst", tq, d)] = sct.n



        _finish(block, s_st, s_stx, s_sth, s_sta, s_stb)
    return nc, dram


def _dbg_emit(block, dbg_rows, dbg_cols, expcl_sb, ball_sb, ysi_sb,
              ysr_sb, mpr_unused, s_vv, s_ss, s_st, zero_rows=False):
    @block.gpsimd
    def _(sy):
        sy.wait_ge(s_vv.h, s_vv.n)
        sy.wait_ge(s_ss.h, s_ss.n)
        sy.wait_ge(s_st.h, s_st.n)
        if zero_rows:
            sy.memset(ysr_sb[:], 0.0)
            sy.drain()
        o = 0
        for src in (expcl_sb, ball_sb, ysr_sb, ysr_sb):
            n = src.shape[1]
            s_st.inc(sy.dma_start(dbg_rows[0:1, o:o + n], src[:]), 16)
            o += n
        s_st.inc(sy.dma_start(dbg_cols[:, 0:ysi_sb.shape[1]], ysi_sb[:]), 16)
        sy.wait_ge(s_st.h, s_st.n)


def _finish(block, s_st, *extra):
    @block.sync
    def _(sy):
        sy.wait_ge(s_st.h, s_st.n)
        for s2 in extra:
            sy.wait_ge(s2.h, s2.n)


def build_nc(upto="final", dbg=False):
    marks = {}
    build_program(marks, upto=upto, dbg=dbg)
    return build_program(marks, upto=upto, dbg=dbg)



def stage_inputs(inputs, n_cores=8):
    f = np.float32
    x = np.asarray(inputs["L0_attn_x"], f)
    B, T, Dm = x.shape
    in_w = np.asarray(inputs["L0_attn_in_w"], f)
    in_b = np.asarray(inputs["L0_attn_in_b"], f)
    conv_w = np.asarray(inputs["L0_attn_conv_w"], f)
    conv_b = np.asarray(inputs["L0_attn_conv_b"], f)
    dt_w = np.asarray(inputs["L0_attn_dt_w"], f)
    dt_b = np.asarray(inputs["L0_attn_dt_b"], f)
    B_w = np.asarray(inputs["L0_attn_B_w"], f)
    B_b = np.asarray(inputs["L0_attn_B_b"], f)
    C_w = np.asarray(inputs["L0_attn_C_w"], f)
    C_b = np.asarray(inputs["L0_attn_C_b"], f)
    D_w = np.asarray(inputs["L0_attn_D_w"], f)
    D_b = np.asarray(inputs["L0_attn_D_b"], f)
    out_w = np.asarray(inputs["L0_attn_out_w"], f)
    out_b = np.asarray(inputs["L0_attn_out_b"], f)
    Wr = np.asarray(inputs["L0_ffn_Wr"], f)
    br = np.asarray(inputs["L0_ffn_br"], f)
    W1 = np.asarray(inputs["L0_ffn_W1"], f)
    W2 = np.asarray(inputs["L0_ffn_W2"], f)
    cmat = np.asarray(inputs["L0_c"], f)
    wmod = np.asarray(inputs["L0_wmod"], f)

    def chunkmaj(vec, nch):
        return np.ascontiguousarray(vec.reshape(nch, P).T)

    shared = dict(
        inwm=np.ascontiguousarray(in_w[:E].T),
        inwg=np.ascontiguousarray(in_w[E:].T),
        dtw=np.ascontiguousarray(dt_w.T),
        bcdw=np.ascontiguousarray(np.concatenate([B_w, C_w, D_w], 0).T),
        outw=np.ascontiguousarray(out_w.T),
        wr=np.ascontiguousarray(Wr),
        w1=np.ascontiguousarray(W1),
        w2=np.ascontiguousarray(W2),
        wmod=np.ascontiguousarray(wmod),
        inbm=chunkmaj(in_b[:E], KE),
        inbg=chunkmaj(in_b[E:], KE),
        convw=np.ascontiguousarray(np.concatenate(
            [chunkmaj(conv_w[:, 0, k], KE) for k in range(3)], 1)),
        convb=chunkmaj(conv_b, KE),
        dtb=chunkmaj(dt_b, KE),
        bcdb=np.ascontiguousarray(np.stack([B_b, C_b, D_b], 1)),
        outb=chunkmaj(out_b, KD),
        brv=np.ascontiguousarray(br.reshape(8, 1)),
        utc=np.ascontiguousarray(np.triu(np.ones((P, P), f))),
        onesc=np.ones((P, P), f),
        maskc=np.ascontiguousarray(
            np.where(np.arange(P)[None, :] >= np.arange(P)[:, None],
                     0.0, NEG).astype(f)),
        identc=np.eye(P, dtype=f),
    )

    in_maps = []
    for c in range(n_cores):
        b, odd = c // 2, c % 2
        xb = x[b].T
        halves = []
        hmask = np.zeros((P, 4), f)
        for hi, h in enumerate((odd, 1 - odd)):
            t0 = h * TPH
            blk = np.zeros((Dm, TPH + 2), f)
            blk[:, 1:TPH + 1] = xb[:, t0:t0 + TPH]
            if t0 - 1 >= 0:
                blk[:, 0] = xb[:, t0 - 1]
                hmask[:, hi * 2] = 1.0
            if t0 + TPH < T:
                blk[:, TPH + 1] = xb[:, t0 + TPH]
                hmask[:, hi * 2 + 1] = 1.0
            halves.append(blk)
        m = dict(shared)
        m["x"] = np.ascontiguousarray(np.concatenate(halves, 1))
        m["cvec"] = np.ascontiguousarray(cmat[b].reshape(KD, P).T)
        m["flagv"] = np.array([[1.0 if odd else 0.0]], f)
        m["hmaskv"] = hmask
        in_maps.append(m)
    return in_maps


def kernel(**inputs):
    nc, _ = build_nc(upto="final", dbg=False)
    in_maps = stage_inputs(inputs)
    res = run_bass_kernel_spmd(nc, in_maps, core_ids=list(range(8)))
    x = np.asarray(inputs["L0_attn_x"])
    B, T, Dm = x.shape
    out = np.empty((B, T, Dm), np.float32)
    for c in range(8):
        b, odd = c // 2, c % 2
        t0 = odd * TPH
        out[b, t0:t0 + TPH, :] = np.asarray(res.results[c]["out"]).T
    return out
